# version 10
# speedup vs baseline: 1.3052x; 1.0045x over previous
import numpy as np
import concourse.bass as bass
import concourse.mybir as mybir
from concourse.bass_utils import run_bass_kernel_spmd
from concourse.bass_core_rust import IndirectOffsetOnAxis

F32 = mybir.dt.float32
BF16 = mybir.dt.bfloat16
I32 = mybir.dt.int32
ALU = mybir.AluOpType
AF = mybir.ActivationFunctionType
AX = mybir.AxisListType

B = 2
T = 2048
D = 1024
H = 8
DK = 64
DV = 128
HD = H * DK
HV = H * DV
NT = B * T
NTILE = 16
C = 32
NCH = T // C
WARM = 192
CAP_PAD = 256
EPS = 1e-6
HUGE = 1 << 22

ENGS = ("sync", "vector", "scalar", "tensor", "gpsimd")


_EMITTING = {
    "tensor_scalar", "tensor_tensor", "tensor_copy", "memset", "memzero",
    "reciprocal", "select", "tensor_tensor_scan", "scalar_tensor_tensor",
    "tensor_reduce", "activation", "iota", "transpose", "stream_shuffle",
    "tensor_add", "tensor_sub", "max", "pool", "pool_avg", "pool_max",
    "bn_stats", "bn_aggr", "partition_broadcast",
}


class _ChainEng:

    def __init__(self, e, csem, counter):
        self._e = e
        self._csem = csem
        self._counter = counter
        self.prev_ins = None

    def __getattr__(self, name):
        a = getattr(self._e, name)
        if name not in _EMITTING:
            return a

        def call(*args, **kw):
            if self.prev_ins is not None and not getattr(self, "_nochain", False):
                self._counter[0] += 1
                self.prev_ins.then_inc(self._csem, 1)
                self._e.wait_ge(self._csem, self._counter[0])
            ins = a(*args, **kw)
            self.prev_ins = ins
            return ins
        return call


class Prog:
    def __init__(self, nc, stack):
        self.nc = nc
        self.ops = {e: [] for e in ENGS}
        self.sem_handles = {}
        self.sem_counts = {}
        self._stack = stack
        self._pe_base = [None]
        self.psD_evac = None
        self.n = 0
        self._chain_counts = {"vector": [0], "scalar": [0]}
        self._last_bar = None

    def sem(self, name):
        if name not in self.sem_handles:
            self.sem_handles[name] = self._stack.enter_context(self.nc.semaphore(name))
            self.sem_counts[name] = 0
        return self.sem_handles[name]

    def op(self, eng, fn, deps=(), inc=None, dma=False, chain=True):
        self.n += 1
        if inc is not None:
            self.sem(inc)
            self.sem_counts[inc] += 16 if dma else 1
            tok = (inc, self.sem_counts[inc])
        else:
            tok = None
        deps = tuple(d for d in deps if d is not None)
        for d in deps:
            self.sem(d[0])
        handles = self.sem_handles
        incs = (inc, 16 if dma else 1) if inc is not None else None
        chain = eng if chain and eng in ("vector", "scalar") and not dma else None
        if chain:
            self.sem("ch_" + chain)

        def record(e, _fn=fn, _deps=deps, _incs=incs, _chain=chain):
            for (sn, sv) in _deps:
                e.wait_ge(handles[sn], sv)
            if _chain:
                px = _ChainEng(e, handles["ch_" + _chain], self._chain_counts[_chain])
                ins = _fn(px)
                if ins is None:
                    ins = px.prev_ins
            else:
                px = _ChainEng(e, None, [0])
                px._nochain = True
                ins = _fn(px)
                if ins is None:
                    ins = px.prev_ins
            if _incs is not None:
                if ins is None:
                    ins = e.nop()
                ins.then_inc(handles[_incs[0]], _incs[1])
            return ins

        self.ops[eng].append(record)
        return tok

    def mm(self, fn, base, deps=(), inc=None):
        need_nop = self._pe_base[0] is not None and self._pe_base[0] != base
        self._pe_base[0] = base

        def wrapped(e, _fn=fn, _n=need_nop):
            if _n:
                e.nop()
                e.nop()
            return _fn(e)

        return self.op("tensor", wrapped, deps=deps, inc=inc)

    def barrier(self, name, engines=ENGS, dma=()):
        sem = self.sem("bar")
        prev = self._last_bar
        self.op("vector", lambda e: e.memset(self.s_barsc[:, 0:1], 0.0),
                deps=[prev] + [d for d in dma if d is not None], inc="bar")
        self.op("scalar", lambda e: e.activation(
            self.s_barsc[:, 1:2], self.s_barsc[:, 2:3], AF.Copy),
            deps=[prev], inc="bar")
        target = self.sem_counts["bar"]
        self._last_bar = ("bar", target)
        for e in engines:
            def record(eng, _t=target):
                eng.wait_ge(sem, _t)
                return None
            self.ops[e].append(record)
        self._pe_base[0] = None
        return ("bar", target)


def _host_consts():
    c = {}
    c["ident"] = np.eye(128, dtype=np.float32)
    i = np.arange(32)
    m_sl = (i[:, None] > i[None, :]).astype(np.float32)
    m_ui = (i[:, None] <= i[None, :]).astype(np.float32)
    eye32 = np.eye(32, dtype=np.float32)
    msl = np.zeros((128, 128), dtype=np.float32)
    for i in range(4):
        msl[32 * i:32 * i + 32, 32 * i:32 * i + 32] = m_sl
    c["mask_sl"] = msl
    c["mask_ui"] = np.tile(m_ui[None, :, None, :], (4, 1, 4, 1)).reshape(128, 128)
    c["i32blk"] = np.tile(eye32[None, :, None, :], (4, 1, 4, 1)).reshape(128, 128)
    ind = np.zeros((128, 4, 8), dtype=np.float32)
    indT = np.zeros((8, 4, 128), dtype=np.float32)
    for v in range(4):
        for p in range(128):
            h = 2 * v + p // 64
            ind[p, v, h] = 1.0
            indT[h, v, p] = 1.0
    c["ind8"] = np.ascontiguousarray(ind.reshape(128, 32))
    c["ind8T"] = np.ascontiguousarray(indT.reshape(8, 512))
    grp = np.arange(128) // 8
    bi = grp // 8
    e_of = grp % 8
    seg = np.arange(128) % 8
    c["g8"] = (grp[:, None] == grp[None, :]).astype(np.float32)
    c["e8"] = ((bi[:, None] == bi[None, :]) & (seg[:, None] == seg[None, :])).astype(np.float32)
    c["lt8"] = ((grp[:, None] == grp[None, :]) & (seg[:, None] < seg[None, :])).astype(np.float32)
    c["eoffm1"] = ((bi * 8 + e_of) * CAP_PAD).astype(np.float32).reshape(128, 1)
    blk = np.arange(32)
    c["iota_tm"] = np.ascontiguousarray(
        (np.arange(128)[:, None] + (blk[None, :] % NTILE) * 128
         + (blk[None, :] // NTILE) * T).astype(np.int32))
    c["huge_i"] = np.full((128, 32), HUGE, dtype=np.int32)
    return c


def build_nc(debug=False, nph=99, ph0cut=99, stcut=NCH, stpart=9):
    import contextlib
    nc = bass.Bass("TRN2", target_bir_lowering=False)
    stack = contextlib.ExitStack()
    P = Prog(nc, stack)
    dt = {}
    s = {}
    levels = []

    def din(name, shape, dtype=F32):
        dt[name] = nc.dram_tensor(name, list(shape), dtype, kind="ExternalInput").ap()

    def dout(name, shape, dtype=F32):
        dt[name] = nc.dram_tensor(name, list(shape), dtype, kind="ExternalOutput").ap()

    def dscr(name, shape, dtype=BF16):
        dt[name] = nc.dram_tensor(name, list(shape), dtype, kind="Internal").ap()

    _sbn = {}

    def sb(name, shape, dtype=F32):
        _sbn[name] = _sbn.get(name, 0) + 1
        uniq = "sb_" + name + ("" if _sbn[name] == 1 else "_%d" % _sbn[name])
        g = nc.sbuf_tensor(uniq, list(shape), dtype)
        s[name] = g.__enter__()
        levels[-1].append(g)

    def push():
        levels.append([])

    def pop():
        for g in reversed(levels.pop()):
            g.__exit__(None, None, None)

    def psum(name, shape, dtype=F32):
        s[name] = stack.enter_context(nc.psum_tensor("ps_" + name, list(shape), dtype))

    din("x_tm", (B, T, D))
    for n, shp in [("wq", (D, HD)), ("wk", (D, HD)), ("wv", (D, HV)), ("w_out", (HV, D)),
                   ("w_spike", (D, HD)), ("w_au", (D, 64)), ("b_au", (64,)), ("w_ad", (64, HD)),
                   ("b_ad", (HD,)), ("alpha_spike", (HD,)), ("w_beta", (D, H)), ("b_beta", (H,)),
                   ("beta_spike", (HD,)), ("w_hn", (H, DV)), ("w_u1", (D, 64)), ("b_u1", (64,)),
                   ("w_u2", (64, D)), ("b_u2", (D,)), ("w_norm_in", (D,)), ("conv_w", (D, 4)),
                   ("conv_b", (D,)), ("w_ffn", (D,)), ("ff1", (D, 4096)), ("ff3", (D, 4096)),
                   ("ff2", (4096, D)), ("gate_w", (D, H)), ("experts", (H, D, D))]:
        din(n, shp)
    for n in ["ident", "mask_sl", "mask_ui", "i32blk", "g8", "e8", "lt8"]:
        din(n, (128, 128))
    din("ind8", (128, 32))
    din("ind8T", (8, 512))
    din("eoffm1", (128, 1))
    din("iota_tm", (128, 32), I32)
    din("huge_i", (128, 32), I32)
    dout("out", (B, T, D))
    if debug:
        dout("dbg_xn", (B, T, D), BF16)
        dout("dbg_p", (128, 8, 512), BF16)
        dout("dbg_gates", (128, 256))
        dout("dbg_q", (B, T, HD), BF16)
        dout("dbg_spk", (128, 8, 512), BF16)
        dout("dbg_beta", (8, NT))
        dout("dbg_ahat", (128, 8, 512), BF16)
        dout("dbg_y", (B, T, HV), BF16)
        dout("dbg_y1", (B, T, D), BF16)
        dout("dbg_ymoe", (B, T, D), BF16)
        dout("dbg_idx", (B * H * CAP_PAD, 1), I32)
    dscr("xn_d", (B * T, D))
    dscr("q_d", (B, T, HD))
    dscr("k_d", (B, T, HD))
    dscr("v_d", (B, T, HV))
    dscr("ab_d", (B, 4, 128, T))
    dscr("kh_d", (8, 128, T))
    dscr("kt_d", (8, 128, T))
    dscr("qh_d", (8, 128, T))
    dscr("ktld_d", (B, T, HD))
    dscr("y1_d", (B, T, D))
    dscr("y2p_d", (B, T, D))
    dscr("ymoe_d", (B * T, D))
    dscr("idx_d", (B * H * CAP_PAD, 1), I32)
    dscr("slot_d", (B, 8, 256), F32)
    dscr("g256_d", (128, 256), F32)

    push()
    sb("identb", (128, 128), BF16)
    sb("identf", (128, 128), F32)
    sb("mask_sl", (128, 128), BF16)
    sb("mask_ui", (128, 128), BF16)
    sb("i32blk", (128, 128), BF16)
    sb("ind8", (128, 32), BF16)
    sb("ind8b", (128, 32), BF16)
    sb("ind8T", (8, 512), F32)
    sb("g8", (128, 128), F32)
    sb("e8", (128, 128), F32)
    sb("lt8", (128, 128), F32)
    sb("eoffm1", (128, 1), F32)
    sb("iota_tm", (128, 32), I32)
    sb("huge_i", (128, 32), I32)
    sb("wau", (128, 8, 64), BF16)
    sb("wad", (64, HD), BF16)
    sb("wu1", (128, 8, 64), BF16)
    sb("wu2", (64, D), BF16)
    sb("wbeta_e", (128, 8, 8), BF16)
    sb("gatew", (128, 8, 8), BF16)
    sb("w_norm", (128, 8), F32)
    sb("conv_wc", (128, 8, 4), F32)
    sb("conv_bc", (128, 8), F32)
    sb("b_adc", (128, 4), F32)
    sb("aspc", (128, 4), F32)
    sb("bspc", (128, 4), F32)
    sb("b_auc", (64, 1), F32)
    sb("b_u1c", (64, 1), F32)
    sb("bbeta", (8, 1), F32)
    sb("whn_col", (128, 8), F32)
    sb("b_u2B", (128, D), F32)
    sb("wffnB", (128, D), F32)
    sb("ngcs", (8, 1), F32)
    sb("band", (128, NT), F32)
    sb("u1", (64, NT), BF16)
    sb("betaTM", (128, 32, 8), F32)
    sb("rstd2", (128, 32), F32)
    sb("aCt", (128, 8, NCH), F32)
    sb("scr_a", (128, 1024), F32)
    sb("scr_b", (128, 1024), F32)
    sb("scr_c", (128, 1024), F32)
    sb("scr_e", (128, 2048), F32)
    sb("scr_bf", (128, 2, 1024), BF16)
    sb("colA", (128, 16), F32)
    sb("c1", (128, 512), F32)
    sb("c09", (128, 512), F32)
    sb("ustate", (128, 8, 16), F32)
    sb("spksc", (128, 8, 16), F32)
    sb("stats", (128, 4), F32)
    s["onesrow"] = s["c1"].ap()[0:1, :]
    s["b_u2r"] = s["scr_a"].ap()[0:1, 0:1024]
    s["wffn_r"] = s["scr_b"].ap()[0:1, 0:1024]
    s["rowsc"] = s["scr_e"].ap()[0:1, 1536:1920]
    s["active"] = s["band"].ap()[0:8, :]
    s["gates"] = s["band"].ap()[32:40, :]
    s["bb"] = s["band"].ap()[64:72, :]
    s["beta"] = s["band"].ap()[96:104, :]

    psum("psA", (128, 512), F32)
    psum("psB", (128, 512), F32)
    psum("psC", (128, 512), F32)
    psum("psD", (128, 512), F32)
    psum("psE", (128, 512), F32)
    psum("psF", (128, 512), F32)
    psum("psG", (128, 512), F32)
    psum("psT", (128, 1024), BF16)

    sb("barsc", (128, 4), F32)
    P.s_barsc = s["barsc"]
    P._last_bar = P.op("vector", lambda e: e.memset(s["barsc"][:], 0.0), inc="dve")
    ctx = dict(nc=nc, P=P, dt=dt, s=s, debug=debug, ph0cut=ph0cut, stcut=stcut, stpart=stpart,
               evac={k: None for k in
                     ("psA", "psB", "psC", "psD", "psE", "psF", "psG", "psT", "bwin")})

    _ph0(ctx)

    push()
    sb("spk", (128, 8, T), BF16)
    push()
    sb("A", (128, 8, 2049), BF16)
    push()
    sb("wq", (128, 8, HD), BF16)
    sb("wk", (128, 8, HD), BF16)
    sb("wvh", (128, 8, 512), BF16)
    sb("wsph", (128, 8, 128), BF16)
    sb("xtm", (128, D), F32)
    sb("xnfm", (128, 8, 128), BF16)
    sb("h1", (128, 8, 516), BF16)
    sb("hc", (128, 8, 512), BF16)
    sb("au", (64, 512), BF16)
    sb("abstg", (128, 512), BF16)
    sb("scr_d", (128, 384), F32)
    if nph > 1:
        _ph1(ctx)
    pop()
    if nph > 2:
        _ph_spike(ctx)
    push()
    sb("lam", (128, 8, 2049), F32)
    if nph > 3:
        _ph_alpha(ctx)
    if nph > 4:
        _ph_ahat(ctx)
    pop()
    push()
    sb("qkt", (128, 4, 512), BF16)
    sb("kstg", (128, 4, 128), BF16)
    sb("k2stg", (128, 4, 128), BF16)
    sb("qstg", (128, 4, 128), BF16)
    sb("ktstg", (128, 512), BF16)
    if nph > 5:
        _ph_fold(ctx)
    pop()
    pop()
    pop()
    push()
    sb("Nst", (128, 16, 512), BF16)
    sb("Bst", (128, 16, 512), BF16)
    push()
    sb("khw", (128, 8, 512), BF16)
    sb("ktw", (128, 8, 512), BF16)
    sb("qhw", (128, 8, 512), BF16)
    sb("wpool", (128, 14, 512), BF16)
    sb("bwin", (128, 4), F32)
    if nph > 6:
        _ph_prep(ctx)
    pop()
    push()
    sb("wexp", (128, 2, 8, 1024), BF16)
    sb("xg", (128, 2, 1024), BF16)
    sb("xgfm", (128, 2, 8, 128), BF16)
    sb("yexp", (128, 2, 1024), BF16)
    sb("idx2", (128, 2), I32)
    sb("ztile", (128, 1024), BF16)
    sb("g256", (128, 256), F32)
    sb("m256", (128, 256), F32)
    sb("x256", (128, 256), F32)
    sb("lo256", (128, 1), F32)
    sb("hi256", (128, 1), F32)
    sb("th256", (128, 1), F32)
    sb("sel256", (128, 256), F32)
    sb("slotsb", (16, 256), F32)
    sb("slotTM", (128, 32), F32)
    sb("slot32", (128, 32), I32)
    sb("idxcol", (128, 1), I32)
    if nph > 7:
        _ph_sinkhorn_moe(ctx)
    pop()
    push()
    sb("wout", (128, 8, D), BF16)
    sb("khs", (128, 2, 8, 128), BF16)
    sb("qhs", (128, 2, 8, 128), BF16)
    sb("ktls", (128, 2, B * HD), BF16)
    sb("vtm", (128, 2, B * HV), BF16)
    sb("Sst", (128, 8, DV), F32)
    sb("Sbf", (128, 8, DV), BF16)
    sb("Xst", (128, B * HV), BF16)
    sb("Wst", (128, B * HV), BF16)
    sb("ytm", (128, 2, B * HV), BF16)
    sb("yfm", (128, 8, 128), BF16)
    sb("y1t", (128, 2, D), BF16)
    sb("scr_y", (128, 1024), F32)
    sb("yraw", (128, 2048), F32)
    if nph > 8:
        _ph_state(ctx)
    pop()
    pop()
    push()
    sb("ff1h", (128, 8, 2048), BF16)
    sb("ff3h", (128, 8, 2048), BF16)
    sb("ff2h", (128, 16, 1024), BF16)
    sb("h2fm", (128, 8, 512), BF16)
    sb("hhfm", (128, 16, 512), BF16)
    sb("y1t", (128, 2, D), BF16)
    sb("ymt", (128, 1024), BF16)
    sb("xtf", (128, 1, D), F32)
    if nph > 9:
        _ph_ffn(ctx)
    pop()
    pop()

    with nc.Block() as block:
        @block.sync
        def _(e):
            for f in P.ops["sync"]:
                f(e)

        @block.vector
        def _(e):
            for f in P.ops["vector"]:
                f(e)

        @block.scalar
        def _(e):
            for f in P.ops["scalar"]:
                f(e)

        @block.tensor
        def _(e):
            for f in P.ops["tensor"]:
                f(e)

        @block.gpsimd
        def _(e):
            for f in P.ops["gpsimd"]:
                f(e)

    stack.close()
    return nc


def _ph0(ctx):
    P, dt, s = ctx["P"], ctx["dt"], ctx["s"]
    cut = ctx.get("ph0cut", 99)

    def mkregs(e):
        ctx["bc_slot"] = e.to_reg(B * H * CAP_PAD - 1)
        ctx["bc_tok"] = e.to_reg(B * T - 1)
        return None
    P.op("gpsimd", mkregs)

    def cl(dst, src):
        return P.op("gpsimd", lambda e, d=dst, r=src: e.dma_start(out=d, in_=r), inc="w0g", dma=True)

    def ld(dst, src):
        return P.op("sync", lambda e, d=dst, r=src: e.dma_start(out=d, in_=r), inc="w0x", dma=True)

    ld(s["identf"][:], dt["ident"])
    cl(s["identb"][:], dt["ident"])
    cl(s["mask_sl"][:], dt["mask_sl"])
    cl(s["mask_ui"][:], dt["mask_ui"])
    cl(s["i32blk"][:], dt["i32blk"])
    cl(s["ind8"][:], dt["ind8"])
    ld(s["ind8T"][:], dt["ind8T"])
    ld(s["g8"][:], dt["g8"])
    ld(s["e8"][:], dt["e8"])
    ld(s["lt8"][:], dt["lt8"])
    ld(s["eoffm1"][:], dt["eoffm1"])
    ld(s["iota_tm"][:], dt["iota_tm"])
    ld(s["huge_i"][:], dt["huge_i"])
    cl(s["wau"][:], dt["w_au"].rearrange("(c p) o -> p c o", p=128))
    cl(s["wad"][:], dt["w_ad"])
    cl(s["wu1"][:], dt["w_u1"].rearrange("(c p) o -> p c o", p=128))
    cl(s["wu2"][:], dt["w_u2"])
    cl(s["wbeta_e"][:], dt["w_beta"].rearrange("(c p) o -> p c o", p=128))
    cl(s["gatew"][:], dt["gate_w"].rearrange("(c p) o -> p c o", p=128))
    ld(s["scr_c"][0:8, 0:128], dt["w_norm_in"].rearrange("(c p) -> c p", p=128))
    ld(s["scr_c"][0:8, 128:256], dt["conv_b"].rearrange("(c p) -> c p", p=128))
    ld(s["scr_c"][0:8, 256:384], dt["w_hn"])
    ld(s["scr_c"][0:4, 384:512], dt["b_ad"].rearrange("(c p) -> c p", p=128))
    ld(s["scr_c"][0:4, 512:640], dt["alpha_spike"].rearrange("(c p) -> c p", p=128))
    ld(s["scr_c"][0:4, 640:768], dt["beta_spike"].rearrange("(c p) -> c p", p=128))
    ld(s["scr_c"][32:40, 0:512], dt["conv_w"].rearrange("(c p) o -> c (p o)", p=128))
    ld(s["b_auc"][:], dt["b_au"].rearrange("o -> o ()"))
    ld(s["b_u1c"][:], dt["b_u1"].rearrange("o -> o ()"))
    ld(s["bbeta"][:], dt["b_beta"].rearrange("h -> h ()"))
    ld(s["b_u2r"][:], dt["b_u2"].rearrange("d -> () d"))
    ld(s["wffn_r"][:], dt["w_ffn"].rearrange("d -> () d"))
    wld = ("w0x", P.sem_counts["w0x"])
    gld = ("w0g", P.sem_counts["w0g"])
    ctx["wld"] = wld
    if cut <= 1:
        return

    t0 = P.op("vector", lambda e: e.memset(s["c1"][:], 1.0), inc="dve")
    P.op("vector", lambda e: e.memset(s["c09"][:], 0.9))
    P.op("vector", lambda e: e.memset(s["ustate"][:], 0.5), inc="dve")

    def bcast(dst, row_ap, pbank, deps):
        tm = P.mm(lambda e, r=row_ap, p=pbank: e.matmul(
            s[p][0:128, 0:512], s["onesrow"][0:1, 0:128], r, start=True, stop=True),
            base=0, deps=deps, inc="pe")
        return P.op("vector", lambda e, d=dst, p=pbank: e.tensor_copy(d, s[p][:, 0:512]),
                    deps=[tm], inc="dve")

    if cut <= 2:
        return
    bcast(s["b_u2B"][:, 0:512], s["b_u2r"][0:1, 0:512], "psA", [t0, wld])
    bcast(s["b_u2B"][:, 512:1024], s["b_u2r"][0:1, 512:1024], "psB", [t0, wld])
    bcast(s["wffnB"][:, 0:512], s["wffn_r"][0:1, 0:512], "psC", [t0, wld])
    bcast(s["wffnB"][:, 512:1024], s["wffn_r"][0:1, 512:1024], "psD", [t0, wld])

    if cut <= 3:
        return

    if cut <= 4:
        return
    specs = [("w_norm", 0, 8), ("conv_bc", 128, 8), ("whn_col", 256, 8),
             ("b_adc", 384, 4), ("aspc", 512, 4), ("bspc", 640, 4)]
    tk = None
    for i, (nm, off, nr) in enumerate(specs):
        tk = P.mm(lambda e, off=off, nr=nr, i=i: e.transpose(
            s["psF"][0:128, i * 8:i * 8 + nr], s["scr_c"][0:nr, off:off + 128],
            s["identf"][0:nr, 0:nr]), base=0, deps=[wld] if i == 0 else (),
            inc="pe" if i == 5 else None)

    def vcopy(e):
        ins = None
        for i, (nm, off, nr) in enumerate(specs):
            ins = e.tensor_copy(s[nm][:], s["psF"][0:128, i * 8:i * 8 + nr])
        return ins
    vt1 = P.op("vector", vcopy, deps=[tk], inc="dve", chain=False)
    cwv = s["scr_c"].ap()[32:40, 0:512].rearrange("c (p o) -> c o p", p=128)
    tk = None
    for o in range(4):
        tk = P.mm(lambda e, o=o: e.transpose(
            s["psG"][0:128, 48 + o * 8:56 + o * 8], cwv[:, o, :],
            s["identf"][32:40, 32:40], tile_position=(32, 0)),
            base=32, inc="pe" if o == 3 else None)

    def ccopy(e):
        ins = None
        for o in range(4):
            ins = e.tensor_copy(s["conv_wc"][:, :, o], s["psG"][0:128, 48 + o * 8:56 + o * 8])
        return ins
    vt2 = P.op("vector", ccopy, deps=[tk], inc="dve", chain=False)
    vtr = [vt1, vt1, vt1, vt1, vt1, vt2]

    if cut <= 5:
        return
    for v in range(4):
        P.op("vector", lambda e, v=v: e.tensor_scalar(
            s["ind8b"][:, v * 8:(v + 1) * 8], s["ind8"][:, v * 8:(v + 1) * 8],
            s["bspc"][:, v:v + 1], None, ALU.mult), deps=[gld, vtr[5]], inc="dve")
    P.barrier("ph0")


def _ph1(ctx):
    P, dt, s, ev = ctx["P"], ctx["dt"], ctx["s"], ctx["evac"]
    dbg = ctx["debug"]
    prev = {}

    for nm, srcn in (("wq", "wq"), ("wk", "wk")):
        P.op("gpsimd", lambda e, nm=nm, srcn=srcn: e.dma_start(
            out=s[nm][:], in_=dt[srcn].rearrange("(c p) o -> p c o", p=128)), inc="wqk", dma=True)
    pwld = ("wqk", P.sem_counts["wqk"])
    z1 = P.op("vector", lambda e: e.memset(s["A"][:, :, 0:1], 0.0), inc="dve")
    z2 = P.op("vector", lambda e: e.memset(s["h1"][:, :, 0:4], 0.0), inc="dve")
    prev[("h1halo",)] = z2
    prev[("pz",)] = z1

    for g in range(B * 4):
        b, gb = divmod(g, 4)
        t0 = gb * 512
        h1_last = None

        for it in range(4):
            tt = t0 + it * 128
            dbuf = (g * 4 + it) % 2
            lx = P.op("sync", lambda e, b=b, tt=tt: e.dma_start(
                out=s["xtm"][:], in_=dt["x_tm"][b, tt:tt + 128, :]),
                deps=[prev.get(("xtm",))], inc="xtm", dma=True)
            a1 = P.op("vector", lambda e: e.tensor_scalar(
                s["scr_a"][:, 0:1024], s["xtm"][:], 1.0, 0.0, ALU.mult, ALU.add,
                accum_out=s["colA"][:, 0:1]), deps=[lx], inc="dve")
            a2 = P.op("scalar", lambda e: e.activation(
                s["scr_b"][:, 0:1024], s["xtm"][:], AF.Square,
                accum_out=s["colA"][:, 1:2]), deps=[lx], inc="act")

            def stats_fn(e):
                e.tensor_scalar(s["colA"][:, 2:3], s["colA"][:, 0:1], 1.0 / D, None, ALU.mult)
                e.tensor_scalar(s["colA"][:, 3:4], s["colA"][:, 1:2], 1.0 / D, None, ALU.mult)
                e.tensor_tensor(s["colA"][:, 4:5], s["colA"][:, 2:3], s["colA"][:, 2:3], ALU.mult)
                e.tensor_tensor(s["colA"][:, 5:6], s["colA"][:, 3:4], s["colA"][:, 4:5], ALU.subtract)
                return e.tensor_scalar(s["colA"][:, 6:7], s["colA"][:, 5:6], 1e-5, None, ALU.add)
            a3 = P.op("vector", stats_fn, deps=[a1, a2], inc="dve")
            a4 = P.op("scalar", lambda e: e.activation(s["colA"][:, 7:8], s["colA"][:, 6:7], AF.Sqrt),
                      deps=[a3], inc="act")
            a5 = P.op("vector", lambda e: e.reciprocal(s["stats"][:, 0:1], s["colA"][:, 7:8]),
                      deps=[a4], inc="dve")

            def stats2_fn(e):
                e.tensor_tensor(s["stats"][:, 1:2], s["colA"][:, 2:3], s["stats"][:, 0:1], ALU.mult)
                e.tensor_tensor(s["colA"][:, 8:9], s["stats"][:, 0:1], s["stats"][:, 0:1], ALU.mult)
                e.tensor_tensor(s["colA"][:, 9:10], s["colA"][:, 5:6], s["colA"][:, 8:9], ALU.mult)
                return e.tensor_scalar(s["colA"][:, 10:11], s["colA"][:, 9:10], EPS, None, ALU.add)
            a6 = P.op("vector", stats2_fn, deps=[a5], inc="dve")
            a7 = P.op("scalar", lambda e: e.activation(s["colA"][:, 11:12], s["colA"][:, 10:11], AF.Sqrt),
                      deps=[a6], inc="act")
            a8 = P.op("vector", lambda e: e.reciprocal(s["stats"][:, 2:3], s["colA"][:, 11:12]),
                      deps=[a7], inc="dve")
            a9 = P.op("vector", lambda e, d=dbuf: e.tensor_scalar(
                s["scr_bf"][:, d, :], s["xtm"][:], s["colA"][:, 2:3], s["stats"][:, 0:1],
                ALU.subtract, ALU.mult), deps=[a8, prev.get(("xnbf", dbuf))], inc="dve")
            sx = P.op("sync", lambda e, b=b, tt=tt, d=dbuf: e.dma_start(
                out=dt["xn_d"][b * T + tt:b * T + tt + 128, :], in_=s["scr_bf"][:, d, :]),
                deps=[a9], inc="sxn%d" % dbuf, dma=True)
            if dbg:
                sx = P.op("sync", lambda e, b=b, tt=tt, d=dbuf: e.dma_start(
                    out=dt["dbg_xn"][b, tt:tt + 128, :], in_=s["scr_bf"][:, d, :]),
                    deps=[a9], inc="sxn%d" % dbuf, dma=True)
            trl = None
            for j in range(8):
                trl = P.mm(lambda e, j=j, d=dbuf: e.transpose(
                    s["psT"][:, j * 128:(j + 1) * 128], s["scr_bf"][:, d, j * 128:(j + 1) * 128],
                    s["identb"][:]), base=0, deps=[a9, ev["psT"]] if j == 0 else (),
                    inc="pe" if j == 7 else None)
            prev[("xnbf", dbuf)] = sx
            exn = P.op("vector", lambda e: e.tensor_copy(
                s["xnfm"][:],
                s["psT"][:].rearrange("p (c t) -> p c t", c=8)),
                deps=[trl, prev.get(("xnfm",))], inc="dve")
            ev["psT"] = exn
            u1m = None
            for j in range(8):
                u1m = P.mm(lambda e, j=j: e.matmul(
                    s["psG"][0:64, 0:128], s["wu1"][:, j, :], s["xnfm"][:, j, :],
                    start=(j == 0), stop=(j == 7)), base=0,
                    deps=[exn, ev["psG"]] if j == 0 else (), inc="pe" if j == 7 else None)
            ue = P.op("scalar", lambda e, b=b, tt=tt: e.activation(
                s["u1"][0:64, b * T + tt:b * T + tt + 128], s["psG"][0:64, 0:128], AF.Silu,
                bias=s["b_u1c"][:], scale=1.0), deps=[u1m], inc="act")
            ev["psG"] = ue
            prev[("xnfm",)] = u1m
            prev[("xtm",)] = a9
            mmL = None
            for j in range(8):
                mmL = P.mm(lambda e, j=j: e.matmul(
                    s["psC"][0:8, 0:128], s["gatew"][:, j, :], s["xnfm"][:, j, :],
                    start=(j == 0), stop=(j == 7)), base=0,
                    deps=[exn, ev["psC"]] if j == 0 else (), inc="pe" if j == 7 else None)
            lg = P.op("vector", lambda e, b=b, tt=tt: e.tensor_copy(
                s["gates"][0:8, b * T + tt:b * T + tt + 128], s["psC"][0:8, 0:128]),
                deps=[mmL], inc="dve")
            ev["psC"] = lg
            trS = P.mm(lambda e: e.transpose(
                s["psD"][0:1, 256:384], s["stats"][:, 2:3], s["identf"][:]),
                base=0, deps=[a8, ev["psD"]], inc="pe")
            eS = P.op("vector", lambda e: e.tensor_copy(s["rowsc"][0:1, 256:384],
                                                        s["psD"][0:1, 256:384]),
                      deps=[trS], inc="dve")
            ev["psD"] = eS
            bc3 = P.mm(lambda e: e.matmul(s["psF"][0:128, 0:128], s["onesrow"][0:1, 0:128],
                                          s["rowsc"][0:1, 256:384], start=True, stop=True),
                       base=0, deps=[eS, ev["psF"]], inc="pe")
            h1l = None
            for j in range(8):
                h1l = P.op("vector", lambda e, j=j, it=it: e.scalar_tensor_tensor(
                    s["h1"][:, j, 4 + it * 128: 4 + (it + 1) * 128],
                    s["xnfm"][:, j, :], s["w_norm"][:, j:j + 1],
                    s["psF"][0:128, 0:128], ALU.mult, ALU.mult),
                    deps=[bc3, exn, prev.get(("h1halo",))] if j == 0 else (),
                    inc="dve" if j == 7 else None)
            ev["psF"] = h1l
            h1_last = h1l
            prev[("xnfm",)] = h1l

        hc_toks = []
        for j in range(8):
            def conv_fn(e, j=j):
                e.tensor_scalar(s["scr_e"][:, 0:512], s["h1"][:, j, 1:513],
                                s["conv_wc"][:, j, 0:1], s["conv_bc"][:, j:j + 1], ALU.mult, ALU.add)
                e.scalar_tensor_tensor(s["scr_e"][:, 0:512], s["h1"][:, j, 2:514],
                                       s["conv_wc"][:, j, 1:2], s["scr_e"][:, 0:512], ALU.mult, ALU.add)
                e.scalar_tensor_tensor(s["scr_e"][:, 0:512], s["h1"][:, j, 3:515],
                                       s["conv_wc"][:, j, 2:3], s["scr_e"][:, 0:512], ALU.mult, ALU.add)
                return e.scalar_tensor_tensor(s["scr_e"][:, 0:512], s["h1"][:, j, 4:516],
                                              s["conv_wc"][:, j, 3:4], s["scr_e"][:, 0:512], ALU.mult, ALU.add)
            cv = P.op("vector", conv_fn,
                      deps=[h1_last, prev.get(("hc", j)), prev.get(("hc_last",))], inc="dve")
            sc = P.op("scalar", lambda e, j=j: e.activation(s["hc"][:, j, :], s["scr_e"][:, 0:512],
                                                            AF.Silu), deps=[cv, pwld], inc="act")
            hc_toks.append(sc)
            prev[("hc", j)] = sc
            prev[("hc_last",)] = sc
        hc_done = hc_toks[-1]
        if gb < 3:
            hhalo = P.op("vector", lambda e: e.tensor_copy(s["h1"][:, :, 0:4], s["h1"][:, :, 512:516]),
                         deps=[hc_done], inc="dve")
        else:
            hhalo = P.op("vector", lambda e: e.memset(s["h1"][:, :, 0:4], 0.0),
                         deps=[hc_done], inc="dve")
        prev[("h1halo",)] = hhalo

        aum = None
        for j in range(8):
            aum = P.mm(lambda e, j=j: e.matmul(
                s["psE"][0:64, 0:512], s["wau"][:, j, :], s["hc"][:, j, :],
                start=(j == 0), stop=(j == 7)), base=0,
                deps=[hc_done, ev["psE"]] if j == 0 else (), inc="pe" if j == 7 else None)
        aue = P.op("scalar", lambda e: e.activation(s["au"][:], s["psE"][0:64, 0:512], AF.Silu,
                                                    bias=s["b_auc"][:], scale=1.0),
                   deps=[aum], inc="act")
        ev["psE"] = aue
        for oc in range(4):
            abm = P.mm(lambda e, oc=oc: e.matmul(
                s["psF"][0:128, 0:512], s["wad"][:, oc * 128:(oc + 1) * 128], s["au"][:],
                start=True, stop=True), base=0, deps=[aue, ev["psF"]], inc="pe")
            abe = P.op("vector", lambda e, oc=oc: e.tensor_scalar(
                s["abstg"][:], s["psF"][0:128, 0:512], s["b_adc"][:, oc:oc + 1],
                None, ALU.add), deps=[abm, prev.get(("abstg",))], inc="dve")
            ev["psF"] = abe
            sab = P.op("sync", lambda e, b=b, t0=t0, oc=oc: e.dma_start(
                out=dt["ab_d"][b, oc, :, t0:t0 + 512], in_=s["abstg"][:]),
                deps=[abe], inc="sab", dma=True)
            prev[("abstg",)] = sab

        bbm = None
        for j in range(8):
            bbm = P.mm(lambda e, j=j: e.matmul(
                s["psD"][0:8, 0:512], s["wbeta_e"][:, j, :], s["hc"][:, j, :],
                start=(j == 0), stop=(j == 7)), base=0,
                deps=[hc_done, ev["psD"]] if j == 0 else (), inc="pe" if j == 7 else None)
        bbe = P.op("vector", lambda e, b=b, t0=t0: e.tensor_scalar(
            s["bb"][0:8, b * T + t0:b * T + t0 + 512], s["psD"][0:8, 0:512], s["bbeta"][:],
            None, ALU.add), deps=[bbm], inc="dve")
        ev["psD"] = bbe

        for oc in range(4):
            lsp = P.op("gpsimd", lambda e, oc=oc: e.dma_start(
                out=s["wsph"][:], in_=dt["w_spike"][:, oc * 128:(oc + 1) * 128]
                .rearrange("(c p) o -> p c o", p=128)),
                deps=[prev.get(("wsph",))], inc="wsp", dma=True)
            sdm = None
            for j in range(8):
                sdm = P.mm(lambda e, j=j: e.matmul(
                    s["psC"][0:128, 0:512], s["wsph"][:, j, :], s["hc"][:, j, :],
                    start=(j == 0), stop=(j == 7)), base=0,
                    deps=[hc_done, lsp, ev["psC"]] if j == 0 else (), inc="pe" if j == 7 else None)
            prev[("wsph",)] = sdm
            r = b * 4 + oc
            pscan = P.op("vector", lambda e, r=r, t0=t0: e.tensor_tensor_scan(
                s["A"][:, r, 1 + t0:1 + t0 + 512], s["c09"][:], s["psC"][0:128, 0:512],
                s["A"][:, r, t0:t0 + 1], ALU.mult, ALU.add),
                deps=[sdm, prev.get(("sd", oc)), prev.get(("pz",))], inc="dve")
            ev["psC"] = pscan
            prev[("sd", oc)] = pscan

        for it in range(4):
            tt = t0 + it * 128
            for which, bank, wname, dst in (("q", "psA", "wq", "q_d"), ("k", "psB", "wk", "k_d")):
                qm = None
                for j in range(8):
                    qm = P.mm(lambda e, j=j, it=it, bank=bank, wname=wname: e.matmul(
                        s[bank][0:128, 0:512], s["hc"][:, j, it * 128:(it + 1) * 128],
                        s[wname][:, j, :], start=(j == 0), stop=(j == 7)), base=0,
                        deps=[hc_done, ev[bank]] if j == 0 else (), inc="pe" if j == 7 else None)
                sq = P.op("scalar", lambda e, bank=bank: e.activation(
                    s["scr_a"][:, 0:512], s[bank][0:128, 0:512], AF.Square), deps=[qm], inc="act")
                rd = P.op("vector", lambda e: e.tensor_reduce(
                    s["colA"][:, 0:8], s["scr_a"][:, 0:512].rearrange("p (h d) -> p h d", h=8),
                    AX.X, ALU.add), deps=[sq], inc="dve")
                sq2 = P.op("scalar", lambda e: e.activation(s["colA"][:, 8:16], s["colA"][:, 0:8],
                                                            AF.Sqrt), deps=[rd], inc="act")

                def rn_fn(e):
                    e.tensor_scalar(s["colA"][:, 0:8], s["colA"][:, 8:16], EPS, None, ALU.add)
                    return e.reciprocal(s["colA"][:, 8:16], s["colA"][:, 0:8])
                rn = P.op("vector", rn_fn, deps=[sq2], inc="dve")
                ap = P.op("vector", lambda e, bank=bank: e.tensor_tensor(
                    s["scr_bf"][:, 0, 0:512].rearrange("p (h d) -> p h d", h=8),
                    s[bank][0:128, 0:512].rearrange("p (h d) -> p h d", h=8),
                    s["colA"][:, 8:16].rearrange("p h -> p h ()").broadcast_to([128, 8, 64]),
                    ALU.mult), deps=[rn, prev.get(("qkstg",)), prev.get(("xnbf", 0))], inc="dve")
                ev[bank] = ap
                sq_dma = P.op("sync", lambda e, b=b, tt=tt, dst=dst: e.dma_start(
                    out=dt[dst][b, tt:tt + 128, :], in_=s["scr_bf"][:, 0, 0:512]),
                    deps=[ap], inc="sqk", dma=True)
                if dbg and which == "q":
                    sq_dma = P.op("sync", lambda e, b=b, tt=tt: e.dma_start(
                        out=dt["dbg_q"][b, tt:tt + 128, :], in_=s["scr_bf"][:, 0, 0:512]),
                        deps=[ap], inc="sqk", dma=True)
                prev[("qkstg",)] = sq_dma
                prev[("xnbf", 0)] = sq_dma

        for half in range(2):
            bank = ("psE", "psF")[half]
            lwv = P.op("gpsimd", lambda e, half=half: e.dma_start(
                out=s["wvh"][:], in_=dt["wv"][:, half * 512:(half + 1) * 512]
                .rearrange("(c p) o -> p c o", p=128)),
                deps=[prev.get(("wvh",))], inc="wvl", dma=True)
            vm_last = None
            for it in range(4):
                tt = t0 + it * 128
                vm = None
                for j in range(8):
                    vm = P.mm(lambda e, j=j, it=it, bank=bank: e.matmul(
                        s[bank][0:128, 0:512], s["hc"][:, j, it * 128:(it + 1) * 128],
                        s["wvh"][:, j, :],
                        start=(j == 0), stop=(j == 7)), base=0,
                        deps=[hc_done, lwv, ev[bank]] if j == 0 else (),
                        inc="pe" if j == 7 else None)
                ve = P.op("vector", lambda e, half=half, bank=bank: e.tensor_copy(
                    s["scr_bf"][:, 1, half * 512:(half + 1) * 512], s[bank][0:128, 0:512]),
                    deps=[vm, prev.get(("vstg",)), prev.get(("xnbf", 1))], inc="dve")
                ev[bank] = ve
                sv = P.op("sync", lambda e, b=b, tt=tt, half=half: e.dma_start(
                    out=dt["v_d"][b, tt:tt + 128, half * 512:(half + 1) * 512],
                    in_=s["scr_bf"][:, 1, half * 512:(half + 1) * 512]),
                    deps=[ve], inc="svd", dma=True)
                prev[("vstg",)] = sv
                prev[("xnbf", 1)] = sv
                vm_last = vm
            prev[("wvh",)] = vm_last

    P.barrier("ph1", dma=[prev.get(("qkstg",)), prev.get(("vstg",)),
                          prev.get(("xnbf", 0)), prev.get(("xnbf", 1)),
                          prev.get(("abstg",))])
    if ctx["debug"]:
        tk = P.op("sync", lambda e: e.dma_start(out=dt["dbg_p"], in_=s["A"][:, :, 1:513]),
                  inc="dbgs", dma=True)
        P.op("vector", lambda e: e.memset(P.s_barsc[:, 3:4], 0.0), deps=[tk], inc="dve")
        P.barrier("ph1dbg")


def _ph_spike(ctx):
    P, s = ctx["P"], ctx["s"]
    Av = s["A"][:, :, 1:2049].rearrange("p r (c t) -> p r c t", c=16)
    Sv = s["spk"][:].rearrange("p r (c t) -> p r c t", c=16)
    last = None
    for step in range(-WARM, 128):
        if step < 0:
            cc0 = (-step + 127) // 128
            off = step + cc0 * 128

            def fstep(e, cc0=cc0, off=off):
                e.tensor_tensor(s["spksc"][:, :, cc0:16],
                                Av[:, :, 0:16 - cc0, off],
                                s["ustate"][:, :, cc0:16], ALU.is_gt)
                e.tensor_scalar(s["spksc"][:, :, cc0:16], s["spksc"][:, :, cc0:16],
                                0.45, 0.05, ALU.mult, ALU.add)
                return e.scalar_tensor_tensor(s["ustate"][:, :, cc0:16],
                                              s["ustate"][:, :, cc0:16],
                                              0.9, s["spksc"][:, :, cc0:16],
                                              ALU.mult, ALU.add)
            last = P.op("vector", fstep, deps=[last], inc="dve")
        else:
            def fstep(e, step=step):
                e.tensor_tensor(Sv[:, :, :, step], Av[:, :, :, step],
                                s["ustate"][:], ALU.is_gt)
                e.tensor_scalar(s["spksc"][:], Sv[:, :, :, step], 0.45, 0.05,
                                ALU.mult, ALU.add)
                return e.scalar_tensor_tensor(s["ustate"][:], s["ustate"][:], 0.9,
                                              s["spksc"][:], ALU.mult, ALU.add)
            last = P.op("vector", fstep, deps=[last], inc="dve")
    P.barrier("spike")
    if ctx["debug"]:
        tk = P.op("sync", lambda e: e.dma_start(out=ctx["dt"]["dbg_spk"],
                                                in_=s["spk"][:, :, 0:512]), inc="dbgs", dma=True)
        P.op("vector", lambda e: e.memset(P.s_barsc[:, 3:4], 0.0), deps=[tk], inc="dve")
        P.barrier("spkdbg")


def _ph_alpha(ctx):
    P, dt, s, ev = ctx["P"], ctx["dt"], ctx["s"], ctx["evac"]
    prev = {}
    zl = P.op("vector", lambda e: e.memset(s["lam"][:, :, 0:1], 0.0), inc="dve")
    for b in range(B):
        for tr in range(4):
            cols = slice(b * T + tr * 512, b * T + tr * 512 + 512)
            tcols = slice(tr * 512, tr * 512 + 512)
            cm = bm = None
            for v in range(4):
                r = b * 4 + v
                cm = P.mm(lambda e, r=r, v=v, tcols=tcols: e.matmul(
                    s["psA"][0:8, 0:512], s["ind8"][:, v * 8:(v + 1) * 8], s["spk"][:, r, tcols],
                    start=(v == 0), stop=(v == 3)), base=0,
                    deps=[ev["psA"]] if v == 0 else (), inc="pe" if v == 3 else None)
            for v in range(4):
                r = b * 4 + v
                bm = P.mm(lambda e, r=r, v=v, tcols=tcols: e.matmul(
                    s["psB"][0:8, 0:512], s["ind8b"][:, v * 8:(v + 1) * 8], s["spk"][:, r, tcols],
                    start=(v == 0), stop=(v == 3)), base=0,
                    deps=[ev["psB"]] if v == 0 else (), inc="pe" if v == 3 else None)
            ac = P.op("vector", lambda e, cols=cols: e.tensor_scalar(
                s["active"][0:8, cols], s["psA"][0:8, 0:512], 0.5, None, ALU.is_gt),
                deps=[cm], inc="dve")
            ev["psA"] = ac

            def beta_fn(e, cols=cols):
                return e.tensor_tensor(s["scr_a"][0:8, 0:512], s["bb"][0:8, cols],
                                       s["psB"][0:8, 0:512], ALU.add)
            b1 = P.op("vector", beta_fn, deps=[bm], inc="dve")
            ev["psB"] = b1
            b2 = P.op("scalar", lambda e: e.activation(s["scr_a"][0:8, 512:1024],
                                                       s["scr_a"][0:8, 0:512], AF.Sigmoid),
                      deps=[b1], inc="act")
            b3 = P.op("vector", lambda e, cols=cols: e.tensor_tensor(
                s["beta"][0:8, cols], s["scr_a"][0:8, 512:1024], s["active"][0:8, cols], ALU.mult),
                deps=[b2, ac], inc="dve")
            prev[("beta", b, tr)] = b3
    if ctx["debug"]:
        tk = P.op("sync", lambda e: e.dma_start(out=dt["dbg_beta"], in_=s["beta"][0:8, :]),
                  deps=[prev[("beta", B - 1, 3)]], inc="dbgs", dma=True)
        P.op("vector", lambda e: e.memset(P.s_barsc[:, 3:4], 0.0), deps=[tk], inc="dve")

    for r in range(8):
        b, v = divmod(r, 4)
        scan_prev = zl
        for tr in range(4):
            dbuf = tr % 2
            cols = slice(b * T + tr * 512, b * T + tr * 512 + 512)
            tcols = slice(tr * 512, tr * 512 + 512)
            la = P.op("sync", lambda e, b=b, v=v, tr=tr, d=dbuf: e.dma_start(
                out=s["scr_bf"][:, d, 0:512], in_=dt["ab_d"][b, v, :, tr * 512:tr * 512 + 512]),
                deps=[prev.get(("abt", dbuf))], inc="abl%d" % dbuf, dma=True)
            am = P.mm(lambda e, v=v, cols=cols: e.matmul(
                s["psC"][0:128, 0:512], s["ind8T"][:, v * 128:(v + 1) * 128], s["active"][0:8, cols],
                start=True, stop=True), base=0,
                deps=[ev["psC"], prev[("beta", b, 3)]], inc="pe")

            def al_fn(e, r=r, v=v, d=dbuf, tcols=tcols):
                e.tensor_scalar(s["scr_e"][:, 0:512], s["spk"][:, r, tcols],
                                s["aspc"][:, v:v + 1], None, ALU.mult)
                return e.tensor_tensor(s["scr_e"][:, 0:512], s["scr_e"][:, 0:512],
                                       s["scr_bf"][:, d, 0:512], ALU.add)
            a1 = P.op("vector", al_fn,
                       deps=[la, prev.get(("scr_e_al",))], inc="dve")
            a2 = P.op("scalar", lambda e: e.activation(s["scr_e"][:, 512:1024], s["scr_e"][:, 0:512],
                                                       AF.Sigmoid), deps=[a1], inc="act")
            a3 = P.op("scalar", lambda e: e.activation(s["scr_e"][:, 1024:1536], s["scr_e"][:, 512:1024],
                                                       AF.Ln), deps=[a2], inc="act")
            a4 = P.op("vector", lambda e: e.tensor_tensor(
                s["scr_e"][:, 0:512], s["scr_e"][:, 1024:1536], s["psC"][0:128, 0:512], ALU.mult),
                deps=[a3, am], inc="dve")
            ev["psC"] = a4
            prev[("abt", dbuf)] = a4
            sc = P.op("vector", lambda e, r=r, tr=tr: e.tensor_tensor_scan(
                s["lam"][:, r, 1 + tr * 512:1 + tr * 512 + 512], s["c1"][:], s["scr_e"][:, 0:512],
                s["lam"][:, r, tr * 512:tr * 512 + 1], ALU.mult, ALU.add),
                deps=[a4, scan_prev], inc="dve")
            scan_prev = sc
            prev[("scr_e_al",)] = sc
    P.barrier("alpha")


def _ph_ahat(ctx):
    P, dt, s = ctx["P"], ctx["dt"], ctx["s"]
    last = None
    for r in range(8):
        def lrel(e, r=r):
            return e.tensor_tensor(
                s["scr_e"][:, 0:2048].rearrange("p (c t) -> p c t", c=64),
                s["lam"][:, r, 1:2049].rearrange("p (c t) -> p c t", c=64),
                s["lam"][:, r, 0:2048].rearrange("p (c t) -> p c t", t=32)[:, :, 0:1]
                .broadcast_to([128, 64, 32]), ALU.subtract)
        t1 = P.op("vector", lrel, deps=[last], inc="dve")
        t2 = P.op("scalar", lambda e, r=r: e.activation(s["spk"][:, r, :], s["scr_e"][:, 0:2048],
                                                        AF.Exp), deps=[t1], inc="act")
        t3 = P.op("vector", lambda e, r=r: e.tensor_copy(
            s["aCt"][:, r, :],
            s["spk"][:, r, :].rearrange("p (c t) -> p c t", t=32)[:, :, 31]), deps=[t2], inc="dve")
        last = t3
    P.barrier("ahat")
    if ctx["debug"]:
        tk = P.op("sync", lambda e: e.dma_start(out=ctx["dt"]["dbg_ahat"],
                                                in_=s["spk"][:, :, 0:512]), inc="dbgs", dma=True)
        P.op("vector", lambda e: e.memset(P.s_barsc[:, 3:4], 0.0), deps=[tk], inc="dve")
        P.barrier("ahdbg")


def _ph_fold(ctx):
    P, dt, s, ev = ctx["P"], ctx["dt"], ctx["s"], ctx["evac"]
    prev = {}
    khv = dt["kh_d"].rearrange("r p t -> p r t")
    ktv = dt["kt_d"].rearrange("r p t -> p r t")
    qhv = dt["qh_d"].rearrange("r p t -> p r t")
    for b in range(B):
        for it in range(NTILE):
            tt = it * 128
            tile = b * NTILE + it
            dbuf = tile % 2
            lq = P.op("sync", lambda e, b=b, tt=tt, d=dbuf: e.dma_start(
                out=s["qkt"][:, d, :], in_=dt["q_d"][b, tt:tt + 128, :]),
                deps=[prev.get(("qkt", dbuf))], inc="qkl%d" % dbuf, dma=True)
            trq = None
            for j in range(4):
                trq = P.mm(lambda e, j=j, d=dbuf: e.transpose(
                    s["psT"][:, j * 128:(j + 1) * 128], s["qkt"][:, d, j * 128:(j + 1) * 128],
                    s["identb"][:]), base=0,
                    deps=[lq, ev["psT"]] if j == 0 else (), inc="pe" if j == 3 else None)
            qe = None
            for j in range(4):
                r = b * 4 + j
                qe = P.op("vector", lambda e, j=j, r=r, tt=tt: e.tensor_tensor(
                    s["qstg"][:, j, :], s["psT"][:, j * 128:(j + 1) * 128],
                    s["spk"][:, r, tt:tt + 128], ALU.mult),
                    deps=[trq, prev.get(("qstg",))] if j == 0 else (),
                    inc="dve" if j == 3 else None)
            ev["psT"] = qe
            sq = P.op("sync", lambda e, b=b, tt=tt: e.dma_start(
                out=qhv[:, b * 4:(b + 1) * 4, tt:tt + 128], in_=s["qstg"][:]),
                deps=[qe], inc="sqh", dma=True)
            prev[("qstg",)] = sq
            ev["psT"] = qe
            prev[("qkt", dbuf)] = trq
            lk = P.op("sync", lambda e, b=b, tt=tt, d=dbuf: e.dma_start(
                out=s["qkt"][:, 2 + d, :], in_=dt["k_d"][b, tt:tt + 128, :]),
                deps=[prev.get(("qkt_k", dbuf))], inc="qkl%d" % dbuf, dma=True)
            trk = None
            for j in range(4):
                trk = P.mm(lambda e, j=j, d=dbuf: e.matmul(
                    s["psF"][:, j * 128:(j + 1) * 128],
                    s["qkt"][:, 2 + d, j * 128:(j + 1) * 128], s["identb"][:],
                    start=True, stop=True), base=0,
                    deps=[lk, ev["psF"]] if j == 0 else (), inc="pe" if j == 3 else None)
            ke = None
            for j in range(4):
                r = b * 4 + j

                def kf(e, j=j, r=r, tt=tt):
                    e.tensor_tensor(s["kstg"][:, j, :],
                                    s["psF"][:, j * 128:(j + 1) * 128],
                                    s["spk"][:, r, tt:tt + 128], ALU.mult)
                    e.reciprocal(s["scr_a"][:, 0:128], s["spk"][:, r, tt:tt + 128])
                    return e.tensor_tensor(s["k2stg"][:, j, :],
                                           s["psF"][:, j * 128:(j + 1) * 128],
                                           s["scr_a"][:, 0:128], ALU.mult)
                ke = P.op("vector", kf,
                          deps=[trk, prev.get(("kstg",))] if j == 0 else (),
                          inc="dve" if j == 3 else None)
            ev["psF"] = ke
            sk = P.op("sync", lambda e, b=b, tt=tt: e.dma_start(
                out=khv[:, b * 4:(b + 1) * 4, tt:tt + 128], in_=s["kstg"][:]),
                deps=[ke], inc="skh", dma=True)
            sk2 = P.op("sync", lambda e, b=b, tt=tt: e.dma_start(
                out=ktv[:, b * 4:(b + 1) * 4, tt:tt + 128], in_=s["k2stg"][:]),
                deps=[ke], inc="skh", dma=True)
            prev[("kstg",)] = sk2
            tra = None
            for j in range(4):
                r = b * 4 + j
                tra = P.mm(lambda e, j=j, r=r, tt=tt: e.matmul(
                    s["psG"][:, j * 128:(j + 1) * 128], s["spk"][:, r, tt:tt + 128],
                    s["identb"][:], start=True, stop=True), base=0,
                    deps=[ev["psG"]] if j == 0 else (),
                    inc="pe" if j == 3 else None)

            def ktm_fn(e, d=dbuf):
                e.tensor_copy(s["scr_a"][:, 0:512], s["psG"][:, 0:512])
                e.reciprocal(s["scr_b"][:, 0:512], s["scr_a"][:, 0:512])
                return e.tensor_tensor(s["ktstg"][:], s["qkt"][:, 2 + d, :],
                                       s["scr_b"][:, 0:512], ALU.mult)
            kt = P.op("vector", ktm_fn, deps=[tra, prev.get(("ktstg",))], inc="dve")
            ev["psG"] = kt
            prev[("qkt_k", dbuf)] = kt
            skt = P.op("sync", lambda e, b=b, tt=tt: e.dma_start(
                out=dt["ktld_d"][b, tt:tt + 128, :], in_=s["ktstg"][:]),
                deps=[kt], inc="skt", dma=True)
            prev[("ktstg",)] = skt
            prev[("skh",)] = sk2
            prev[("sqh",)] = sq
            trb = P.mm(lambda e, b=b, tt=tt: e.transpose(
                s["psA"][0:128, 0:8], s["beta"][0:8, b * T + tt:b * T + tt + 128],
                s["identf"][96:104, 96:104], tile_position=(96, 0)),
                base=96, deps=[ev["psA"]], inc="pe")
            be = P.op("vector", lambda e, tile=tile: e.tensor_copy(
                s["betaTM"][:, tile, :], s["psA"][0:128, 0:8]), deps=[trb], inc="dve")
            ev["psA"] = be
    P.barrier("fold", dma=[prev.get(("ktstg",)), prev.get(("skh",)),
                           prev.get(("sqh",))])


(LP, LT, U2, D2, U4, D4, U8, D8, U16, FT, P1T, P2T, P3T, WSC) = range(14)


def _blk_mm(P, s, dst, lhs, rhs, deps, bank="psD"):
    W = s["wpool"]
    mm = None
    for g in range(4):
        mm = P.mm(lambda e, g=g: e.matmul(
            s[bank][0:128, g * 128:(g + 1) * 128],
            W[:, lhs, g * 128:(g + 1) * 128],
            W[:, rhs, g * 128:(g + 1) * 128],
            start=True, stop=True), base=0,
            deps=(list(deps or ()) + [P.psD_evac]) if g == 0 else (),
            inc="pe" if g == 3 else None)
    evt = P.op("vector", lambda e: e.tensor_copy(W[:, dst, :], s[bank][:, 0:512]),
               deps=[mm], inc="dve")
    P.psD_evac = evt
    return evt


def _ph_prep(ctx):
    P, dt, s, ev = ctx["P"], ctx["dt"], ctx["s"], ctx["evac"]
    W = s["wpool"]
    khv = dt["kh_d"].rearrange("r p t -> p r t")
    ktv = dt["kt_d"].rearrange("r p t -> p r t")
    qhv = dt["qh_d"].rearrange("r p t -> p r t")
    prevw = {}
    for w in range(4):
        wc = slice(w * 512, (w + 1) * 512)
        lw1 = P.op("sync", lambda e, wc=wc: e.dma_start(out=s["khw"][:], in_=khv[:, :, wc]),
                   deps=[prevw.get("kh")], inc="lkh", dma=True)
        lw2 = P.op("sync", lambda e, wc=wc: e.dma_start(out=s["ktw"][:], in_=ktv[:, :, wc]),
                   deps=[prevw.get("kt")], inc="lkt", dma=True)
        lw3 = P.op("sync", lambda e, wc=wc: e.dma_start(out=s["qhw"][:], in_=qhv[:, :, wc]),
                   deps=[prevw.get("qh")], inc="lqh", dma=True)
        for bh in range(16):
            b, h = divmod(bh, 8)
            r = b * 4 + h // 2
            pb = 64 * (h % 2)

            def bw_fn(e, b=b, h=h, w=w):
                for sl in range(4):
                    e.tensor_copy(s["bwin"][:, sl:sl + 1],
                                  s["betaTM"][:, b * NTILE + w * 4 + sl, h:h + 1])
                return None
            bw = P.op("vector", bw_fn, deps=[ev["bwin"]], inc="dve", chain=False)
            mmA = None
            for g in range(4):
                for i in range(4):
                    tl = (g * 4 + i) * 32
                    mmA = P.mm(lambda e, g=g, i=i, tl=tl, r=r, pb=pb: e.matmul(
                        s["psA"][32 * i:32 * i + 32,
                                 g * 128 + 32 * i:g * 128 + 32 * i + 32],
                        s["khw"][pb:pb + 64, r, tl:tl + 32],
                        s["ktw"][pb:pb + 64, r, tl:tl + 32],
                        start=True, stop=True, tile_position=(pb, 32 * i)),
                        base=pb,
                        deps=[ev["psA"], bw, lw1, lw2] if (g, i) == (0, 0) else (),
                        inc="pe" if (g, i) == (3, 3) else None)

            def la_fn(e):
                e.tensor_tensor(s["scr_a"][:, 0:512].rearrange("p (g j) -> p g j", g=4),
                                s["psA"][:, 0:512].rearrange("p (g j) -> p g j", g=4),
                                s["bwin"][:].rearrange("p g -> p g ()").broadcast_to([128, 4, 128]),
                                ALU.mult)
                return e.tensor_tensor(
                    W[:, LP, :].rearrange("p (g j) -> p g j", g=4),
                    s["scr_a"][:, 0:512].rearrange("p (g j) -> p g j", g=4),
                    s["mask_sl"][:].rearrange("p j -> p () j").broadcast_to([128, 4, 128]),
                    ALU.mult)
            ea = P.op("vector", la_fn, deps=[mmA, P.psD_evac], inc="dve")
            ev["psA"] = ea
            trt = None
            for g in range(4):
                trt = P.mm(lambda e, g=g: e.transpose(
                    s["psT"][:, g * 128:(g + 1) * 128], W[:, LP, g * 128:(g + 1) * 128],
                    s["identb"][:]), base=0,
                    deps=[ea, ev["psT"]] if g == 0 else (),
                    inc="pe" if g == 3 else None)
            et = P.op("vector", lambda e: e.tensor_copy(W[:, LT, :], s["psT"][:, 0:512]),
                      deps=[trt], inc="dve")
            ev["psT"] = et
            e_u2 = _blk_mm(P, s, U2, LP, LT, [et])
            e_d2 = _blk_mm(P, s, D2, LT, LP, None)
            e_u4 = _blk_mm(P, s, U4, D2, U2, None)
            e_d4 = _blk_mm(P, s, D4, U2, D2, None)
            e_u8 = _blk_mm(P, s, U8, D4, U4, None)
            e_d8 = _blk_mm(P, s, D8, U4, D4, None)
            e_u16 = _blk_mm(P, s, U16, D8, U8, None)

            idb = s["identb"][:].rearrange("p j -> p () j").broadcast_to([128, 4, 128])

            def iadds(e):
                e.scalar_tensor_tensor(W[:, FT, :].rearrange("p (g j) -> p g j", g=4),
                                       W[:, LP, :].rearrange("p (g j) -> p g j", g=4),
                                       -1.0, idb, ALU.mult, ALU.add)
                for sl in (U2, U4, U8, U16):
                    e.tensor_tensor(W[:, sl, :].rearrange("p (g j) -> p g j", g=4),
                                    W[:, sl, :].rearrange("p (g j) -> p g j", g=4),
                                    idb, ALU.add)
                return None
            ia = P.op("vector", iadds, deps=[e_u16], inc="dve", chain=False)
            e_p1t = _blk_mm(P, s, P1T, U2, FT, [ia])
            e_p2t = _blk_mm(P, s, P2T, U4, P1T, None)
            e_p3t = _blk_mm(P, s, P3T, U8, P2T, None)
            pn = None
            for g in range(4):
                pn = P.mm(lambda e, g=g: e.matmul(
                    s["psB"][0:128, g * 128:(g + 1) * 128],
                    W[:, P3T, g * 128:(g + 1) * 128],
                    W[:, U16, g * 128:(g + 1) * 128],
                    start=True, stop=True), base=0,
                    deps=[e_p3t, ev["psB"]] if g == 0 else (),
                    inc="pe" if g == 3 else None)

            def en_fn(e, bh=bh, w=w):
                for i in range(4):
                    pv = s["psB"].ap()[32 * i:32 * i + 32, :].rearrange(
                        "p (g m) -> p g m", g=4)[:, :, 32 * i:32 * i + 32]
                    ov = s["Nst"].ap()[32 * i:32 * i + 32, bh,
                                       w * 128:(w + 1) * 128].rearrange(
                        "p (g m) -> p g m", g=4)
                    bv = s["bwin"].ap()[32 * i:32 * i + 32, :].rearrange(
                        "p g -> p g ()").broadcast_to([32, 4, 32])
                    e.tensor_tensor(ov, pv, bv, ALU.mult)
                return None
            en = P.op("vector", en_fn, deps=[pn], inc="dve", chain=False)
            ev["psB"] = en
            ev["bwin"] = en
            mmB = None
            for rng in range(4):
                for slot in range(4):
                    tl = (slot * 4 + rng) * 32
                    mmB = P.mm(lambda e, rng=rng, slot=slot, tl=tl, r=r, pb=pb: e.matmul(
                        s["psC"][32 * rng:32 * rng + 32, slot * 32:slot * 32 + 32],
                        s["ktw"][pb:pb + 64, r, tl:tl + 32],
                        s["qhw"][pb:pb + 64, r, tl:tl + 32],
                        start=True, stop=True, tile_position=(pb, 32 * rng)),
                        base=pb, deps=[ev["psC"], lw3] if (rng, slot) == (0, 0) else (),
                        inc="pe" if (rng, slot) == (3, 3) else None)
            ebb = P.op("vector", lambda e, bh=bh, w=w: e.tensor_tensor(
                s["Bst"][:, bh, w * 128:(w + 1) * 128], s["psC"][:, 0:128], s["mask_ui"][:],
                ALU.mult), deps=[mmB], inc="dve")
            ev["psC"] = ebb
        prevw["kh"] = mmA
        prevw["kt"] = mmB
        prevw["qh"] = mmB
    P.barrier("prep")


def _ph_sinkhorn_moe(ctx):
    P, dt, s, ev = ctx["P"], ctx["dt"], ctx["s"], ctx["evac"]

    t1 = P.op("sync", lambda e: e.dma_start(
        out=dt["g256_d"].rearrange("(b e s) c -> e b s c", b=2, s=8),
        in_=s["gates"][0:8, :].rearrange("e (b s c) -> e b s c", b=2, s=8)), inc="sg", dma=True)
    t2 = P.op("sync", lambda e: e.dma_start(out=s["g256"][:], in_=dt["g256_d"]),
              deps=[t1], inc="lg", dma=True)
    t3 = P.op("vector", lambda e: e.tensor_scalar(s["g256"][:], s["g256"][:], EPS, None, ALU.max),
              deps=[t2], inc="dve")
    t4 = P.op("scalar", lambda e: e.activation(s["g256"][:], s["g256"][:], AF.Ln), deps=[t3], inc="act")
    last = t4
    for it in range(8):
        x1 = P.op("scalar", lambda e: e.activation(s["x256"][:], s["g256"][:], AF.Exp,
                                                   accum_out=s["lo256"][:]), deps=[last], inc="act")
        m1 = P.mm(lambda e: e.matmul(s["psA"][0:128, 0:1], s["g8"][:], s["lo256"][:],
                                     start=True, stop=True), base=0, deps=[x1, ev["psA"]], inc="pe")
        l1 = P.op("scalar", lambda e: e.activation(s["hi256"][:], s["psA"][0:128, 0:1], AF.Ln),
                  deps=[m1], inc="act")
        ev["psA"] = l1
        s1 = P.op("vector", lambda e: e.tensor_scalar(s["g256"][:], s["g256"][:], s["hi256"][:],
                                                      None, ALU.subtract), deps=[l1], inc="dve")
        x2 = P.op("scalar", lambda e: e.activation(s["x256"][:], s["g256"][:], AF.Exp),
                  deps=[s1], inc="act")
        m2 = P.mm(lambda e: e.matmul(s["psB"][0:128, 0:256], s["e8"][:], s["x256"][:],
                                     start=True, stop=True), base=0, deps=[x2, ev["psB"]], inc="pe")
        l2 = P.op("scalar", lambda e: e.activation(s["m256"][:], s["psB"][0:128, 0:256], AF.Ln),
                  deps=[m2], inc="act")
        ev["psB"] = l2
        last = P.op("vector", lambda e: e.tensor_tensor(s["g256"][:], s["g256"][:], s["m256"][:],
                                                        ALU.subtract), deps=[l2], inc="dve")
    gex = P.op("scalar", lambda e: e.activation(s["g256"][:], s["g256"][:], AF.Exp),
               deps=[last], inc="act")
    if ctx["debug"]:
        tk = P.op("sync", lambda e: e.dma_start(out=dt["dbg_gates"], in_=s["g256"][:]),
                  deps=[gex], inc="dbgs", dma=True)
        P.op("vector", lambda e: e.memset(P.s_barsc[:, 3:4], 0.0), deps=[tk], inc="dve")

    P.op("vector", lambda e: e.memset(s["lo256"][:], 0.0), deps=[gex])
    P.op("vector", lambda e: e.memset(s["hi256"][:], 1.0))
    last = P.op("vector", lambda e: e.memset(s["th256"][:], 0.5), inc="dve")
    for it in range(36):
        c1 = P.op("vector", lambda e: e.tensor_scalar(
            s["sel256"][:], s["g256"][:], s["th256"][:], 0.0, ALU.is_gt, ALU.add,
            accum_out=s["m256"][:, 0:1]), deps=[last], inc="dve")
        m1 = P.mm(lambda e: e.matmul(s["psA"][0:128, 0:1], s["g8"][:], s["m256"][:, 0:1],
                                     start=True, stop=True), base=0, deps=[c1, ev["psA"]], inc="pe")

        def upd(e):
            e.tensor_scalar(s["m256"][:, 1:2], s["psA"][0:128, 0:1], 256.5, None, ALU.is_gt)
            e.tensor_tensor(s["m256"][:, 2:3], s["th256"][:], s["lo256"][:], ALU.subtract)
            e.tensor_tensor(s["m256"][:, 3:4], s["m256"][:, 1:2], s["m256"][:, 2:3], ALU.mult)
            e.tensor_tensor(s["lo256"][:], s["lo256"][:], s["m256"][:, 3:4], ALU.add)
            e.tensor_scalar(s["m256"][:, 1:2], s["m256"][:, 1:2], 1.0, None, ALU.is_lt)
            e.tensor_tensor(s["m256"][:, 2:3], s["th256"][:], s["hi256"][:], ALU.subtract)
            e.tensor_tensor(s["m256"][:, 3:4], s["m256"][:, 1:2], s["m256"][:, 2:3], ALU.mult)
            e.tensor_tensor(s["hi256"][:], s["hi256"][:], s["m256"][:, 3:4], ALU.add)
            e.tensor_tensor(s["m256"][:, 3:4], s["lo256"][:], s["hi256"][:], ALU.add)
            return e.tensor_scalar(s["th256"][:], s["m256"][:, 3:4], 0.5, None, ALU.mult)
        last = P.op("vector", upd, deps=[m1], inc="dve")
        ev["psA"] = last

    sel1 = P.op("vector", lambda e: e.tensor_scalar(s["sel256"][:], s["g256"][:], s["th256"][:],
                                                    None, ALU.is_gt), deps=[last], inc="dve")
    nm = P.mm(lambda e: e.matmul(s["psA"][0:128, 0:256], s["e8"][:], s["sel256"][:],
                                 start=True, stop=True), base=0, deps=[sel1, ev["psA"]], inc="pe")

    def wm_fn(e):
        e.tensor_scalar(s["m256"][:], s["g256"][:], 0.5, None, ALU.is_gt)
        e.tensor_scalar(s["x256"][:], s["psA"][0:128, 0:256], 1.5, None, ALU.is_lt)
        return e.tensor_tensor(s["m256"][:], s["m256"][:], s["x256"][:], ALU.mult)
    wm = P.op("vector", wm_fn, deps=[nm], inc="dve")
    ev["psA"] = wm
    pf1 = P.op("vector", lambda e: e.memset(s["x256"][:], 1.0), deps=[wm], inc="dve")
    pf2 = P.op("vector", lambda e: e.tensor_tensor_scan(
        s["sel256"][:], s["x256"][:], s["m256"][:], 0.0, ALU.mult, ALU.add), deps=[pf1], inc="dve")
    tm = P.mm(lambda e: e.matmul(s["psB"][0:128, 0:1], s["lt8"][:], s["sel256"][:, 255:256],
                                 start=True, stop=True), base=0, deps=[pf2, ev["psB"]], inc="pe")

    def slot_fn(e):
        e.tensor_scalar(s["x256"][:], s["sel256"][:], s["psB"][0:128, 0:1], None, ALU.add)
        e.tensor_scalar(s["x256"][:], s["x256"][:], s["eoffm1"][:], None, ALU.add)
        return e.tensor_tensor(s["x256"][:], s["x256"][:], s["m256"][:], ALU.mult)
    sl1 = P.op("vector", slot_fn, deps=[tm], inc="dve")
    ev["psB"] = sl1
    sm = P.mm(lambda e: e.matmul(s["psC"][0:128, 0:256], s["e8"][:], s["x256"][:],
                                 start=True, stop=True), base=0, deps=[sl1, ev["psC"]], inc="pe")
    se = P.op("vector", lambda e: e.tensor_scalar(s["x256"][:], s["psC"][0:128, 0:256], 1.0,
                                                  None, ALU.subtract), deps=[sm], inc="dve")
    ev["psC"] = se
    P.op("sync", lambda e: e.dma_start(
        out=dt["slot_d"][0], in_=s["x256"][0:8, :]), deps=[se], inc="ssl", dma=True)
    sb1 = P.op("sync", lambda e: e.dma_start(
        out=dt["slot_d"][1], in_=s["x256"][64:72, :]), inc="ssl", dma=True)
    sb2 = P.op("sync", lambda e: e.dma_start(out=s["slotsb"][:], in_=dt["slot_d"].rearrange(
        "b s c -> (b s) c")), deps=[sb1], inc="lsl", dma=True)
    trc = None
    for c2 in range(2):
        trc = P.mm(lambda e, c2=c2: e.transpose(
            s["psD"][0:128, c2 * 16:(c2 + 1) * 16], s["slotsb"][:, c2 * 128:(c2 + 1) * 128],
            s["identf"][0:16, 0:16]), base=0,
            deps=[sb2, ev["psD"], P.psD_evac] if c2 == 0 else (),
            inc="pe" if c2 == 1 else None)

    def unpack(e):
        for c2 in range(2):
            e.tensor_copy(
                s["slotTM"][:].rearrange("p (b sg c2) -> p b sg c2", b=2, c2=2)[:, :, :, c2],
                s["psD"][0:128, c2 * 16:(c2 + 1) * 16].rearrange("p (b sg) -> p b sg", b=2))
        return None
    up = P.op("vector", unpack, deps=[trc], inc="dve", chain=False)
    ev["psD"] = up
    P.psD_evac = up

    def slot_map(e):
        e.tensor_scalar(s["m256"][:, 0:32], s["slotTM"][:], 0.0, None, ALU.is_lt)
        return e.scalar_tensor_tensor(s["slotTM"][:], s["m256"][:, 0:32], float(HUGE),
                                      s["slotTM"][:], ALU.mult, ALU.add)
    sl2 = P.op("vector", slot_map, deps=[up], inc="dve")
    sl3 = P.op("vector", lambda e: e.tensor_copy(s["slot32"][:], s["slotTM"][:]), deps=[sl2], inc="dve")

    pre = P.op("sync", lambda e: e.dma_start(
        out=dt["idx_d"].rearrange("(p a) o -> p a o", p=128), in_=s["huge_i"][:]),
        inc="pidx", dma=True)
    sc_toks = []
    for blk in range(32):
        tok = P.op("gpsimd", lambda e, blk=blk: e.indirect_dma_start(
            dt["idx_d"], IndirectOffsetOnAxis(s["slot32"][:, blk:blk + 1], 0),
            s["iota_tm"][:, blk:blk + 1], None,
            bounds_check=ctx["bc_slot"], oob_is_err=False),
            deps=[sl3, pre] if blk == 0 else (), inc="gsc", dma=True)
        sc_toks.append(tok)
    if ctx["debug"]:
        tk = P.op("sync", lambda e: e.dma_start(out=dt["dbg_idx"], in_=dt["idx_d"]),
                  deps=[sc_toks[-1]], inc="dbgs", dma=True)
        P.op("vector", lambda e: e.memset(P.s_barsc[:, 3:4], 0.0), deps=[tk], inc="dve")

    z0 = P.op("vector", lambda e: e.memset(s["ztile"][:], 0.0), inc="dve")
    zf = None
    for b in range(B):
        for tt in range(0, T, 128):
            zf = P.op("sync", lambda e, b=b, tt=tt: e.dma_start(
                out=dt["ymoe_d"][b * T + tt:b * T + tt + 128, :], in_=s["ztile"][:]),
                deps=[z0] if tt == 0 and b == 0 else (), inc="zym", dma=True)

    prevw = {}
    scat = sc_toks[-1]
    sc = None
    for e_ in range(H):
        dbuf = e_ % 2
        lw = P.op("gpsimd", lambda e, e_=e_, d=dbuf: e.dma_start(
            out=s["wexp"][:, d, :, :], in_=dt["experts"][e_].rearrange("(c p) o -> p c o", p=128)),
            deps=[prevw.get(("wexp", dbuf))], inc="wex%d" % dbuf, dma=True)
        for b in range(B):
            gidx = b * 8 + e_
            for part, rows in ((0, 128), (1, 64)):
                base_slot = gidx * CAP_PAD + part * 128
                d2 = ((b * 2) + part) % 2
                bkA, bkB = (("psE", "psF"), ("psC", "psD"))[d2]
                lic = P.op("sync", lambda e, base_slot=base_slot, rows=rows, d2=d2: e.dma_start(
                    out=s["idx2"][0:rows, d2:d2 + 1], in_=dt["idx_d"][base_slot:base_slot + rows, :]),
                    deps=[scat, prevw.get(("idxcol", d2))], inc="lic", dma=True)
                ga = P.op("gpsimd", lambda e, rows=rows, d2=d2: e.indirect_dma_start(
                    s["xg"][0:rows, d2, :], None, dt["xn_d"],
                    IndirectOffsetOnAxis(s["idx2"][0:rows, d2:d2 + 1], 0),
                    bounds_check=ctx["bc_tok"], oob_is_err=False),
                    deps=[lic, prevw.get(("xg", d2))], inc="gga", dma=True)
                if prevw.get(("scpend",)) is not None:
                    sc = _moe_scatter(ctx, prevw, *prevw[("scpend",)])
                    prevw[("scpend",)] = None
                trl = None
                for j in range(8):
                    trl = P.mm(lambda e, j=j, rows=rows, d2=d2: e.transpose(
                        s["psT"][:, j * 128:j * 128 + rows],
                        s["xg"][0:rows, d2, j * 128:(j + 1) * 128],
                        s["identb"][0:rows, 0:rows]), base=0,
                        deps=[ga, ev["psT"]] if j == 0 else (), inc="pe" if j == 7 else None)
                ex = P.op("vector", lambda e, rows=rows, d2=d2: e.tensor_copy(
                    s["xgfm"][:, d2, :, 0:rows],
                    s["psT"][:].rearrange("p (c t) -> p c t", c=8)[:, :, 0:rows]),
                    deps=[trl], inc="dve")
                ev["psT"] = ex
                prevw[("xg", d2)] = ex
                mm1 = mm2 = None
                for j in range(8):
                    mm1 = P.mm(lambda e, j=j, d=dbuf, rows=rows, d2=d2, bkA=bkA: e.matmul(
                        s[bkA][0:rows, 0:512], s["xgfm"][:, d2, j, 0:rows],
                        s["wexp"][:, d, j, 0:512],
                        start=(j == 0), stop=(j == 7)), base=0,
                        deps=[ex, lw, ev[bkA]] if j == 0 else (), inc="pe" if j == 7 else None)
                for j in range(8):
                    mm2 = P.mm(lambda e, j=j, d=dbuf, rows=rows, d2=d2, bkB=bkB: e.matmul(
                        s[bkB][0:rows, 0:512], s["xgfm"][:, d2, j, 0:rows],
                        s["wexp"][:, d, j, 512:1024],
                        start=(j == 0), stop=(j == 7)), base=0,
                        deps=[ev[bkB]] if j == 0 else (), inc="pe" if j == 7 else None)
                ye1 = P.op("vector", lambda e, rows=rows, d2=d2, bkA=bkA: e.tensor_copy(
                    s["yexp"][0:rows, d2, 0:512], s[bkA][0:rows, 0:512]),
                    deps=[mm1, prevw.get(("yx", d2))], inc="dve")
                ye2 = P.op("vector", lambda e, rows=rows, d2=d2, bkB=bkB: e.tensor_copy(
                    s["yexp"][0:rows, d2, 512:1024], s[bkB][0:rows, 0:512]), deps=[mm2], inc="dve")
                ev[bkA], ev[bkB] = ye1, ye2
                prevw[("scpend",)] = (rows, d2, ye1, ye2, zf)
        prevw[("wexp", dbuf)] = ye2
    if prevw.get(("scpend",)) is not None:
        sc = _moe_scatter(ctx, prevw, *prevw[("scpend",)])
        prevw[("scpend",)] = None
    P.barrier("moe", dma=[sc])
    if ctx["debug"]:
        tk = None
        for b in range(B):
            for tt in range(0, T, 512):
                tk = P.op("sync", lambda e, b=b, tt=tt: e.dma_start(
                    out=dt["dbg_ymoe"][b, tt:tt + 512, :],
                    in_=dt["ymoe_d"][b * T + tt:b * T + tt + 512, :]), inc="dbgs", dma=True)
        P.op("vector", lambda e: e.memset(P.s_barsc[:, 3:4], 0.0), deps=[tk], inc="dve")
        P.barrier("moedbg")


def _moe_scatter(ctx, prevw, rows, d2, ye1, ye2, zf):
    P, dt, s = ctx["P"], ctx["dt"], ctx["s"]
    sc = P.op("gpsimd", lambda e, rows=rows, d2=d2: e.indirect_dma_start(
        dt["ymoe_d"],
        IndirectOffsetOnAxis(s["idx2"][0:rows, d2:d2 + 1], 0),
        s["yexp"][0:rows, d2, :], None,
        bounds_check=ctx["bc_tok"], oob_is_err=False),
        deps=[ye1, ye2, zf], inc="gsy", dma=True)
    prevw[("yx", d2)] = sc
    prevw[("idxcol", d2)] = sc
    return sc


def _ph_state(ctx):
    P, dt, s, ev = ctx["P"], ctx["dt"], ctx["s"], ctx["evac"]
    dbg = ctx["debug"]
    nch_cut = ctx.get("stcut", NCH)
    prev = {}
    banks = ("psA", "psB", "psC", "psD")
    khv = dt["kh_d"].rearrange("r p t -> p r t")
    qhv = dt["qh_d"].rearrange("r p t -> p r t")

    lwo = P.op("gpsimd", lambda e: e.dma_start(
        out=s["wout"][:], in_=dt["w_out"].rearrange("(c p) o -> p c o", p=128)), inc="lwo", dma=True)
    wsc = None
    for cch in range(8):
        wsc = P.op("vector", lambda e, c=cch: e.tensor_scalar(
            s["wout"][:, c, :], s["wout"][:, c, :], s["whn_col"][:, c:c + 1], None, ALU.mult),
            deps=[lwo] if cch == 0 else (), inc="dve" if cch == 7 else None)
    z1 = P.op("vector", lambda e: e.memset(s["Sst"][:], 0.0))
    z2 = P.op("vector", lambda e: e.memset(s["Sbf"][:], 0.0), inc="dve")
    prev[("S",)] = z2

    for c in range(nch_cut):
        rng = c % 4
        tile_i = c // 4
        w = c // 16
        slot = (c % 16) // 4
        tcol = c * 32
        pr = slice(32 * rng, 32 * rng + 32)
        dbuf = tile_i % 2
        if rng == 0:
            vl = None
            for b in range(B):
                vl = P.op("sync", lambda e, b=b, t=tile_i * 128, d=dbuf: e.dma_start(
                    out=s["vtm"][:, d, b * HV:(b + 1) * HV], in_=dt["v_d"][b, t:t + 128, :]),
                    deps=[prev.get(("vread", dbuf))], inc="stv%d" % dbuf, dma=True)
                vl = P.op("sync", lambda e, b=b, t=tile_i * 128, d=dbuf: e.dma_start(
                    out=s["ktls"][:, d, b * HD:(b + 1) * HD], in_=dt["ktld_d"][b, t:t + 128, :]),
                    inc="stv%d" % dbuf, dma=True)
            vl = P.op("sync", lambda e, t=tile_i * 128, d=dbuf: e.dma_start(
                out=s["khs"][:, d, :, :], in_=khv[:, :, t:t + 128]), inc="stv%d" % dbuf, dma=True)
            vl = P.op("sync", lambda e, t=tile_i * 128, d=dbuf: e.dma_start(
                out=s["qhs"][:, d, :, :], in_=qhv[:, :, t:t + 128]), inc="stv%d" % dbuf, dma=True)
            prev[("vload",)] = vl

        lcol = tcol - tile_i * 128
        for bh in range(16):
            b, h = divmod(bh, 8)
            r = b * 4 + h // 2
            pb = 64 * (h % 2)
            bank = banks[bh // 4]
            off = (bh % 4) * 128
            tok = P.mm(lambda e, r=r, pb=pb, bank=bank, off=off, lcol=lcol, pr=pr, rng=rng, d=dbuf: e.matmul(
                s[bank][pr, off:off + 128],
                s["khs"][pb:pb + 64, d, r, lcol:lcol + 32],
                s["Sbf"][pb:pb + 64, r, 0:128],
                start=True, stop=True, tile_position=(pb, 32 * rng)),
                base=pb,
                deps=[ev[bank], prev.get(("S",)), prev.get(("vload",)), wsc] if bh % 4 == 0 else (),
                inc="pe" if bh % 4 == 3 else None)
            if bh % 4 == 3:
                prev[("u0", bh // 4)] = tok
        if ctx.get("stpart", 9) <= 1:
            continue
        xd = []
        for q in range(4):
            bank = banks[q]
            b0, h0 = divmod(4 * q, 8)

            def xfn(e, q=q, bank=bank, pr=pr, d=dbuf, b0=b0, h0=h0):
                return e.tensor_tensor(
                    s["Xst"][pr, q * 512:(q + 1) * 512],
                    s["vtm"][pr, d, b0 * HV + h0 * DV: b0 * HV + h0 * DV + 512],
                    s[bank][pr, 0:512], ALU.subtract)
            t = P.op("vector", xfn, deps=[prev[("u0", q)]], inc="dve", chain=False)
            ev[bank] = t
            xd.append(t)
        if ctx.get("stpart", 9) <= 2:
            continue
        wev = []
        for half in range(2):
            for q in range(8):
                bh = half * 8 + q
                bank = ("psE", "psF")[q // 4]
                off = (q % 4) * 128
                tok = P.mm(lambda e, bh=bh, bank=bank, off=off, w=w, slot=slot, pr=pr, rng=rng: e.matmul(
                    s[bank][pr, off:off + 128],
                    s["Nst"][pr, bh, w * 128 + slot * 32: w * 128 + slot * 32 + 32],
                    s["Xst"][pr, bh * 128:(bh + 1) * 128],
                    start=True, stop=True, tile_position=(32 * rng, 32 * rng)),
                    base=32 * rng,
                    deps=(xd + [ev[bank]]) if q % 4 == 0 else (),
                    inc="pe" if q % 4 == 3 else None)
                if q % 4 == 3:
                    prev[("wmm", q // 4)] = tok
            for q2 in range(2):
                bank = ("psE", "psF")[q2]
                t = P.op("vector", lambda e, q2=q2, half=half, bank=bank, pr=pr: e.tensor_copy(
                    s["Wst"][pr, (half * 8 + q2 * 4) * 128:(half * 8 + q2 * 4 + 4) * 128],
                    s[bank][pr, 0:512]), deps=[prev[("wmm", q2)]], inc="dve")
                ev[bank] = t
                wev.append(t)

        if ctx.get("stpart", 9) <= 3:
            continue
        for bh in range(16):
            b, h = divmod(bh, 8)
            r = b * 4 + h // 2
            pb = 64 * (h % 2)
            bank = banks[bh // 4]
            off = (bh % 4) * 128
            P.mm(lambda e, r=r, pb=pb, bank=bank, off=off, lcol=lcol, pr=pr, rng=rng, d=dbuf: e.matmul(
                s[bank][pr, off:off + 128],
                s["qhs"][pb:pb + 64, d, r, lcol:lcol + 32],
                s["Sbf"][pb:pb + 64, r, 0:128],
                start=True, stop=False, tile_position=(pb, 32 * rng)),
                base=pb,
                deps=(wev + [ev[bank]]) if bh % 4 == 0 else ())
            m2 = P.mm(lambda e, bh=bh, bank=bank, off=off, w=w, slot=slot, pr=pr, rng=rng: e.matmul(
                s[bank][pr, off:off + 128],
                s["Bst"][pr, bh, w * 128 + slot * 32: w * 128 + slot * 32 + 32],
                s["Wst"][pr, bh * 128:(bh + 1) * 128],
                start=False, stop=True, tile_position=(32 * rng, 32 * rng)),
                base=32 * rng, inc="pe" if bh % 4 == 3 else None)
            if bh % 4 == 3:
                prev[("omm", bh // 4)] = m2
        if ctx.get("stpart", 9) <= 4:
            continue

        def yevac_v(e, pr=pr):
            e.tensor_copy(s["yraw"][pr, 0:512], s[banks[0]][pr, 0:512])
            e.tensor_copy(s["yraw"][pr, 512:1024], s[banks[1]][pr, 0:512])
            return None
        yrd = list(prev.get(("yraw_rd", rng), ()))
        yev_v = P.op("vector", yevac_v,
                     deps=[prev[("omm", 0)], prev[("omm", 1)]] + yrd,
                     inc="dve", chain=False)

        def yevac_s(e, pr=pr):
            e.activation(s["yraw"][pr, 1024:1536], s[banks[2]][pr, 0:512], AF.Copy)
            e.activation(s["yraw"][pr, 1536:2048], s[banks[3]][pr, 0:512], AF.Copy)
            return None
        yev_s = P.op("scalar", yevac_s,
                     deps=[prev[("omm", 2)], prev[("omm", 3)]] + yrd,
                     inc="act", chain=False)
        yevq = (yev_v, yev_v, yev_s, yev_s)
        for q in range(4):
            ev[banks[q]] = yevq[q]

        if ctx.get("stpart", 9) <= 5:
            continue
        for half in range(2):
            bank = ("psG", "psE")[half]
            tok = None
            for q in range(8):
                bh = half * 8 + q
                b, h = divmod(bh, 8)
                pb = 64 * (h % 2)
                tok = P.mm(lambda e, b=b, h=h, pb=pb, bank=bank, lcol=lcol, pr=pr, rng=rng, d=dbuf: e.matmul(
                    s[bank][pb:pb + 64, (h // 2) * 128:(h // 2) * 128 + 128],
                    s["ktls"][pr, d, b * HD + h * 64:b * HD + (h + 1) * 64],
                    s["Wst"][pr, (b * 8 + h) * 128:(b * 8 + h + 1) * 128],
                    start=True, stop=True, tile_position=(32 * rng, pb)),
                    base=32 * rng,
                    deps=(wev + [ev[bank]]) if q == 0 else (),
                    inc="pe" if q == 7 else None)
            prev[("smm", half)] = tok

        def sev_w(e, c=c):
            for half in range(2):
                bank = ("psG", "psE")[half]
                e.tensor_tensor(
                    s["scr_b"][:, half * 512:(half + 1) * 512].rearrange(
                        "p (r v) -> p r v", r=4),
                    s["Sst"][:, half * 4:(half + 1) * 4, :],
                    s[bank][0:128, 0:512].rearrange("p (r v) -> p r v", r=4), ALU.add)
            return None
        sew = P.op("vector", sev_w, deps=[prev[("smm", 0)], prev[("smm", 1)]],
                   inc="dve", chain=False)

        def sev_r(e, c=c):
            for half in range(2):
                e.tensor_tensor(
                    s["Sst"][:, half * 4:(half + 1) * 4, :],
                    s["scr_b"][:, half * 512:(half + 1) * 512].rearrange(
                        "p (r v) -> p r v", r=4),
                    s["aCt"][:, half * 4:(half + 1) * 4, c:c + 1].broadcast_to(
                        [128, 4, DV]), ALU.mult)
            return None
        se = P.op("vector", sev_r, deps=[sew], inc="dve", chain=False)
        scp = P.op("vector", lambda e: e.tensor_copy(s["Sbf"][:], s["Sst"][:]),
                   deps=[se], inc="dve")
        ev["psG"], ev["psE"] = sew, sew
        prev[("S",)] = scp

        sqt = P.op("scalar", lambda e, pr=pr: e.activation(
            s["scr_e"][pr, 0:2048], s["yraw"][pr, 0:2048],
            AF.Square), deps=[yev_v, yev_s], inc="act")
        rdt = P.op("vector", lambda e, pr=pr: e.tensor_reduce(
            s["colA"][pr, 0:16],
            s["scr_e"][pr, 0:2048].rearrange("p (x v) -> p x v", x=16),
            AX.X, ALU.add), deps=[sqt], inc="dve")
        rst = P.op("vector", lambda e, pr=pr: e.tensor_scalar(
            s["scr_y"][pr, 0:16], s["colA"][pr, 0:16],
            1.0 / DV, EPS, ALU.mult, ALU.add), deps=[rdt], inc="dve")
        sq2 = P.op("scalar", lambda e, pr=pr: e.activation(
            s["scr_y"][pr, 16:32], s["scr_y"][pr, 0:16], AF.Sqrt),
            deps=[rst], inc="act")
        rc = P.op("vector", lambda e, pr=pr: e.reciprocal(
            s["colA"][pr, 0:16], s["scr_y"][pr, 16:32]),
            deps=[sq2], inc="dve")
        yw = P.op("vector", lambda e, pr=pr, d=dbuf: e.tensor_tensor(
            s["ytm"][:, d, 0:2048].rearrange("p (x v) -> p x v", x=16)[pr],
            s["yraw"][pr, 0:2048].rearrange("p (x v) -> p x v", x=16),
            s["colA"][pr, 0:16].rearrange("p x -> p x ()").broadcast_to(
                [32, 16, DV]), ALU.mult), deps=[rc, prev.get(("ytmrd", dbuf))], inc="dve")
        ysl = [yw]
        prev[("ytile", rng)] = ysl
        prev[("yraw_rd", rng)] = ysl

        if rng == 3:
            ytoks = prev[("ytile", 0)] + prev[("ytile", 1)] + prev[("ytile", 2)] + prev[("ytile", 3)]
            pend = prev.get(("ypend",))
            prev[("ypend",)] = (tile_i, dbuf, ytoks)
            if pend is not None:
                _emit_y1(ctx, prev, *pend)
            prev[("vread", dbuf)] = scp
    pend = prev.get(("ypend",))
    if pend is not None:
        _emit_y1(ctx, prev, *pend)
    P.barrier("state", dma=[prev.get(("y1t", 0)), prev.get(("y1t", 1))])


def _emit_y1(ctx, prev, tile_i, dbuf, ytoks):
    P, dt, s, ev = ctx["P"], ctx["dt"], ctx["s"], ctx["evac"]
    tt = tile_i * 128
    trl = None
    for b in range(B):
        for j in range(8):
            trl = P.mm(lambda e, j=j, b=b, d=dbuf: e.transpose(
                s["psT"][:, j * 128:(j + 1) * 128],
                s["ytm"][:, d, b * HV + j * 128:b * HV + (j + 1) * 128],
                s["identb"][:]), base=0,
                deps=(ytoks + [ev["psT"]]) if j == 0 else (), inc="pe" if j == 7 else None)
        ey = P.op("vector", lambda e: e.tensor_copy(
            s["yfm"][:], s["psT"][:].rearrange("p (c t) -> p c t", c=8)),
            deps=[trl], inc="dve")
        ev["psT"] = ey
        mo = None
        for half in range(2):
            bank = ("psF", "psG")[half]
            for j in range(8):
                mo = P.mm(lambda e, j=j, half=half, bank=bank: e.matmul(
                    s[bank][0:128, 0:512], s["yfm"][:, j, :],
                    s["wout"][:, j, half * 512:(half + 1) * 512],
                    start=(j == 0), stop=(j == 7)), base=0,
                    deps=[ey, ev[bank]] if j == 0 else (), inc="pe" if j == 7 else None)
            prev[("wo", half)] = mo
        for half in range(2):
            bank = ("psA", "psB")[half]
            mg = P.mm(lambda e, b=b, tt=tt, half=half, bank=bank: e.matmul(
                s[bank][0:128, 0:512], s["u1"][:, b * T + tt:b * T + tt + 128],
                s["wu2"][:, half * 512:(half + 1) * 512], start=True, stop=True),
                base=0, deps=[ev[bank]], inc="pe")
            prev[("gt", half)] = mg
        lxn = P.op("sync", lambda e, b=b, tt=tt: e.dma_start(
            out=s["scr_bf"][:, 0, :], in_=dt["xn_d"][b * T + tt:b * T + tt + 128, :]),
            deps=[prev.get(("xnld",))], inc="xnl", dma=True)

        def y1_fn(e):
            for half in range(2):
                gb_ = ("psA", "psB")[half]
                wb_ = ("psF", "psG")[half]
                sl_ = slice(half * 512, (half + 1) * 512)
                e.tensor_tensor(s["scr_a"][:, sl_], s[gb_][0:128, 0:512],
                                s["b_u2B"][:, sl_], ALU.add)
                e.tensor_copy(s["scr_c"][:, sl_], s[wb_][0:128, 0:512])
            return None
        y1a = P.op("vector", y1_fn,
                   deps=[prev[("wo", 0)], prev[("wo", 1)], prev[("gt", 0)],
                         prev[("gt", 1)], lxn, prev.get(("r2a",))], inc="dve")
        for bank in ("psA", "psB", "psF", "psG"):
            ev[bank] = y1a
        y1b = P.op("scalar", lambda e: e.activation(s["scr_a"][:, 0:1024],
                                                    s["scr_a"][:, 0:1024], AF.Sigmoid),
                   deps=[y1a], inc="act")

        def y1c_fn(e, d=dbuf):
            e.tensor_tensor(s["scr_c"][:, 0:1024], s["scr_c"][:, 0:1024],
                            s["scr_a"][:, 0:1024], ALU.mult)
            return e.tensor_tensor(s["y1t"][:, d, :], s["scr_bf"][:, 0, :],
                                   s["scr_c"][:, 0:1024], ALU.add)
        y1c = P.op("vector", y1c_fn, deps=[y1b, prev.get(("y1t", dbuf))], inc="dve")
        prev[("xnld",)] = y1c
        sy = P.op("sync", lambda e, b=b, tt=tt, d=dbuf: e.dma_start(
            out=dt["y1_d"][b, tt:tt + 128, :], in_=s["y1t"][:, d, :]),
            deps=[y1c], inc="sy1%d" % dbuf, dma=True)
        r2a = P.op("scalar", lambda e, d=dbuf: e.activation(
            s["scr_a"][:, 0:1024], s["y1t"][:, d, :], AF.Square,
            accum_out=s["stats"][:, 0:1]), deps=[y1c, prev.get(("r2b",))], inc="act")
        prev[("r2a",)] = r2a
        r2b = P.op("vector", lambda e: e.tensor_scalar(
            s["stats"][:, 1:2], s["stats"][:, 0:1], 1.0 / D, EPS, ALU.mult, ALU.add),
            deps=[r2a], inc="dve")
        prev[("r2b",)] = r2b
        r2c = P.op("scalar", lambda e: e.activation(s["stats"][:, 2:3], s["stats"][:, 1:2],
                                                    AF.Sqrt), deps=[r2b], inc="act")
        r2d = P.op("vector", lambda e, b=b, tile_i=tile_i: e.reciprocal(
            s["rstd2"][:, b * NTILE + tile_i:b * NTILE + tile_i + 1], s["stats"][:, 2:3]),
            deps=[r2c], inc="dve")
        prev[("y1t", dbuf)] = sy
    prev[("ytmrd", dbuf)] = trl


def _ph_ffn(ctx):
    P, dt, s, ev = ctx["P"], ctx["dt"], ctx["s"], ctx["evac"]
    prev = {}
    mo = None
    for half in range(2):
        lf1 = P.op("gpsimd", lambda e, half=half: e.dma_start(
            out=s["ff1h"][:], in_=dt["ff1"][:, half * 2048:(half + 1) * 2048]
            .rearrange("(c p) o -> p c o", p=128)), deps=[prev.get(("ffw",))], inc="ff1w", dma=True)
        lf3 = P.op("gpsimd", lambda e, half=half: e.dma_start(
            out=s["ff3h"][:], in_=dt["ff3"][:, half * 2048:(half + 1) * 2048]
            .rearrange("(c p) o -> p c o", p=128)), inc="ff3w", dma=True)
        lf2 = P.op("gpsimd", lambda e, half=half: e.dma_start(
            out=s["ff2h"][:], in_=dt["ff2"][half * 2048:(half + 1) * 2048, :]
            .rearrange("(c p) o -> p c o", p=128)), inc="ff2w", dma=True)
        for b in range(B):
            for gb in range(4):
                t0 = gb * 512
                for it in range(4):
                    tt = t0 + it * 128
                    tile = b * NTILE + gb * 4 + it
                    db = it % 2
                    ly = P.op("sync", lambda e, b=b, tt=tt, d=db: e.dma_start(
                        out=s["y1t"][:, d, :], in_=dt["y1_d"][b, tt:tt + 128, :]),
                        deps=[prev.get(("y1ld", db))], inc="y1l%d" % db, dma=True)

                    def h2fn(e, d=db, tile=tile):
                        e.tensor_scalar(s["scr_a"][:, 0:1024], s["y1t"][:, d, :],
                                        s["rstd2"][:, tile:tile + 1], None, ALU.mult)
                        return e.tensor_tensor(s["scr_bf"][:, 1, :], s["scr_a"][:, 0:1024],
                                               s["wffnB"][:], ALU.mult)
                    h2 = P.op("vector", h2fn, deps=[ly, prev.get(("h2bf",)), lf1,
                                                    prev.get(("hhprev",))], inc="dve")
                    trl = None
                    for j in range(8):
                        trl = P.mm(lambda e, j=j: e.transpose(
                            s["psT"][:, j * 128:(j + 1) * 128], s["scr_bf"][:, 1, j * 128:(j + 1) * 128],
                            s["identb"][:]), base=0,
                            deps=[h2, ev["psT"]] if j == 0 else (), inc="pe" if j == 7 else None)
                    prev[("h2bf",)] = trl
                    eh = P.op("vector", lambda e, it=it: e.tensor_copy(
                        s["h2fm"][:, :, it * 128:(it + 1) * 128],
                        s["psT"][:].rearrange("p (c t) -> p c t", c=8)), deps=[trl], inc="dve")
                    ev["psT"] = eh
                    prev[("y1ld", db)] = eh
                    prev[("h2", it)] = eh
                h2done = [prev[("h2", i)] for i in range(4)]
                for oc in range(16):
                    bankA = ("psA", "psE")[oc % 2]
                    bankB = ("psB", "psF")[oc % 2]
                    sca = ("scr_a", "scr_e")[oc % 2]
                    mm1 = mm3 = None
                    for j in range(8):
                        mm1 = P.mm(lambda e, j=j, oc=oc, bankA=bankA: e.matmul(
                            s[bankA][0:128, 0:512], s["ff1h"][:, j, oc * 128:(oc + 1) * 128],
                            s["h2fm"][:, j, :], start=(j == 0), stop=(j == 7)), base=0,
                            deps=(h2done + [lf1, ev[bankA]]) if j == 0 else (),
                            inc="pe" if j == 7 else None)
                    for j in range(8):
                        mm3 = P.mm(lambda e, j=j, oc=oc, bankB=bankB: e.matmul(
                            s[bankB][0:128, 0:512], s["ff3h"][:, j, oc * 128:(oc + 1) * 128],
                            s["h2fm"][:, j, :], start=(j == 0), stop=(j == 7)), base=0,
                            deps=[lf3, ev[bankB]] if j == 0 else (), inc="pe" if j == 7 else None)
                    s1 = P.op("scalar", lambda e, bankA=bankA, sca=sca: e.activation(
                        s[sca][:, 0:512], s[bankA][0:128, 0:512], AF.Silu),
                        deps=[mm1, prev.get(("hhp", oc % 2))], inc="act")
                    ev[bankA] = s1
                    hh = P.op("vector", lambda e, oc=oc, bankB=bankB, sca=sca: e.tensor_tensor(
                        s["hhfm"][:, oc, :], s[sca][:, 0:512], s[bankB][0:128, 0:512], ALU.mult),
                        deps=[s1, mm3], inc="dve")
                    ev[bankB] = hh
                    prev[("hh", oc)] = hh
                    prev[("hhp", oc % 2)] = hh
                    prev[("hhprev",)] = hh
                hhdone = [prev[("hh", i)] for i in range(16)]
                for it in range(4):
                    tt = t0 + it * 128
                    db = it % 2
                    for hf in range(2):
                        bank = (("psC", "psD"), ("psG", "psA"))[it % 2][hf]
                        mo = None
                        for j in range(16):
                            mo = P.mm(lambda e, j=j, hf=hf, it=it, bank=bank: e.matmul(
                                s[bank][0:128, 0:512], s["hhfm"][:, j, it * 128:(it + 1) * 128],
                                s["ff2h"][:, j, hf * 512:(hf + 1) * 512],
                                start=(j == 0), stop=(j == 15)), base=0,
                                deps=(hhdone + [lf2, ev[bank]]) if j == 0 else (),
                                inc="pe" if j == 15 else None)
                        prev[("f2", hf)] = mo
                    ly = P.op("sync", lambda e, b=b, tt=tt, d=db, half=half: e.dma_start(
                        out=s["y1t"][:, d, :],
                        in_=(dt["y1_d"] if half == 0 else dt["y2p_d"])[b, tt:tt + 128, :]),
                        deps=[prev.get(("y2ld", db)), prev.get(("y1ld", db))],
                        inc="y1l%d" % db, dma=True)
                    prev[("y1ld", db)] = None
                    bk0, bk1 = (("psC", "psD"), ("psG", "psA"))[it % 2]
                    if half == 0:
                        def fin0(e, d=db, bk0=bk0, bk1=bk1):
                            e.tensor_tensor(s["scr_a"][:, 0:512], s["y1t"][:, d, 0:512],
                                            s[bk0][0:128, 0:512], ALU.add)
                            e.tensor_tensor(s["scr_a"][:, 512:1024], s["y1t"][:, d, 512:1024],
                                            s[bk1][0:128, 0:512], ALU.add)
                            return e.tensor_copy(s["scr_bf"][:, 1, :], s["scr_a"][:, 0:1024])
                        f0 = P.op("vector", fin0, deps=[prev[("f2", 0)], prev[("f2", 1)], ly,
                                                        prev.get(("h2bf",)),
                                                        prev.get(("hhprev",))], inc="dve")
                        ev[bk0] = ev[bk1] = f0
                        sy = P.op("sync", lambda e, b=b, tt=tt: e.dma_start(
                            out=dt["y2p_d"][b, tt:tt + 128, :], in_=s["scr_bf"][:, 1, :]),
                            deps=[f0], inc="sy2p", dma=True)
                        prev[("h2bf",)] = sy
                        prev[("y2ld", db)] = sy
                    else:
                        lym = P.op("sync", lambda e, b=b, tt=tt: e.dma_start(
                            out=s["ymt"][:], in_=dt["ymoe_d"][b * T + tt:b * T + tt + 128, :]),
                            deps=[prev.get(("ymld",))], inc="yml", dma=True)
                        lxx = P.op("sync", lambda e, b=b, tt=tt: e.dma_start(
                            out=s["xtf"][:, 0, :], in_=dt["x_tm"][b, tt:tt + 128, :]),
                            deps=[prev.get(("outst",))], inc="xtl", dma=True)
                        gsl = P.op("scalar", lambda e: e.activation(
                            s["scr_b"][:, 0:1024], s["ymt"][:], AF.Silu), deps=[lym], inc="act")

                        def fin1(e, d=db, bk0=bk0, bk1=bk1):
                            e.tensor_tensor(s["scr_a"][:, 0:512], s["y1t"][:, d, 0:512],
                                            s[bk0][0:128, 0:512], ALU.add)
                            e.tensor_tensor(s["scr_a"][:, 512:1024], s["y1t"][:, d, 512:1024],
                                            s[bk1][0:128, 0:512], ALU.add)
                            e.tensor_tensor(s["scr_a"][:, 0:1024], s["scr_a"][:, 0:1024],
                                            s["scr_b"][:, 0:1024], ALU.mult)
                            return e.tensor_tensor(s["scr_c"][:, 0:1024], s["xtf"][:, 0, :],
                                                   s["scr_a"][:, 0:1024], ALU.add)
                        f1 = P.op("vector", fin1, deps=[prev[("f2", 0)], prev[("f2", 1)], ly,
                                                        gsl, lxx, prev.get(("outst",)),
                                                        prev.get(("hhprev",))], inc="dve")
                        ev[bk0] = ev[bk1] = f1
                        prev[("ymld",)] = f1
                        so = P.op("sync", lambda e, b=b, tt=tt: e.dma_start(
                            out=dt["out"][b, tt:tt + 128, :], in_=s["scr_c"][:, 0:1024]),
                            deps=[f1], inc="sout", dma=True)
                        prev[("y2ld", db)] = so
                        prev[("outst",)] = so
        prev[("ffw",)] = mo
    fin = ("sout", P.sem_counts["sout"])
    P.op("sync", lambda e: None, deps=[fin])
    P.barrier("fin")


_CONSTS = _host_consts()

_WMAP = [("wq", "L0_attn_wq"), ("wk", "L0_attn_wk"), ("wv", "L0_attn_wv"),
         ("w_out", "L0_attn_w_out"), ("w_spike", "L0_attn_w_spike"), ("w_au", "L0_attn_w_au"),
         ("b_au", "L0_attn_b_au"), ("w_ad", "L0_attn_w_ad"), ("b_ad", "L0_attn_b_ad"),
         ("w_beta", "L0_attn_w_beta"), ("b_beta", "L0_attn_b_beta"), ("w_hn", "L0_attn_w_hn"),
         ("w_u1", "L0_attn_w_u1"), ("b_u1", "L0_attn_b_u1"), ("w_u2", "L0_attn_w_u2"),
         ("b_u2", "L0_attn_b_u2"), ("w_norm_in", "L0_attn_w_norm_in"),
         ("conv_w", "L0_attn_conv_w"), ("conv_b", "L0_attn_conv_b"), ("w_ffn", "L0_attn_w_ffn"),
         ("ff1", "L0_attn_w_ff1"), ("ff3", "L0_attn_w_ff3"), ("ff2", "L0_attn_w_ff2"),
         ("experts", "L0_ffn_experts")]


def make_in_maps(inputs, n_cores=8):
    x = np.ascontiguousarray(np.asarray(inputs["L0_attn_x"], dtype=np.float32))
    shared = {}
    for kn, inn in _WMAP:
        shared[kn] = np.ascontiguousarray(np.asarray(inputs[inn], dtype=np.float32))
    shared["gate_w"] = np.ascontiguousarray(np.asarray(inputs["L0_ffn_gate_w"],
                                                       dtype=np.float32)[0])
    shared["alpha_spike"] = np.ascontiguousarray(
        np.asarray(inputs["L0_attn_alpha_spike"], dtype=np.float32).reshape(HD))
    shared["beta_spike"] = np.ascontiguousarray(
        np.asarray(inputs["L0_attn_beta_spike"], dtype=np.float32).reshape(HD))
    shared.update(_CONSTS)
    maps = []
    for core in range(n_cores):
        m = dict(shared)
        m["x_tm"] = np.ascontiguousarray(x[B * core:B * core + B])
        maps.append(m)
    return maps


def kernel(**inputs):
    nc = build_nc(debug=False)
    in_maps = make_in_maps(inputs)
    res = run_bass_kernel_spmd(nc, in_maps, core_ids=list(range(8)))
    return np.concatenate([r["out"] for r in res.results], axis=0)


# version 11 (final)
# speedup vs baseline: 1.3110x; 1.0044x over previous
import numpy as np
import concourse.bass as bass
import concourse.mybir as mybir
from concourse.bass_utils import run_bass_kernel_spmd
from concourse.bass_core_rust import IndirectOffsetOnAxis

F32 = mybir.dt.float32
BF16 = mybir.dt.bfloat16
I32 = mybir.dt.int32
ALU = mybir.AluOpType
AF = mybir.ActivationFunctionType
AX = mybir.AxisListType

B = 2
T = 2048
D = 1024
H = 8
DK = 64
DV = 128
HD = H * DK
HV = H * DV
NT = B * T
NTILE = 16
C = 32
NCH = T // C
WARM = 192
CAP_PAD = 256
EPS = 1e-6
HUGE = 1 << 22

ENGS = ("sync", "vector", "scalar", "tensor", "gpsimd")


_EMITTING = {
    "tensor_scalar", "tensor_tensor", "tensor_copy", "memset", "memzero",
    "reciprocal", "select", "tensor_tensor_scan", "scalar_tensor_tensor",
    "tensor_reduce", "activation", "iota", "transpose", "stream_shuffle",
    "tensor_add", "tensor_sub", "max", "pool", "pool_avg", "pool_max",
    "bn_stats", "bn_aggr", "partition_broadcast",
}


class _ChainEng:

    def __init__(self, e, csem, counter):
        self._e = e
        self._csem = csem
        self._counter = counter
        self.prev_ins = None

    def __getattr__(self, name):
        a = getattr(self._e, name)
        if name not in _EMITTING:
            return a

        def call(*args, **kw):
            if self.prev_ins is not None and not getattr(self, "_nochain", False):
                self._counter[0] += 1
                self.prev_ins.then_inc(self._csem, 1)
                self._e.wait_ge(self._csem, self._counter[0])
            ins = a(*args, **kw)
            self.prev_ins = ins
            return ins
        return call


class Prog:
    def __init__(self, nc, stack):
        self.nc = nc
        self.ops = {e: [] for e in ENGS}
        self.sem_handles = {}
        self.sem_counts = {}
        self._stack = stack
        self._pe_base = [None]
        self.psD_evac = None
        self.n = 0
        self._chain_counts = {"vector": [0], "scalar": [0]}
        self._last_bar = None

    def sem(self, name):
        if name not in self.sem_handles:
            self.sem_handles[name] = self._stack.enter_context(self.nc.semaphore(name))
            self.sem_counts[name] = 0
        return self.sem_handles[name]

    def op(self, eng, fn, deps=(), inc=None, dma=False, chain=True):
        self.n += 1
        if inc is not None:
            self.sem(inc)
            self.sem_counts[inc] += 16 if dma else 1
            tok = (inc, self.sem_counts[inc])
        else:
            tok = None
        deps = tuple(d for d in deps if d is not None)
        for d in deps:
            self.sem(d[0])
        handles = self.sem_handles
        incs = (inc, 16 if dma else 1) if inc is not None else None
        chain = eng if chain and eng in ("vector", "scalar") and not dma else None
        if chain:
            self.sem("ch_" + chain)

        def record(e, _fn=fn, _deps=deps, _incs=incs, _chain=chain):
            for (sn, sv) in _deps:
                e.wait_ge(handles[sn], sv)
            if _chain:
                px = _ChainEng(e, handles["ch_" + _chain], self._chain_counts[_chain])
                ins = _fn(px)
                if ins is None:
                    ins = px.prev_ins
            else:
                px = _ChainEng(e, None, [0])
                px._nochain = True
                ins = _fn(px)
                if ins is None:
                    ins = px.prev_ins
            if _incs is not None:
                if ins is None:
                    ins = e.nop()
                ins.then_inc(handles[_incs[0]], _incs[1])
            return ins

        self.ops[eng].append(record)
        return tok

    def mm(self, fn, base, deps=(), inc=None):
        need_nop = self._pe_base[0] is not None and self._pe_base[0] != base
        self._pe_base[0] = base

        def wrapped(e, _fn=fn, _n=need_nop):
            if _n:
                e.nop()
                e.nop()
            return _fn(e)

        return self.op("tensor", wrapped, deps=deps, inc=inc)

    def barrier(self, name, engines=ENGS, dma=()):
        sem = self.sem("bar")
        prev = self._last_bar
        self.op("vector", lambda e: e.memset(self.s_barsc[:, 0:1], 0.0),
                deps=[prev] + [d for d in dma if d is not None], inc="bar")
        self.op("scalar", lambda e: e.activation(
            self.s_barsc[:, 1:2], self.s_barsc[:, 2:3], AF.Copy),
            deps=[prev], inc="bar")
        target = self.sem_counts["bar"]
        self._last_bar = ("bar", target)
        for e in engines:
            def record(eng, _t=target):
                eng.wait_ge(sem, _t)
                return None
            self.ops[e].append(record)
        self._pe_base[0] = None
        return ("bar", target)


def _host_consts():
    c = {}
    c["ident"] = np.eye(128, dtype=np.float32)
    i = np.arange(32)
    m_sl = (i[:, None] > i[None, :]).astype(np.float32)
    m_ui = (i[:, None] <= i[None, :]).astype(np.float32)
    eye32 = np.eye(32, dtype=np.float32)
    msl = np.zeros((128, 128), dtype=np.float32)
    for i in range(4):
        msl[32 * i:32 * i + 32, 32 * i:32 * i + 32] = m_sl
    c["mask_sl"] = msl
    c["mask_ui"] = np.tile(m_ui[None, :, None, :], (4, 1, 4, 1)).reshape(128, 128)
    c["i32blk"] = np.tile(eye32[None, :, None, :], (4, 1, 4, 1)).reshape(128, 128)
    ind = np.zeros((128, 4, 8), dtype=np.float32)
    indT = np.zeros((8, 4, 128), dtype=np.float32)
    for v in range(4):
        for p in range(128):
            h = 2 * v + p // 64
            ind[p, v, h] = 1.0
            indT[h, v, p] = 1.0
    c["ind8"] = np.ascontiguousarray(ind.reshape(128, 32))
    c["ind8T"] = np.ascontiguousarray(indT.reshape(8, 512))
    grp = np.arange(128) // 8
    bi = grp // 8
    e_of = grp % 8
    seg = np.arange(128) % 8
    c["g8"] = (grp[:, None] == grp[None, :]).astype(np.float32)
    c["e8"] = ((bi[:, None] == bi[None, :]) & (seg[:, None] == seg[None, :])).astype(np.float32)
    c["lt8"] = ((grp[:, None] == grp[None, :]) & (seg[:, None] < seg[None, :])).astype(np.float32)
    c["eoffm1"] = ((bi * 8 + e_of) * CAP_PAD).astype(np.float32).reshape(128, 1)
    blk = np.arange(32)
    c["iota_tm"] = np.ascontiguousarray(
        (np.arange(128)[:, None] + (blk[None, :] % NTILE) * 128
         + (blk[None, :] // NTILE) * T).astype(np.int32))
    c["huge_i"] = np.full((128, 32), HUGE, dtype=np.int32)
    return c


def build_nc(debug=False, nph=99, ph0cut=99, stcut=NCH, stpart=9):
    import contextlib
    nc = bass.Bass("TRN2", target_bir_lowering=False)
    stack = contextlib.ExitStack()
    P = Prog(nc, stack)
    dt = {}
    s = {}
    levels = []

    def din(name, shape, dtype=F32):
        dt[name] = nc.dram_tensor(name, list(shape), dtype, kind="ExternalInput").ap()

    def dout(name, shape, dtype=F32):
        dt[name] = nc.dram_tensor(name, list(shape), dtype, kind="ExternalOutput").ap()

    def dscr(name, shape, dtype=BF16):
        dt[name] = nc.dram_tensor(name, list(shape), dtype, kind="Internal").ap()

    _sbn = {}

    def sb(name, shape, dtype=F32):
        _sbn[name] = _sbn.get(name, 0) + 1
        uniq = "sb_" + name + ("" if _sbn[name] == 1 else "_%d" % _sbn[name])
        g = nc.sbuf_tensor(uniq, list(shape), dtype)
        s[name] = g.__enter__()
        levels[-1].append(g)

    def push():
        levels.append([])

    def pop():
        for g in reversed(levels.pop()):
            g.__exit__(None, None, None)

    def psum(name, shape, dtype=F32):
        s[name] = stack.enter_context(nc.psum_tensor("ps_" + name, list(shape), dtype))

    din("x_tm", (B, T, D))
    for n, shp in [("wq", (D, HD)), ("wk", (D, HD)), ("wv", (D, HV)), ("w_out", (HV, D)),
                   ("w_spike", (D, HD)), ("w_au", (D, 64)), ("b_au", (64,)), ("w_ad", (64, HD)),
                   ("b_ad", (HD,)), ("alpha_spike", (HD,)), ("w_beta", (D, H)), ("b_beta", (H,)),
                   ("beta_spike", (HD,)), ("w_hn", (H, DV)), ("w_u1", (D, 64)), ("b_u1", (64,)),
                   ("w_u2", (64, D)), ("b_u2", (D,)), ("w_norm_in", (D,)), ("conv_w", (D, 4)),
                   ("conv_b", (D,)), ("w_ffn", (D,)), ("ff1", (D, 4096)), ("ff3", (D, 4096)),
                   ("ff2", (4096, D)), ("gate_w", (D, H)), ("experts", (H, D, D))]:
        din(n, shp)
    for n in ["ident", "mask_sl", "mask_ui", "i32blk", "g8", "e8", "lt8"]:
        din(n, (128, 128))
    din("ind8", (128, 32))
    din("ind8T", (8, 512))
    din("eoffm1", (128, 1))
    din("iota_tm", (128, 32), I32)
    din("huge_i", (128, 32), I32)
    dout("out", (B, T, D))
    if debug:
        dout("dbg_xn", (B, T, D), BF16)
        dout("dbg_p", (128, 8, 512), BF16)
        dout("dbg_gates", (128, 256))
        dout("dbg_q", (B, T, HD), BF16)
        dout("dbg_spk", (128, 8, 512), BF16)
        dout("dbg_beta", (8, NT))
        dout("dbg_ahat", (128, 8, 512), BF16)
        dout("dbg_y", (B, T, HV), BF16)
        dout("dbg_y1", (B, T, D), BF16)
        dout("dbg_ymoe", (B, T, D), BF16)
        dout("dbg_idx", (B * H * CAP_PAD, 1), I32)
    dscr("xn_d", (B * T, D))
    dscr("q_d", (B, T, HD))
    dscr("k_d", (B, T, HD))
    dscr("v_d", (B, T, HV))
    dscr("ab_d", (B, 4, 128, T))
    dscr("kh_d", (8, 128, T))
    dscr("kt_d", (8, 128, T))
    dscr("qh_d", (8, 128, T))
    dscr("ktld_d", (B, T, HD))
    dscr("y1_d", (B, T, D))
    dscr("y2p_d", (B, T, D))
    dscr("ymoe_d", (B * T, D))
    dscr("idx_d", (B * H * CAP_PAD, 1), I32)
    dscr("slot_d", (B, 8, 256), F32)
    dscr("g256_d", (128, 256), F32)

    push()
    sb("identb", (128, 128), BF16)
    sb("identf", (128, 128), F32)
    sb("mask_sl", (128, 128), BF16)
    sb("mask_ui", (128, 128), BF16)
    sb("i32blk", (128, 128), BF16)
    sb("ind8", (128, 32), BF16)
    sb("ind8b", (128, 32), BF16)
    sb("ind8T", (8, 512), F32)
    sb("g8", (128, 128), F32)
    sb("e8", (128, 128), F32)
    sb("lt8", (128, 128), F32)
    sb("eoffm1", (128, 1), F32)
    sb("iota_tm", (128, 32), I32)
    sb("huge_i", (128, 32), I32)
    sb("wau", (128, 8, 64), BF16)
    sb("wad", (64, HD), BF16)
    sb("wu1", (128, 8, 64), BF16)
    sb("wu2", (64, D), BF16)
    sb("wbeta_e", (128, 8, 8), BF16)
    sb("gatew", (128, 8, 8), BF16)
    sb("w_norm", (128, 8), F32)
    sb("conv_wc", (128, 8, 4), F32)
    sb("conv_bc", (128, 8), F32)
    sb("b_adc", (128, 4), F32)
    sb("aspc", (128, 4), F32)
    sb("bspc", (128, 4), F32)
    sb("b_auc", (64, 1), F32)
    sb("b_u1c", (64, 1), F32)
    sb("bbeta", (8, 1), F32)
    sb("whn_col", (128, 8), F32)
    sb("b_u2B", (128, D), F32)
    sb("wffnB", (128, D), F32)
    sb("ngcs", (8, 1), F32)
    sb("band", (128, NT), F32)
    sb("u1", (64, NT), BF16)
    sb("betaTM", (128, 32, 8), F32)
    sb("rstd2", (128, 32), F32)
    sb("aCt", (128, 8, NCH), F32)
    sb("scr_a", (128, 1024), F32)
    sb("scr_b", (128, 1024), F32)
    sb("scr_c", (128, 1024), F32)
    sb("scr_e", (128, 2048), F32)
    sb("scr_bf", (128, 2, 1024), BF16)
    sb("colA", (128, 16), F32)
    sb("c1", (128, 512), F32)
    sb("c09", (128, 512), F32)
    sb("ustate", (128, 8, 16), F32)
    sb("spksc", (128, 8, 16), F32)
    sb("stats", (128, 4), F32)
    s["onesrow"] = s["c1"].ap()[0:1, :]
    s["b_u2r"] = s["scr_a"].ap()[0:1, 0:1024]
    s["wffn_r"] = s["scr_b"].ap()[0:1, 0:1024]
    s["rowsc"] = s["scr_e"].ap()[0:1, 1536:1920]
    s["active"] = s["band"].ap()[0:8, :]
    s["gates"] = s["band"].ap()[32:40, :]
    s["bb"] = s["band"].ap()[64:72, :]
    s["beta"] = s["band"].ap()[96:104, :]

    psum("psA", (128, 512), F32)
    psum("psB", (128, 512), F32)
    psum("psC", (128, 512), F32)
    psum("psD", (128, 512), F32)
    psum("psE", (128, 512), F32)
    psum("psF", (128, 512), F32)
    psum("psG", (128, 512), F32)
    psum("psT", (128, 1024), BF16)

    sb("barsc", (128, 4), F32)
    P.s_barsc = s["barsc"]
    P._last_bar = P.op("vector", lambda e: e.memset(s["barsc"][:], 0.0), inc="dve")
    ctx = dict(nc=nc, P=P, dt=dt, s=s, debug=debug, ph0cut=ph0cut, stcut=stcut, stpart=stpart,
               evac={k: None for k in
                     ("psA", "psB", "psC", "psD", "psE", "psF", "psG", "psT", "bwin")})

    _ph0(ctx)

    push()
    sb("spk", (128, 8, T), BF16)
    push()
    sb("A", (128, 8, 2049), BF16)
    push()
    sb("wq", (128, 8, HD), BF16)
    sb("wk", (128, 8, HD), BF16)
    sb("wvh", (128, 8, 512), BF16)
    sb("wsph", (128, 8, 128), BF16)
    sb("xtm", (128, D), F32)
    sb("xnfm", (128, 8, 128), BF16)
    sb("h1", (128, 8, 516), BF16)
    sb("hc", (128, 8, 512), BF16)
    sb("au", (64, 512), BF16)
    sb("abstg", (128, 512), BF16)
    sb("scr_d", (128, 384), F32)
    if nph > 1:
        _ph1(ctx)
    pop()
    if nph > 2:
        _ph_spike(ctx)
    push()
    sb("lam", (128, 8, 2049), F32)
    if nph > 3:
        _ph_alpha(ctx)
    if nph > 4:
        _ph_ahat(ctx)
    pop()
    push()
    sb("qkt", (128, 4, 512), BF16)
    sb("kstg", (128, 4, 128), BF16)
    sb("k2stg", (128, 4, 128), BF16)
    sb("qstg", (128, 4, 128), BF16)
    sb("ktstg", (128, 512), BF16)
    if nph > 5:
        _ph_fold(ctx)
    pop()
    pop()
    pop()
    push()
    sb("Nst", (128, 16, 512), BF16)
    sb("Bst", (128, 16, 512), BF16)
    push()
    sb("khw", (128, 8, 512), BF16)
    sb("ktw", (128, 8, 512), BF16)
    sb("qhw", (128, 8, 512), BF16)
    sb("wpool", (128, 14, 512), BF16)
    sb("bwin", (128, 4), F32)
    if nph > 6:
        _ph_prep(ctx)
    pop()
    push()
    sb("wexp", (128, 2, 8, 1024), BF16)
    sb("xg", (128, 2, 1024), BF16)
    sb("xgfm", (128, 2, 8, 128), BF16)
    sb("yexp", (128, 2, 1024), BF16)
    sb("idx2", (128, 2), I32)
    sb("ztile", (128, 1024), BF16)
    sb("g256", (128, 256), F32)
    sb("m256", (128, 256), F32)
    sb("x256", (128, 256), F32)
    sb("lo256", (128, 1), F32)
    sb("hi256", (128, 1), F32)
    sb("th256", (128, 1), F32)
    sb("sel256", (128, 256), F32)
    sb("slotsb", (16, 256), F32)
    sb("slotTM", (128, 32), F32)
    sb("slot32", (128, 32), I32)
    sb("idxcol", (128, 1), I32)
    if nph > 7:
        _ph_sinkhorn_moe(ctx)
    pop()
    push()
    sb("wout", (128, 8, D), BF16)
    sb("khs", (128, 2, 8, 128), BF16)
    sb("qhs", (128, 2, 8, 128), BF16)
    sb("ktls", (128, 2, B * HD), BF16)
    sb("vtm", (128, 2, B * HV), BF16)
    sb("Sst", (128, 8, DV), F32)
    sb("Sbf", (128, 8, DV), BF16)
    sb("Xst", (128, B * HV), BF16)
    sb("Wst", (128, B * HV), BF16)
    sb("ytm", (128, 2, B * HV), BF16)
    sb("yfm", (128, 8, 128), BF16)
    sb("y1t", (128, 2, D), BF16)
    sb("scr_y", (128, 1024), F32)
    sb("yraw", (128, 2048), F32)
    if nph > 8:
        _ph_state(ctx)
    pop()
    pop()
    push()
    sb("ff1h", (128, 8, 2048), BF16)
    sb("ff3h", (128, 8, 2048), BF16)
    sb("ff2h", (128, 16, 1024), BF16)
    sb("h2fm", (128, 1, 8, 512), BF16)
    sb("h2st", (128, 2, 1024), BF16)
    sb("hhfm", (128, 16, 512), BF16)
    sb("y1t", (128, 2, D), BF16)
    sb("ymt", (128, 1024), BF16)
    s["xtf"] = s["scr_e"].ap()[:, 1024:2048].rearrange("p (o d) -> p o d", o=1)
    if nph > 9:
        _ph_ffn(ctx)
    pop()
    pop()

    with nc.Block() as block:
        @block.sync
        def _(e):
            for f in P.ops["sync"]:
                f(e)

        @block.vector
        def _(e):
            for f in P.ops["vector"]:
                f(e)

        @block.scalar
        def _(e):
            for f in P.ops["scalar"]:
                f(e)

        @block.tensor
        def _(e):
            for f in P.ops["tensor"]:
                f(e)

        @block.gpsimd
        def _(e):
            for f in P.ops["gpsimd"]:
                f(e)

    stack.close()
    return nc


def _ph0(ctx):
    P, dt, s = ctx["P"], ctx["dt"], ctx["s"]
    cut = ctx.get("ph0cut", 99)

    def mkregs(e):
        ctx["bc_slot"] = e.to_reg(B * H * CAP_PAD - 1)
        ctx["bc_tok"] = e.to_reg(B * T - 1)
        return None
    P.op("gpsimd", mkregs)

    def cl(dst, src):
        return P.op("gpsimd", lambda e, d=dst, r=src: e.dma_start(out=d, in_=r), inc="w0g", dma=True)

    def ld(dst, src):
        return P.op("sync", lambda e, d=dst, r=src: e.dma_start(out=d, in_=r), inc="w0x", dma=True)

    ld(s["identf"][:], dt["ident"])
    cl(s["identb"][:], dt["ident"])
    cl(s["mask_sl"][:], dt["mask_sl"])
    cl(s["mask_ui"][:], dt["mask_ui"])
    cl(s["i32blk"][:], dt["i32blk"])
    cl(s["ind8"][:], dt["ind8"])
    ld(s["ind8T"][:], dt["ind8T"])
    ld(s["g8"][:], dt["g8"])
    ld(s["e8"][:], dt["e8"])
    ld(s["lt8"][:], dt["lt8"])
    ld(s["eoffm1"][:], dt["eoffm1"])
    ld(s["iota_tm"][:], dt["iota_tm"])
    ld(s["huge_i"][:], dt["huge_i"])
    cl(s["wau"][:], dt["w_au"].rearrange("(c p) o -> p c o", p=128))
    cl(s["wad"][:], dt["w_ad"])
    cl(s["wu1"][:], dt["w_u1"].rearrange("(c p) o -> p c o", p=128))
    cl(s["wu2"][:], dt["w_u2"])
    cl(s["wbeta_e"][:], dt["w_beta"].rearrange("(c p) o -> p c o", p=128))
    cl(s["gatew"][:], dt["gate_w"].rearrange("(c p) o -> p c o", p=128))
    ld(s["scr_c"][0:8, 0:128], dt["w_norm_in"].rearrange("(c p) -> c p", p=128))
    ld(s["scr_c"][0:8, 128:256], dt["conv_b"].rearrange("(c p) -> c p", p=128))
    ld(s["scr_c"][0:8, 256:384], dt["w_hn"])
    ld(s["scr_c"][0:4, 384:512], dt["b_ad"].rearrange("(c p) -> c p", p=128))
    ld(s["scr_c"][0:4, 512:640], dt["alpha_spike"].rearrange("(c p) -> c p", p=128))
    ld(s["scr_c"][0:4, 640:768], dt["beta_spike"].rearrange("(c p) -> c p", p=128))
    ld(s["scr_c"][32:40, 0:512], dt["conv_w"].rearrange("(c p) o -> c (p o)", p=128))
    ld(s["b_auc"][:], dt["b_au"].rearrange("o -> o ()"))
    ld(s["b_u1c"][:], dt["b_u1"].rearrange("o -> o ()"))
    ld(s["bbeta"][:], dt["b_beta"].rearrange("h -> h ()"))
    ld(s["b_u2r"][:], dt["b_u2"].rearrange("d -> () d"))
    ld(s["wffn_r"][:], dt["w_ffn"].rearrange("d -> () d"))
    wld = ("w0x", P.sem_counts["w0x"])
    gld = ("w0g", P.sem_counts["w0g"])
    ctx["wld"] = wld
    if cut <= 1:
        return

    t0 = P.op("vector", lambda e: e.memset(s["c1"][:], 1.0), inc="dve")
    P.op("vector", lambda e: e.memset(s["c09"][:], 0.9))
    P.op("vector", lambda e: e.memset(s["ustate"][:], 0.5), inc="dve")

    def bcast(dst, row_ap, pbank, deps):
        tm = P.mm(lambda e, r=row_ap, p=pbank: e.matmul(
            s[p][0:128, 0:512], s["onesrow"][0:1, 0:128], r, start=True, stop=True),
            base=0, deps=deps, inc="pe")
        return P.op("vector", lambda e, d=dst, p=pbank: e.tensor_copy(d, s[p][:, 0:512]),
                    deps=[tm], inc="dve")

    if cut <= 2:
        return
    bcast(s["b_u2B"][:, 0:512], s["b_u2r"][0:1, 0:512], "psA", [t0, wld])
    bcast(s["b_u2B"][:, 512:1024], s["b_u2r"][0:1, 512:1024], "psB", [t0, wld])
    bcast(s["wffnB"][:, 0:512], s["wffn_r"][0:1, 0:512], "psC", [t0, wld])
    bcast(s["wffnB"][:, 512:1024], s["wffn_r"][0:1, 512:1024], "psD", [t0, wld])

    if cut <= 3:
        return

    if cut <= 4:
        return
    specs = [("w_norm", 0, 8), ("conv_bc", 128, 8), ("whn_col", 256, 8),
             ("b_adc", 384, 4), ("aspc", 512, 4), ("bspc", 640, 4)]
    tk = None
    for i, (nm, off, nr) in enumerate(specs):
        tk = P.mm(lambda e, off=off, nr=nr, i=i: e.transpose(
            s["psF"][0:128, i * 8:i * 8 + nr], s["scr_c"][0:nr, off:off + 128],
            s["identf"][0:nr, 0:nr]), base=0, deps=[wld] if i == 0 else (),
            inc="pe" if i == 5 else None)

    def vcopy(e):
        ins = None
        for i, (nm, off, nr) in enumerate(specs):
            ins = e.tensor_copy(s[nm][:], s["psF"][0:128, i * 8:i * 8 + nr])
        return ins
    vt1 = P.op("vector", vcopy, deps=[tk], inc="dve", chain=False)
    cwv = s["scr_c"].ap()[32:40, 0:512].rearrange("c (p o) -> c o p", p=128)
    tk = None
    for o in range(4):
        tk = P.mm(lambda e, o=o: e.transpose(
            s["psG"][0:128, 48 + o * 8:56 + o * 8], cwv[:, o, :],
            s["identf"][32:40, 32:40], tile_position=(32, 0)),
            base=32, inc="pe" if o == 3 else None)

    def ccopy(e):
        ins = None
        for o in range(4):
            ins = e.tensor_copy(s["conv_wc"][:, :, o], s["psG"][0:128, 48 + o * 8:56 + o * 8])
        return ins
    vt2 = P.op("vector", ccopy, deps=[tk], inc="dve", chain=False)
    vtr = [vt1, vt1, vt1, vt1, vt1, vt2]

    if cut <= 5:
        return
    for v in range(4):
        P.op("vector", lambda e, v=v: e.tensor_scalar(
            s["ind8b"][:, v * 8:(v + 1) * 8], s["ind8"][:, v * 8:(v + 1) * 8],
            s["bspc"][:, v:v + 1], None, ALU.mult), deps=[gld, vtr[5]], inc="dve")
    P.barrier("ph0")


def _ph1(ctx):
    P, dt, s, ev = ctx["P"], ctx["dt"], ctx["s"], ctx["evac"]
    dbg = ctx["debug"]
    prev = {}

    for nm, srcn in (("wq", "wq"), ("wk", "wk")):
        P.op("gpsimd", lambda e, nm=nm, srcn=srcn: e.dma_start(
            out=s[nm][:], in_=dt[srcn].rearrange("(c p) o -> p c o", p=128)), inc="wqk", dma=True)
    pwld = ("wqk", P.sem_counts["wqk"])
    z1 = P.op("vector", lambda e: e.memset(s["A"][:, :, 0:1], 0.0), inc="dve")
    z2 = P.op("vector", lambda e: e.memset(s["h1"][:, :, 0:4], 0.0), inc="dve")
    prev[("h1halo",)] = z2
    prev[("pz",)] = z1

    for g in range(B * 4):
        b, gb = divmod(g, 4)
        t0 = gb * 512
        h1_last = None

        for it in range(4):
            tt = t0 + it * 128
            dbuf = (g * 4 + it) % 2
            lx = P.op("sync", lambda e, b=b, tt=tt: e.dma_start(
                out=s["xtm"][:], in_=dt["x_tm"][b, tt:tt + 128, :]),
                deps=[prev.get(("xtm",))], inc="xtm", dma=True)
            a1 = P.op("vector", lambda e: e.tensor_scalar(
                s["scr_a"][:, 0:1024], s["xtm"][:], 1.0, 0.0, ALU.mult, ALU.add,
                accum_out=s["colA"][:, 0:1]), deps=[lx], inc="dve")
            a2 = P.op("scalar", lambda e: e.activation(
                s["scr_b"][:, 0:1024], s["xtm"][:], AF.Square,
                accum_out=s["colA"][:, 1:2]), deps=[lx], inc="act")

            def stats_fn(e):
                e.tensor_scalar(s["colA"][:, 2:3], s["colA"][:, 0:1], 1.0 / D, None, ALU.mult)
                e.tensor_scalar(s["colA"][:, 3:4], s["colA"][:, 1:2], 1.0 / D, None, ALU.mult)
                e.tensor_tensor(s["colA"][:, 4:5], s["colA"][:, 2:3], s["colA"][:, 2:3], ALU.mult)
                e.tensor_tensor(s["colA"][:, 5:6], s["colA"][:, 3:4], s["colA"][:, 4:5], ALU.subtract)
                return e.tensor_scalar(s["colA"][:, 6:7], s["colA"][:, 5:6], 1e-5, None, ALU.add)
            a3 = P.op("vector", stats_fn, deps=[a1, a2], inc="dve")
            a4 = P.op("scalar", lambda e: e.activation(s["colA"][:, 7:8], s["colA"][:, 6:7], AF.Sqrt),
                      deps=[a3], inc="act")
            a5 = P.op("vector", lambda e: e.reciprocal(s["stats"][:, 0:1], s["colA"][:, 7:8]),
                      deps=[a4], inc="dve")

            def stats2_fn(e):
                e.tensor_tensor(s["stats"][:, 1:2], s["colA"][:, 2:3], s["stats"][:, 0:1], ALU.mult)
                e.tensor_tensor(s["colA"][:, 8:9], s["stats"][:, 0:1], s["stats"][:, 0:1], ALU.mult)
                e.tensor_tensor(s["colA"][:, 9:10], s["colA"][:, 5:6], s["colA"][:, 8:9], ALU.mult)
                return e.tensor_scalar(s["colA"][:, 10:11], s["colA"][:, 9:10], EPS, None, ALU.add)
            a6 = P.op("vector", stats2_fn, deps=[a5], inc="dve")
            a7 = P.op("scalar", lambda e: e.activation(s["colA"][:, 11:12], s["colA"][:, 10:11], AF.Sqrt),
                      deps=[a6], inc="act")
            a8 = P.op("vector", lambda e: e.reciprocal(s["stats"][:, 2:3], s["colA"][:, 11:12]),
                      deps=[a7], inc="dve")
            a9 = P.op("vector", lambda e, d=dbuf: e.tensor_scalar(
                s["scr_bf"][:, d, :], s["xtm"][:], s["colA"][:, 2:3], s["stats"][:, 0:1],
                ALU.subtract, ALU.mult), deps=[a8, prev.get(("xnbf", dbuf))], inc="dve")
            sx = P.op("sync", lambda e, b=b, tt=tt, d=dbuf: e.dma_start(
                out=dt["xn_d"][b * T + tt:b * T + tt + 128, :], in_=s["scr_bf"][:, d, :]),
                deps=[a9], inc="sxn%d" % dbuf, dma=True)
            if dbg:
                sx = P.op("sync", lambda e, b=b, tt=tt, d=dbuf: e.dma_start(
                    out=dt["dbg_xn"][b, tt:tt + 128, :], in_=s["scr_bf"][:, d, :]),
                    deps=[a9], inc="sxn%d" % dbuf, dma=True)
            trl = None
            for j in range(8):
                trl = P.mm(lambda e, j=j, d=dbuf: e.transpose(
                    s["psT"][:, j * 128:(j + 1) * 128], s["scr_bf"][:, d, j * 128:(j + 1) * 128],
                    s["identb"][:]), base=0, deps=[a9, ev["psT"]] if j == 0 else (),
                    inc="pe" if j == 7 else None)
            prev[("xnbf", dbuf)] = sx
            exn = P.op("vector", lambda e: e.tensor_copy(
                s["xnfm"][:],
                s["psT"][:].rearrange("p (c t) -> p c t", c=8)),
                deps=[trl, prev.get(("xnfm",))], inc="dve")
            ev["psT"] = exn
            u1m = None
            for j in range(8):
                u1m = P.mm(lambda e, j=j: e.matmul(
                    s["psG"][0:64, 0:128], s["wu1"][:, j, :], s["xnfm"][:, j, :],
                    start=(j == 0), stop=(j == 7)), base=0,
                    deps=[exn, ev["psG"]] if j == 0 else (), inc="pe" if j == 7 else None)
            ue = P.op("scalar", lambda e, b=b, tt=tt: e.activation(
                s["u1"][0:64, b * T + tt:b * T + tt + 128], s["psG"][0:64, 0:128], AF.Silu,
                bias=s["b_u1c"][:], scale=1.0), deps=[u1m], inc="act")
            ev["psG"] = ue
            prev[("xnfm",)] = u1m
            prev[("xtm",)] = a9
            mmL = None
            for j in range(8):
                mmL = P.mm(lambda e, j=j: e.matmul(
                    s["psC"][0:8, 0:128], s["gatew"][:, j, :], s["xnfm"][:, j, :],
                    start=(j == 0), stop=(j == 7)), base=0,
                    deps=[exn, ev["psC"]] if j == 0 else (), inc="pe" if j == 7 else None)
            lg = P.op("vector", lambda e, b=b, tt=tt: e.tensor_copy(
                s["gates"][0:8, b * T + tt:b * T + tt + 128], s["psC"][0:8, 0:128]),
                deps=[mmL], inc="dve")
            ev["psC"] = lg
            trS = P.mm(lambda e: e.transpose(
                s["psD"][0:1, 256:384], s["stats"][:, 2:3], s["identf"][:]),
                base=0, deps=[a8, ev["psD"]], inc="pe")
            eS = P.op("vector", lambda e: e.tensor_copy(s["rowsc"][0:1, 256:384],
                                                        s["psD"][0:1, 256:384]),
                      deps=[trS], inc="dve")
            ev["psD"] = eS
            bc3 = P.mm(lambda e: e.matmul(s["psF"][0:128, 0:128], s["onesrow"][0:1, 0:128],
                                          s["rowsc"][0:1, 256:384], start=True, stop=True),
                       base=0, deps=[eS, ev["psF"]], inc="pe")
            h1l = None
            for j in range(8):
                h1l = P.op("vector", lambda e, j=j, it=it: e.scalar_tensor_tensor(
                    s["h1"][:, j, 4 + it * 128: 4 + (it + 1) * 128],
                    s["xnfm"][:, j, :], s["w_norm"][:, j:j + 1],
                    s["psF"][0:128, 0:128], ALU.mult, ALU.mult),
                    deps=[bc3, exn, prev.get(("h1halo",))] if j == 0 else (),
                    inc="dve" if j == 7 else None)
            ev["psF"] = h1l
            h1_last = h1l
            prev[("xnfm",)] = h1l

        hc_toks = []
        for j in range(8):
            def conv_fn(e, j=j):
                e.tensor_scalar(s["scr_e"][:, 0:512], s["h1"][:, j, 1:513],
                                s["conv_wc"][:, j, 0:1], s["conv_bc"][:, j:j + 1], ALU.mult, ALU.add)
                e.scalar_tensor_tensor(s["scr_e"][:, 0:512], s["h1"][:, j, 2:514],
                                       s["conv_wc"][:, j, 1:2], s["scr_e"][:, 0:512], ALU.mult, ALU.add)
                e.scalar_tensor_tensor(s["scr_e"][:, 0:512], s["h1"][:, j, 3:515],
                                       s["conv_wc"][:, j, 2:3], s["scr_e"][:, 0:512], ALU.mult, ALU.add)
                return e.scalar_tensor_tensor(s["scr_e"][:, 0:512], s["h1"][:, j, 4:516],
                                              s["conv_wc"][:, j, 3:4], s["scr_e"][:, 0:512], ALU.mult, ALU.add)
            cv = P.op("vector", conv_fn,
                      deps=[h1_last, prev.get(("hc", j)), prev.get(("hc_last",))], inc="dve")
            sc = P.op("scalar", lambda e, j=j: e.activation(s["hc"][:, j, :], s["scr_e"][:, 0:512],
                                                            AF.Silu), deps=[cv, pwld], inc="act")
            hc_toks.append(sc)
            prev[("hc", j)] = sc
            prev[("hc_last",)] = sc
        hc_done = hc_toks[-1]
        if gb < 3:
            hhalo = P.op("vector", lambda e: e.tensor_copy(s["h1"][:, :, 0:4], s["h1"][:, :, 512:516]),
                         deps=[hc_done], inc="dve")
        else:
            hhalo = P.op("vector", lambda e: e.memset(s["h1"][:, :, 0:4], 0.0),
                         deps=[hc_done], inc="dve")
        prev[("h1halo",)] = hhalo

        aum = None
        for j in range(8):
            aum = P.mm(lambda e, j=j: e.matmul(
                s["psE"][0:64, 0:512], s["wau"][:, j, :], s["hc"][:, j, :],
                start=(j == 0), stop=(j == 7)), base=0,
                deps=[hc_done, ev["psE"]] if j == 0 else (), inc="pe" if j == 7 else None)
        aue = P.op("scalar", lambda e: e.activation(s["au"][:], s["psE"][0:64, 0:512], AF.Silu,
                                                    bias=s["b_auc"][:], scale=1.0),
                   deps=[aum], inc="act")
        ev["psE"] = aue
        for oc in range(4):
            abm = P.mm(lambda e, oc=oc: e.matmul(
                s["psF"][0:128, 0:512], s["wad"][:, oc * 128:(oc + 1) * 128], s["au"][:],
                start=True, stop=True), base=0, deps=[aue, ev["psF"]], inc="pe")
            abe = P.op("vector", lambda e, oc=oc: e.tensor_scalar(
                s["abstg"][:], s["psF"][0:128, 0:512], s["b_adc"][:, oc:oc + 1],
                None, ALU.add), deps=[abm, prev.get(("abstg",))], inc="dve")
            ev["psF"] = abe
            sab = P.op("sync", lambda e, b=b, t0=t0, oc=oc: e.dma_start(
                out=dt["ab_d"][b, oc, :, t0:t0 + 512], in_=s["abstg"][:]),
                deps=[abe], inc="sab", dma=True)
            prev[("abstg",)] = sab

        bbm = None
        for j in range(8):
            bbm = P.mm(lambda e, j=j: e.matmul(
                s["psD"][0:8, 0:512], s["wbeta_e"][:, j, :], s["hc"][:, j, :],
                start=(j == 0), stop=(j == 7)), base=0,
                deps=[hc_done, ev["psD"]] if j == 0 else (), inc="pe" if j == 7 else None)
        bbe = P.op("vector", lambda e, b=b, t0=t0: e.tensor_scalar(
            s["bb"][0:8, b * T + t0:b * T + t0 + 512], s["psD"][0:8, 0:512], s["bbeta"][:],
            None, ALU.add), deps=[bbm], inc="dve")
        ev["psD"] = bbe

        for oc in range(4):
            lsp = P.op("gpsimd", lambda e, oc=oc: e.dma_start(
                out=s["wsph"][:], in_=dt["w_spike"][:, oc * 128:(oc + 1) * 128]
                .rearrange("(c p) o -> p c o", p=128)),
                deps=[prev.get(("wsph",))], inc="wsp", dma=True)
            sdm = None
            for j in range(8):
                sdm = P.mm(lambda e, j=j: e.matmul(
                    s["psC"][0:128, 0:512], s["wsph"][:, j, :], s["hc"][:, j, :],
                    start=(j == 0), stop=(j == 7)), base=0,
                    deps=[hc_done, lsp, ev["psC"]] if j == 0 else (), inc="pe" if j == 7 else None)
            prev[("wsph",)] = sdm
            r = b * 4 + oc
            pscan = P.op("vector", lambda e, r=r, t0=t0: e.tensor_tensor_scan(
                s["A"][:, r, 1 + t0:1 + t0 + 512], s["c09"][:], s["psC"][0:128, 0:512],
                s["A"][:, r, t0:t0 + 1], ALU.mult, ALU.add),
                deps=[sdm, prev.get(("sd", oc)), prev.get(("pz",))], inc="dve")
            ev["psC"] = pscan
            prev[("sd", oc)] = pscan

        for it in range(4):
            tt = t0 + it * 128
            for which, bank, wname, dst in (("q", "psA", "wq", "q_d"), ("k", "psB", "wk", "k_d")):
                qm = None
                for j in range(8):
                    qm = P.mm(lambda e, j=j, it=it, bank=bank, wname=wname: e.matmul(
                        s[bank][0:128, 0:512], s["hc"][:, j, it * 128:(it + 1) * 128],
                        s[wname][:, j, :], start=(j == 0), stop=(j == 7)), base=0,
                        deps=[hc_done, ev[bank]] if j == 0 else (), inc="pe" if j == 7 else None)
                sq = P.op("scalar", lambda e, bank=bank: e.activation(
                    s["scr_a"][:, 0:512], s[bank][0:128, 0:512], AF.Square), deps=[qm], inc="act")
                rd = P.op("vector", lambda e: e.tensor_reduce(
                    s["colA"][:, 0:8], s["scr_a"][:, 0:512].rearrange("p (h d) -> p h d", h=8),
                    AX.X, ALU.add), deps=[sq], inc="dve")
                sq2 = P.op("scalar", lambda e: e.activation(s["colA"][:, 8:16], s["colA"][:, 0:8],
                                                            AF.Sqrt), deps=[rd], inc="act")

                def rn_fn(e):
                    e.tensor_scalar(s["colA"][:, 0:8], s["colA"][:, 8:16], EPS, None, ALU.add)
                    return e.reciprocal(s["colA"][:, 8:16], s["colA"][:, 0:8])
                rn = P.op("vector", rn_fn, deps=[sq2], inc="dve")
                ap = P.op("vector", lambda e, bank=bank: e.tensor_tensor(
                    s["scr_bf"][:, 0, 0:512].rearrange("p (h d) -> p h d", h=8),
                    s[bank][0:128, 0:512].rearrange("p (h d) -> p h d", h=8),
                    s["colA"][:, 8:16].rearrange("p h -> p h ()").broadcast_to([128, 8, 64]),
                    ALU.mult), deps=[rn, prev.get(("qkstg",)), prev.get(("xnbf", 0))], inc="dve")
                ev[bank] = ap
                sq_dma = P.op("sync", lambda e, b=b, tt=tt, dst=dst: e.dma_start(
                    out=dt[dst][b, tt:tt + 128, :], in_=s["scr_bf"][:, 0, 0:512]),
                    deps=[ap], inc="sqk", dma=True)
                if dbg and which == "q":
                    sq_dma = P.op("sync", lambda e, b=b, tt=tt: e.dma_start(
                        out=dt["dbg_q"][b, tt:tt + 128, :], in_=s["scr_bf"][:, 0, 0:512]),
                        deps=[ap], inc="sqk", dma=True)
                prev[("qkstg",)] = sq_dma
                prev[("xnbf", 0)] = sq_dma

        for half in range(2):
            bank = ("psE", "psF")[half]
            lwv = P.op("gpsimd", lambda e, half=half: e.dma_start(
                out=s["wvh"][:], in_=dt["wv"][:, half * 512:(half + 1) * 512]
                .rearrange("(c p) o -> p c o", p=128)),
                deps=[prev.get(("wvh",))], inc="wvl", dma=True)
            vm_last = None
            for it in range(4):
                tt = t0 + it * 128
                vm = None
                for j in range(8):
                    vm = P.mm(lambda e, j=j, it=it, bank=bank: e.matmul(
                        s[bank][0:128, 0:512], s["hc"][:, j, it * 128:(it + 1) * 128],
                        s["wvh"][:, j, :],
                        start=(j == 0), stop=(j == 7)), base=0,
                        deps=[hc_done, lwv, ev[bank]] if j == 0 else (),
                        inc="pe" if j == 7 else None)
                ve = P.op("vector", lambda e, half=half, bank=bank: e.tensor_copy(
                    s["scr_bf"][:, 1, half * 512:(half + 1) * 512], s[bank][0:128, 0:512]),
                    deps=[vm, prev.get(("vstg",)), prev.get(("xnbf", 1))], inc="dve")
                ev[bank] = ve
                sv = P.op("sync", lambda e, b=b, tt=tt, half=half: e.dma_start(
                    out=dt["v_d"][b, tt:tt + 128, half * 512:(half + 1) * 512],
                    in_=s["scr_bf"][:, 1, half * 512:(half + 1) * 512]),
                    deps=[ve], inc="svd", dma=True)
                prev[("vstg",)] = sv
                prev[("xnbf", 1)] = sv
                vm_last = vm
            prev[("wvh",)] = vm_last

    P.barrier("ph1", dma=[prev.get(("qkstg",)), prev.get(("vstg",)),
                          prev.get(("xnbf", 0)), prev.get(("xnbf", 1)),
                          prev.get(("abstg",))])
    if ctx["debug"]:
        tk = P.op("sync", lambda e: e.dma_start(out=dt["dbg_p"], in_=s["A"][:, :, 1:513]),
                  inc="dbgs", dma=True)
        P.op("vector", lambda e: e.memset(P.s_barsc[:, 3:4], 0.0), deps=[tk], inc="dve")
        P.barrier("ph1dbg")


def _ph_spike(ctx):
    P, s = ctx["P"], ctx["s"]
    Av = s["A"][:, :, 1:2049].rearrange("p r (c t) -> p r c t", c=16)
    Sv = s["spk"][:].rearrange("p r (c t) -> p r c t", c=16)
    last = None
    for step in range(-WARM, 128):
        if step < 0:
            cc0 = (-step + 127) // 128
            off = step + cc0 * 128

            def fstep(e, cc0=cc0, off=off):
                e.tensor_tensor(s["spksc"][:, :, cc0:16],
                                Av[:, :, 0:16 - cc0, off],
                                s["ustate"][:, :, cc0:16], ALU.is_gt)
                e.tensor_scalar(s["spksc"][:, :, cc0:16], s["spksc"][:, :, cc0:16],
                                0.45, 0.05, ALU.mult, ALU.add)
                return e.scalar_tensor_tensor(s["ustate"][:, :, cc0:16],
                                              s["ustate"][:, :, cc0:16],
                                              0.9, s["spksc"][:, :, cc0:16],
                                              ALU.mult, ALU.add)
            last = P.op("vector", fstep, deps=[last], inc="dve")
        else:
            def fstep(e, step=step):
                e.tensor_tensor(Sv[:, :, :, step], Av[:, :, :, step],
                                s["ustate"][:], ALU.is_gt)
                e.tensor_scalar(s["spksc"][:], Sv[:, :, :, step], 0.45, 0.05,
                                ALU.mult, ALU.add)
                return e.scalar_tensor_tensor(s["ustate"][:], s["ustate"][:], 0.9,
                                              s["spksc"][:], ALU.mult, ALU.add)
            last = P.op("vector", fstep, deps=[last], inc="dve")
    P.barrier("spike")
    if ctx["debug"]:
        tk = P.op("sync", lambda e: e.dma_start(out=ctx["dt"]["dbg_spk"],
                                                in_=s["spk"][:, :, 0:512]), inc="dbgs", dma=True)
        P.op("vector", lambda e: e.memset(P.s_barsc[:, 3:4], 0.0), deps=[tk], inc="dve")
        P.barrier("spkdbg")


def _ph_alpha(ctx):
    P, dt, s, ev = ctx["P"], ctx["dt"], ctx["s"], ctx["evac"]
    prev = {}
    zl = P.op("vector", lambda e: e.memset(s["lam"][:, :, 0:1], 0.0), inc="dve")
    for b in range(B):
        for tr in range(4):
            cols = slice(b * T + tr * 512, b * T + tr * 512 + 512)
            tcols = slice(tr * 512, tr * 512 + 512)
            cm = bm = None
            for v in range(4):
                r = b * 4 + v
                cm = P.mm(lambda e, r=r, v=v, tcols=tcols: e.matmul(
                    s["psA"][0:8, 0:512], s["ind8"][:, v * 8:(v + 1) * 8], s["spk"][:, r, tcols],
                    start=(v == 0), stop=(v == 3)), base=0,
                    deps=[ev["psA"]] if v == 0 else (), inc="pe" if v == 3 else None)
            for v in range(4):
                r = b * 4 + v
                bm = P.mm(lambda e, r=r, v=v, tcols=tcols: e.matmul(
                    s["psB"][0:8, 0:512], s["ind8b"][:, v * 8:(v + 1) * 8], s["spk"][:, r, tcols],
                    start=(v == 0), stop=(v == 3)), base=0,
                    deps=[ev["psB"]] if v == 0 else (), inc="pe" if v == 3 else None)
            ac = P.op("vector", lambda e, cols=cols: e.tensor_scalar(
                s["active"][0:8, cols], s["psA"][0:8, 0:512], 0.5, None, ALU.is_gt),
                deps=[cm], inc="dve")
            ev["psA"] = ac

            def beta_fn(e, cols=cols):
                return e.tensor_tensor(s["scr_a"][0:8, 0:512], s["bb"][0:8, cols],
                                       s["psB"][0:8, 0:512], ALU.add)
            b1 = P.op("vector", beta_fn, deps=[bm], inc="dve")
            ev["psB"] = b1
            b2 = P.op("scalar", lambda e: e.activation(s["scr_a"][0:8, 512:1024],
                                                       s["scr_a"][0:8, 0:512], AF.Sigmoid),
                      deps=[b1], inc="act")
            b3 = P.op("vector", lambda e, cols=cols: e.tensor_tensor(
                s["beta"][0:8, cols], s["scr_a"][0:8, 512:1024], s["active"][0:8, cols], ALU.mult),
                deps=[b2, ac], inc="dve")
            prev[("beta", b, tr)] = b3
    if ctx["debug"]:
        tk = P.op("sync", lambda e: e.dma_start(out=dt["dbg_beta"], in_=s["beta"][0:8, :]),
                  deps=[prev[("beta", B - 1, 3)]], inc="dbgs", dma=True)
        P.op("vector", lambda e: e.memset(P.s_barsc[:, 3:4], 0.0), deps=[tk], inc="dve")

    for r in range(8):
        b, v = divmod(r, 4)
        scan_prev = zl
        for tr in range(4):
            dbuf = tr % 2
            cols = slice(b * T + tr * 512, b * T + tr * 512 + 512)
            tcols = slice(tr * 512, tr * 512 + 512)
            la = P.op("sync", lambda e, b=b, v=v, tr=tr, d=dbuf: e.dma_start(
                out=s["scr_bf"][:, d, 0:512], in_=dt["ab_d"][b, v, :, tr * 512:tr * 512 + 512]),
                deps=[prev.get(("abt", dbuf))], inc="abl%d" % dbuf, dma=True)
            am = P.mm(lambda e, v=v, cols=cols: e.matmul(
                s["psC"][0:128, 0:512], s["ind8T"][:, v * 128:(v + 1) * 128], s["active"][0:8, cols],
                start=True, stop=True), base=0,
                deps=[ev["psC"], prev[("beta", b, 3)]], inc="pe")

            def al_fn(e, r=r, v=v, d=dbuf, tcols=tcols):
                e.tensor_scalar(s["scr_e"][:, 0:512], s["spk"][:, r, tcols],
                                s["aspc"][:, v:v + 1], None, ALU.mult)
                return e.tensor_tensor(s["scr_e"][:, 0:512], s["scr_e"][:, 0:512],
                                       s["scr_bf"][:, d, 0:512], ALU.add)
            a1 = P.op("vector", al_fn,
                       deps=[la, prev.get(("scr_e_al",))], inc="dve")
            a2 = P.op("scalar", lambda e: e.activation(s["scr_e"][:, 512:1024], s["scr_e"][:, 0:512],
                                                       AF.Sigmoid), deps=[a1], inc="act")
            a3 = P.op("scalar", lambda e: e.activation(s["scr_e"][:, 1024:1536], s["scr_e"][:, 512:1024],
                                                       AF.Ln), deps=[a2], inc="act")
            a4 = P.op("vector", lambda e: e.tensor_tensor(
                s["scr_e"][:, 0:512], s["scr_e"][:, 1024:1536], s["psC"][0:128, 0:512], ALU.mult),
                deps=[a3, am], inc="dve")
            ev["psC"] = a4
            prev[("abt", dbuf)] = a4
            sc = P.op("vector", lambda e, r=r, tr=tr: e.tensor_tensor_scan(
                s["lam"][:, r, 1 + tr * 512:1 + tr * 512 + 512], s["c1"][:], s["scr_e"][:, 0:512],
                s["lam"][:, r, tr * 512:tr * 512 + 1], ALU.mult, ALU.add),
                deps=[a4, scan_prev], inc="dve")
            scan_prev = sc
            prev[("scr_e_al",)] = sc
    P.barrier("alpha")


def _ph_ahat(ctx):
    P, dt, s = ctx["P"], ctx["dt"], ctx["s"]
    last = None
    for r in range(8):
        def lrel(e, r=r):
            return e.tensor_tensor(
                s["scr_e"][:, 0:2048].rearrange("p (c t) -> p c t", c=64),
                s["lam"][:, r, 1:2049].rearrange("p (c t) -> p c t", c=64),
                s["lam"][:, r, 0:2048].rearrange("p (c t) -> p c t", t=32)[:, :, 0:1]
                .broadcast_to([128, 64, 32]), ALU.subtract)
        t1 = P.op("vector", lrel, deps=[last], inc="dve")
        t2 = P.op("scalar", lambda e, r=r: e.activation(s["spk"][:, r, :], s["scr_e"][:, 0:2048],
                                                        AF.Exp), deps=[t1], inc="act")
        t3 = P.op("vector", lambda e, r=r: e.tensor_copy(
            s["aCt"][:, r, :],
            s["spk"][:, r, :].rearrange("p (c t) -> p c t", t=32)[:, :, 31]), deps=[t2], inc="dve")
        last = t3
    P.barrier("ahat")
    if ctx["debug"]:
        tk = P.op("sync", lambda e: e.dma_start(out=ctx["dt"]["dbg_ahat"],
                                                in_=s["spk"][:, :, 0:512]), inc="dbgs", dma=True)
        P.op("vector", lambda e: e.memset(P.s_barsc[:, 3:4], 0.0), deps=[tk], inc="dve")
        P.barrier("ahdbg")


def _ph_fold(ctx):
    P, dt, s, ev = ctx["P"], ctx["dt"], ctx["s"], ctx["evac"]
    prev = {}
    khv = dt["kh_d"].rearrange("r p t -> p r t")
    ktv = dt["kt_d"].rearrange("r p t -> p r t")
    qhv = dt["qh_d"].rearrange("r p t -> p r t")
    for b in range(B):
        for it in range(NTILE):
            tt = it * 128
            tile = b * NTILE + it
            dbuf = tile % 2
            lq = P.op("sync", lambda e, b=b, tt=tt, d=dbuf: e.dma_start(
                out=s["qkt"][:, d, :], in_=dt["q_d"][b, tt:tt + 128, :]),
                deps=[prev.get(("qkt", dbuf))], inc="qkl%d" % dbuf, dma=True)
            trq = None
            for j in range(4):
                trq = P.mm(lambda e, j=j, d=dbuf: e.transpose(
                    s["psT"][:, j * 128:(j + 1) * 128], s["qkt"][:, d, j * 128:(j + 1) * 128],
                    s["identb"][:]), base=0,
                    deps=[lq, ev["psT"]] if j == 0 else (), inc="pe" if j == 3 else None)
            qe = None
            for j in range(4):
                r = b * 4 + j
                qe = P.op("vector", lambda e, j=j, r=r, tt=tt: e.tensor_tensor(
                    s["qstg"][:, j, :], s["psT"][:, j * 128:(j + 1) * 128],
                    s["spk"][:, r, tt:tt + 128], ALU.mult),
                    deps=[trq, prev.get(("qstg",))] if j == 0 else (),
                    inc="dve" if j == 3 else None)
            ev["psT"] = qe
            sq = P.op("sync", lambda e, b=b, tt=tt: e.dma_start(
                out=qhv[:, b * 4:(b + 1) * 4, tt:tt + 128], in_=s["qstg"][:]),
                deps=[qe], inc="sqh", dma=True)
            prev[("qstg",)] = sq
            ev["psT"] = qe
            prev[("qkt", dbuf)] = trq
            lk = P.op("sync", lambda e, b=b, tt=tt, d=dbuf: e.dma_start(
                out=s["qkt"][:, 2 + d, :], in_=dt["k_d"][b, tt:tt + 128, :]),
                deps=[prev.get(("qkt_k", dbuf))], inc="qkl%d" % dbuf, dma=True)
            trk = None
            for j in range(4):
                trk = P.mm(lambda e, j=j, d=dbuf: e.matmul(
                    s["psF"][:, j * 128:(j + 1) * 128],
                    s["qkt"][:, 2 + d, j * 128:(j + 1) * 128], s["identb"][:],
                    start=True, stop=True), base=0,
                    deps=[lk, ev["psF"]] if j == 0 else (), inc="pe" if j == 3 else None)
            ke = None
            for j in range(4):
                r = b * 4 + j

                def kf(e, j=j, r=r, tt=tt):
                    e.tensor_tensor(s["kstg"][:, j, :],
                                    s["psF"][:, j * 128:(j + 1) * 128],
                                    s["spk"][:, r, tt:tt + 128], ALU.mult)
                    e.reciprocal(s["scr_a"][:, 0:128], s["spk"][:, r, tt:tt + 128])
                    return e.tensor_tensor(s["k2stg"][:, j, :],
                                           s["psF"][:, j * 128:(j + 1) * 128],
                                           s["scr_a"][:, 0:128], ALU.mult)
                ke = P.op("vector", kf,
                          deps=[trk, prev.get(("kstg",))] if j == 0 else (),
                          inc="dve" if j == 3 else None)
            ev["psF"] = ke
            sk = P.op("sync", lambda e, b=b, tt=tt: e.dma_start(
                out=khv[:, b * 4:(b + 1) * 4, tt:tt + 128], in_=s["kstg"][:]),
                deps=[ke], inc="skh", dma=True)
            sk2 = P.op("sync", lambda e, b=b, tt=tt: e.dma_start(
                out=ktv[:, b * 4:(b + 1) * 4, tt:tt + 128], in_=s["k2stg"][:]),
                deps=[ke], inc="skh", dma=True)
            prev[("kstg",)] = sk2
            tra = None
            for j in range(4):
                r = b * 4 + j
                tra = P.mm(lambda e, j=j, r=r, tt=tt: e.matmul(
                    s["psG"][:, j * 128:(j + 1) * 128], s["spk"][:, r, tt:tt + 128],
                    s["identb"][:], start=True, stop=True), base=0,
                    deps=[ev["psG"]] if j == 0 else (),
                    inc="pe" if j == 3 else None)

            def ktm_fn(e, d=dbuf):
                e.tensor_copy(s["scr_a"][:, 0:512], s["psG"][:, 0:512])
                e.reciprocal(s["scr_b"][:, 0:512], s["scr_a"][:, 0:512])
                return e.tensor_tensor(s["ktstg"][:], s["qkt"][:, 2 + d, :],
                                       s["scr_b"][:, 0:512], ALU.mult)
            kt = P.op("vector", ktm_fn, deps=[tra, prev.get(("ktstg",))], inc="dve")
            ev["psG"] = kt
            prev[("qkt_k", dbuf)] = kt
            skt = P.op("sync", lambda e, b=b, tt=tt: e.dma_start(
                out=dt["ktld_d"][b, tt:tt + 128, :], in_=s["ktstg"][:]),
                deps=[kt], inc="skt", dma=True)
            prev[("ktstg",)] = skt
            prev[("skh",)] = sk2
            prev[("sqh",)] = sq
            trb = P.mm(lambda e, b=b, tt=tt: e.transpose(
                s["psA"][0:128, 0:8], s["beta"][0:8, b * T + tt:b * T + tt + 128],
                s["identf"][96:104, 96:104], tile_position=(96, 0)),
                base=96, deps=[ev["psA"]], inc="pe")
            be = P.op("vector", lambda e, tile=tile: e.tensor_copy(
                s["betaTM"][:, tile, :], s["psA"][0:128, 0:8]), deps=[trb], inc="dve")
            ev["psA"] = be
    P.barrier("fold", dma=[prev.get(("ktstg",)), prev.get(("skh",)),
                           prev.get(("sqh",))])


(LP, LT, U2, D2, U4, D4, U8, D8, U16, FT, P1T, P2T, P3T, WSC) = range(14)


def _blk_mm(P, s, dst, lhs, rhs, deps, bank="psD"):
    W = s["wpool"]
    mm = None
    for g in range(4):
        mm = P.mm(lambda e, g=g: e.matmul(
            s[bank][0:128, g * 128:(g + 1) * 128],
            W[:, lhs, g * 128:(g + 1) * 128],
            W[:, rhs, g * 128:(g + 1) * 128],
            start=True, stop=True), base=0,
            deps=(list(deps or ()) + [P.psD_evac]) if g == 0 else (),
            inc="pe" if g == 3 else None)
    evt = P.op("vector", lambda e: e.tensor_copy(W[:, dst, :], s[bank][:, 0:512]),
               deps=[mm], inc="dve")
    P.psD_evac = evt
    return evt


def _ph_prep(ctx):
    P, dt, s, ev = ctx["P"], ctx["dt"], ctx["s"], ctx["evac"]
    W = s["wpool"]
    khv = dt["kh_d"].rearrange("r p t -> p r t")
    ktv = dt["kt_d"].rearrange("r p t -> p r t")
    qhv = dt["qh_d"].rearrange("r p t -> p r t")
    prevw = {}
    for w in range(4):
        wc = slice(w * 512, (w + 1) * 512)
        lw1 = P.op("sync", lambda e, wc=wc: e.dma_start(out=s["khw"][:], in_=khv[:, :, wc]),
                   deps=[prevw.get("kh")], inc="lkh", dma=True)
        lw2 = P.op("sync", lambda e, wc=wc: e.dma_start(out=s["ktw"][:], in_=ktv[:, :, wc]),
                   deps=[prevw.get("kt")], inc="lkt", dma=True)
        lw3 = P.op("sync", lambda e, wc=wc: e.dma_start(out=s["qhw"][:], in_=qhv[:, :, wc]),
                   deps=[prevw.get("qh")], inc="lqh", dma=True)
        for bh in range(16):
            b, h = divmod(bh, 8)
            r = b * 4 + h // 2
            pb = 64 * (h % 2)

            def bw_fn(e, b=b, h=h, w=w):
                for sl in range(4):
                    e.tensor_copy(s["bwin"][:, sl:sl + 1],
                                  s["betaTM"][:, b * NTILE + w * 4 + sl, h:h + 1])
                return None
            bw = P.op("vector", bw_fn, deps=[ev["bwin"]], inc="dve", chain=False)
            mmA = None
            for g in range(4):
                for i in range(4):
                    tl = (g * 4 + i) * 32
                    mmA = P.mm(lambda e, g=g, i=i, tl=tl, r=r, pb=pb: e.matmul(
                        s["psA"][32 * i:32 * i + 32,
                                 g * 128 + 32 * i:g * 128 + 32 * i + 32],
                        s["khw"][pb:pb + 64, r, tl:tl + 32],
                        s["ktw"][pb:pb + 64, r, tl:tl + 32],
                        start=True, stop=True, tile_position=(pb, 32 * i)),
                        base=pb,
                        deps=[ev["psA"], bw, lw1, lw2] if (g, i) == (0, 0) else (),
                        inc="pe" if (g, i) == (3, 3) else None)

            def la_fn(e):
                e.tensor_tensor(s["scr_a"][:, 0:512].rearrange("p (g j) -> p g j", g=4),
                                s["psA"][:, 0:512].rearrange("p (g j) -> p g j", g=4),
                                s["bwin"][:].rearrange("p g -> p g ()").broadcast_to([128, 4, 128]),
                                ALU.mult)
                return e.tensor_tensor(
                    W[:, LP, :].rearrange("p (g j) -> p g j", g=4),
                    s["scr_a"][:, 0:512].rearrange("p (g j) -> p g j", g=4),
                    s["mask_sl"][:].rearrange("p j -> p () j").broadcast_to([128, 4, 128]),
                    ALU.mult)
            ea = P.op("vector", la_fn, deps=[mmA, P.psD_evac], inc="dve")
            ev["psA"] = ea
            trt = None
            for g in range(4):
                trt = P.mm(lambda e, g=g: e.transpose(
                    s["psT"][:, g * 128:(g + 1) * 128], W[:, LP, g * 128:(g + 1) * 128],
                    s["identb"][:]), base=0,
                    deps=[ea, ev["psT"]] if g == 0 else (),
                    inc="pe" if g == 3 else None)
            et = P.op("vector", lambda e: e.tensor_copy(W[:, LT, :], s["psT"][:, 0:512]),
                      deps=[trt], inc="dve")
            ev["psT"] = et
            e_u2 = _blk_mm(P, s, U2, LP, LT, [et])
            e_d2 = _blk_mm(P, s, D2, LT, LP, None)
            e_u4 = _blk_mm(P, s, U4, D2, U2, None)
            e_d4 = _blk_mm(P, s, D4, U2, D2, None)
            e_u8 = _blk_mm(P, s, U8, D4, U4, None)
            e_d8 = _blk_mm(P, s, D8, U4, D4, None)
            e_u16 = _blk_mm(P, s, U16, D8, U8, None)

            idb = s["identb"][:].rearrange("p j -> p () j").broadcast_to([128, 4, 128])

            def iadds(e):
                e.scalar_tensor_tensor(W[:, FT, :].rearrange("p (g j) -> p g j", g=4),
                                       W[:, LP, :].rearrange("p (g j) -> p g j", g=4),
                                       -1.0, idb, ALU.mult, ALU.add)
                for sl in (U2, U4, U8, U16):
                    e.tensor_tensor(W[:, sl, :].rearrange("p (g j) -> p g j", g=4),
                                    W[:, sl, :].rearrange("p (g j) -> p g j", g=4),
                                    idb, ALU.add)
                return None
            ia = P.op("vector", iadds, deps=[e_u16], inc="dve", chain=False)
            e_p1t = _blk_mm(P, s, P1T, U2, FT, [ia])
            e_p2t = _blk_mm(P, s, P2T, U4, P1T, None)
            e_p3t = _blk_mm(P, s, P3T, U8, P2T, None)
            pn = None
            for g in range(4):
                pn = P.mm(lambda e, g=g: e.matmul(
                    s["psB"][0:128, g * 128:(g + 1) * 128],
                    W[:, P3T, g * 128:(g + 1) * 128],
                    W[:, U16, g * 128:(g + 1) * 128],
                    start=True, stop=True), base=0,
                    deps=[e_p3t, ev["psB"]] if g == 0 else (),
                    inc="pe" if g == 3 else None)

            def en_fn(e, bh=bh, w=w):
                for i in range(4):
                    pv = s["psB"].ap()[32 * i:32 * i + 32, :].rearrange(
                        "p (g m) -> p g m", g=4)[:, :, 32 * i:32 * i + 32]
                    ov = s["Nst"].ap()[32 * i:32 * i + 32, bh,
                                       w * 128:(w + 1) * 128].rearrange(
                        "p (g m) -> p g m", g=4)
                    bv = s["bwin"].ap()[32 * i:32 * i + 32, :].rearrange(
                        "p g -> p g ()").broadcast_to([32, 4, 32])
                    e.tensor_tensor(ov, pv, bv, ALU.mult)
                return None
            en = P.op("vector", en_fn, deps=[pn], inc="dve", chain=False)
            ev["psB"] = en
            ev["bwin"] = en
            mmB = None
            for rng in range(4):
                for slot in range(4):
                    tl = (slot * 4 + rng) * 32
                    mmB = P.mm(lambda e, rng=rng, slot=slot, tl=tl, r=r, pb=pb: e.matmul(
                        s["psC"][32 * rng:32 * rng + 32, slot * 32:slot * 32 + 32],
                        s["ktw"][pb:pb + 64, r, tl:tl + 32],
                        s["qhw"][pb:pb + 64, r, tl:tl + 32],
                        start=True, stop=True, tile_position=(pb, 32 * rng)),
                        base=pb, deps=[ev["psC"], lw3] if (rng, slot) == (0, 0) else (),
                        inc="pe" if (rng, slot) == (3, 3) else None)
            ebb = P.op("vector", lambda e, bh=bh, w=w: e.tensor_tensor(
                s["Bst"][:, bh, w * 128:(w + 1) * 128], s["psC"][:, 0:128], s["mask_ui"][:],
                ALU.mult), deps=[mmB], inc="dve")
            ev["psC"] = ebb
        prevw["kh"] = mmA
        prevw["kt"] = mmB
        prevw["qh"] = mmB
    P.barrier("prep")


def _ph_sinkhorn_moe(ctx):
    P, dt, s, ev = ctx["P"], ctx["dt"], ctx["s"], ctx["evac"]

    t1 = P.op("sync", lambda e: e.dma_start(
        out=dt["g256_d"].rearrange("(b e s) c -> e b s c", b=2, s=8),
        in_=s["gates"][0:8, :].rearrange("e (b s c) -> e b s c", b=2, s=8)), inc="sg", dma=True)
    t2 = P.op("sync", lambda e: e.dma_start(out=s["g256"][:], in_=dt["g256_d"]),
              deps=[t1], inc="lg", dma=True)
    t3 = P.op("vector", lambda e: e.tensor_scalar(s["g256"][:], s["g256"][:], EPS, None, ALU.max),
              deps=[t2], inc="dve")
    t4 = P.op("scalar", lambda e: e.activation(s["g256"][:], s["g256"][:], AF.Ln), deps=[t3], inc="act")
    last = t4
    for it in range(8):
        x1 = P.op("scalar", lambda e: e.activation(s["x256"][:], s["g256"][:], AF.Exp,
                                                   accum_out=s["lo256"][:]), deps=[last], inc="act")
        m1 = P.mm(lambda e: e.matmul(s["psA"][0:128, 0:1], s["g8"][:], s["lo256"][:],
                                     start=True, stop=True), base=0, deps=[x1, ev["psA"]], inc="pe")
        l1 = P.op("scalar", lambda e: e.activation(s["hi256"][:], s["psA"][0:128, 0:1], AF.Ln),
                  deps=[m1], inc="act")
        ev["psA"] = l1
        s1 = P.op("vector", lambda e: e.tensor_scalar(s["g256"][:], s["g256"][:], s["hi256"][:],
                                                      None, ALU.subtract), deps=[l1], inc="dve")
        x2 = P.op("scalar", lambda e: e.activation(s["x256"][:], s["g256"][:], AF.Exp),
                  deps=[s1], inc="act")
        m2 = P.mm(lambda e: e.matmul(s["psB"][0:128, 0:256], s["e8"][:], s["x256"][:],
                                     start=True, stop=True), base=0, deps=[x2, ev["psB"]], inc="pe")
        l2 = P.op("scalar", lambda e: e.activation(s["m256"][:], s["psB"][0:128, 0:256], AF.Ln),
                  deps=[m2], inc="act")
        ev["psB"] = l2
        last = P.op("vector", lambda e: e.tensor_tensor(s["g256"][:], s["g256"][:], s["m256"][:],
                                                        ALU.subtract), deps=[l2], inc="dve")
    gex = P.op("scalar", lambda e: e.activation(s["g256"][:], s["g256"][:], AF.Exp),
               deps=[last], inc="act")
    if ctx["debug"]:
        tk = P.op("sync", lambda e: e.dma_start(out=dt["dbg_gates"], in_=s["g256"][:]),
                  deps=[gex], inc="dbgs", dma=True)
        P.op("vector", lambda e: e.memset(P.s_barsc[:, 3:4], 0.0), deps=[tk], inc="dve")

    P.op("vector", lambda e: e.memset(s["lo256"][:], 0.0), deps=[gex])
    P.op("vector", lambda e: e.memset(s["hi256"][:], 1.0))
    last = P.op("vector", lambda e: e.memset(s["th256"][:], 0.5), inc="dve")
    for it in range(36):
        c1 = P.op("vector", lambda e: e.tensor_scalar(
            s["sel256"][:], s["g256"][:], s["th256"][:], 0.0, ALU.is_gt, ALU.add,
            accum_out=s["m256"][:, 0:1]), deps=[last], inc="dve")
        m1 = P.mm(lambda e: e.matmul(s["psA"][0:128, 0:1], s["g8"][:], s["m256"][:, 0:1],
                                     start=True, stop=True), base=0, deps=[c1, ev["psA"]], inc="pe")

        def upd(e):
            e.tensor_scalar(s["m256"][:, 1:2], s["psA"][0:128, 0:1], 256.5, None, ALU.is_gt)
            e.tensor_tensor(s["m256"][:, 2:3], s["th256"][:], s["lo256"][:], ALU.subtract)
            e.tensor_tensor(s["m256"][:, 3:4], s["m256"][:, 1:2], s["m256"][:, 2:3], ALU.mult)
            e.tensor_tensor(s["lo256"][:], s["lo256"][:], s["m256"][:, 3:4], ALU.add)
            e.tensor_scalar(s["m256"][:, 1:2], s["m256"][:, 1:2], 1.0, None, ALU.is_lt)
            e.tensor_tensor(s["m256"][:, 2:3], s["th256"][:], s["hi256"][:], ALU.subtract)
            e.tensor_tensor(s["m256"][:, 3:4], s["m256"][:, 1:2], s["m256"][:, 2:3], ALU.mult)
            e.tensor_tensor(s["hi256"][:], s["hi256"][:], s["m256"][:, 3:4], ALU.add)
            e.tensor_tensor(s["m256"][:, 3:4], s["lo256"][:], s["hi256"][:], ALU.add)
            return e.tensor_scalar(s["th256"][:], s["m256"][:, 3:4], 0.5, None, ALU.mult)
        last = P.op("vector", upd, deps=[m1], inc="dve")
        ev["psA"] = last

    sel1 = P.op("vector", lambda e: e.tensor_scalar(s["sel256"][:], s["g256"][:], s["th256"][:],
                                                    None, ALU.is_gt), deps=[last], inc="dve")
    nm = P.mm(lambda e: e.matmul(s["psA"][0:128, 0:256], s["e8"][:], s["sel256"][:],
                                 start=True, stop=True), base=0, deps=[sel1, ev["psA"]], inc="pe")

    def wm_fn(e):
        e.tensor_scalar(s["m256"][:], s["g256"][:], 0.5, None, ALU.is_gt)
        e.tensor_scalar(s["x256"][:], s["psA"][0:128, 0:256], 1.5, None, ALU.is_lt)
        return e.tensor_tensor(s["m256"][:], s["m256"][:], s["x256"][:], ALU.mult)
    wm = P.op("vector", wm_fn, deps=[nm], inc="dve")
    ev["psA"] = wm
    pf1 = P.op("vector", lambda e: e.memset(s["x256"][:], 1.0), deps=[wm], inc="dve")
    pf2 = P.op("vector", lambda e: e.tensor_tensor_scan(
        s["sel256"][:], s["x256"][:], s["m256"][:], 0.0, ALU.mult, ALU.add), deps=[pf1], inc="dve")
    tm = P.mm(lambda e: e.matmul(s["psB"][0:128, 0:1], s["lt8"][:], s["sel256"][:, 255:256],
                                 start=True, stop=True), base=0, deps=[pf2, ev["psB"]], inc="pe")

    def slot_fn(e):
        e.tensor_scalar(s["x256"][:], s["sel256"][:], s["psB"][0:128, 0:1], None, ALU.add)
        e.tensor_scalar(s["x256"][:], s["x256"][:], s["eoffm1"][:], None, ALU.add)
        return e.tensor_tensor(s["x256"][:], s["x256"][:], s["m256"][:], ALU.mult)
    sl1 = P.op("vector", slot_fn, deps=[tm], inc="dve")
    ev["psB"] = sl1
    sm = P.mm(lambda e: e.matmul(s["psC"][0:128, 0:256], s["e8"][:], s["x256"][:],
                                 start=True, stop=True), base=0, deps=[sl1, ev["psC"]], inc="pe")
    se = P.op("vector", lambda e: e.tensor_scalar(s["x256"][:], s["psC"][0:128, 0:256], 1.0,
                                                  None, ALU.subtract), deps=[sm], inc="dve")
    ev["psC"] = se
    P.op("sync", lambda e: e.dma_start(
        out=dt["slot_d"][0], in_=s["x256"][0:8, :]), deps=[se], inc="ssl", dma=True)
    sb1 = P.op("sync", lambda e: e.dma_start(
        out=dt["slot_d"][1], in_=s["x256"][64:72, :]), inc="ssl", dma=True)
    sb2 = P.op("sync", lambda e: e.dma_start(out=s["slotsb"][:], in_=dt["slot_d"].rearrange(
        "b s c -> (b s) c")), deps=[sb1], inc="lsl", dma=True)
    trc = None
    for c2 in range(2):
        trc = P.mm(lambda e, c2=c2: e.transpose(
            s["psD"][0:128, c2 * 16:(c2 + 1) * 16], s["slotsb"][:, c2 * 128:(c2 + 1) * 128],
            s["identf"][0:16, 0:16]), base=0,
            deps=[sb2, ev["psD"], P.psD_evac] if c2 == 0 else (),
            inc="pe" if c2 == 1 else None)

    def unpack(e):
        for c2 in range(2):
            e.tensor_copy(
                s["slotTM"][:].rearrange("p (b sg c2) -> p b sg c2", b=2, c2=2)[:, :, :, c2],
                s["psD"][0:128, c2 * 16:(c2 + 1) * 16].rearrange("p (b sg) -> p b sg", b=2))
        return None
    up = P.op("vector", unpack, deps=[trc], inc="dve", chain=False)
    ev["psD"] = up
    P.psD_evac = up

    def slot_map(e):
        e.tensor_scalar(s["m256"][:, 0:32], s["slotTM"][:], 0.0, None, ALU.is_lt)
        return e.scalar_tensor_tensor(s["slotTM"][:], s["m256"][:, 0:32], float(HUGE),
                                      s["slotTM"][:], ALU.mult, ALU.add)
    sl2 = P.op("vector", slot_map, deps=[up], inc="dve")
    sl3 = P.op("vector", lambda e: e.tensor_copy(s["slot32"][:], s["slotTM"][:]), deps=[sl2], inc="dve")

    pre = P.op("sync", lambda e: e.dma_start(
        out=dt["idx_d"].rearrange("(p a) o -> p a o", p=128), in_=s["huge_i"][:]),
        inc="pidx", dma=True)
    sc_toks = []
    for blk in range(32):
        tok = P.op("gpsimd", lambda e, blk=blk: e.indirect_dma_start(
            dt["idx_d"], IndirectOffsetOnAxis(s["slot32"][:, blk:blk + 1], 0),
            s["iota_tm"][:, blk:blk + 1], None,
            bounds_check=ctx["bc_slot"], oob_is_err=False),
            deps=[sl3, pre] if blk == 0 else (), inc="gsc", dma=True)
        sc_toks.append(tok)
    if ctx["debug"]:
        tk = P.op("sync", lambda e: e.dma_start(out=dt["dbg_idx"], in_=dt["idx_d"]),
                  deps=[sc_toks[-1]], inc="dbgs", dma=True)
        P.op("vector", lambda e: e.memset(P.s_barsc[:, 3:4], 0.0), deps=[tk], inc="dve")

    z0 = P.op("vector", lambda e: e.memset(s["ztile"][:], 0.0), inc="dve")
    zf = None
    for b in range(B):
        for tt in range(0, T, 128):
            zf = P.op("sync", lambda e, b=b, tt=tt: e.dma_start(
                out=dt["ymoe_d"][b * T + tt:b * T + tt + 128, :], in_=s["ztile"][:]),
                deps=[z0] if tt == 0 and b == 0 else (), inc="zym", dma=True)

    prevw = {}
    scat = sc_toks[-1]
    sc = None
    for e_ in range(H):
        dbuf = e_ % 2
        lw = P.op("gpsimd", lambda e, e_=e_, d=dbuf: e.dma_start(
            out=s["wexp"][:, d, :, :], in_=dt["experts"][e_].rearrange("(c p) o -> p c o", p=128)),
            deps=[prevw.get(("wexp", dbuf))], inc="wex%d" % dbuf, dma=True)
        for b in range(B):
            gidx = b * 8 + e_
            for part, rows in ((0, 128), (1, 64)):
                base_slot = gidx * CAP_PAD + part * 128
                d2 = ((b * 2) + part) % 2
                bkA, bkB = (("psE", "psF"), ("psC", "psD"))[d2]
                lic = P.op("sync", lambda e, base_slot=base_slot, rows=rows, d2=d2: e.dma_start(
                    out=s["idx2"][0:rows, d2:d2 + 1], in_=dt["idx_d"][base_slot:base_slot + rows, :]),
                    deps=[scat, prevw.get(("idxcol", d2))], inc="lic", dma=True)
                ga = P.op("gpsimd", lambda e, rows=rows, d2=d2: e.indirect_dma_start(
                    s["xg"][0:rows, d2, :], None, dt["xn_d"],
                    IndirectOffsetOnAxis(s["idx2"][0:rows, d2:d2 + 1], 0),
                    bounds_check=ctx["bc_tok"], oob_is_err=False),
                    deps=[lic, prevw.get(("xg", d2))], inc="gga", dma=True)
                if prevw.get(("scpend",)) is not None:
                    sc = _moe_scatter(ctx, prevw, *prevw[("scpend",)])
                    prevw[("scpend",)] = None
                trl = None
                for j in range(8):
                    trl = P.mm(lambda e, j=j, rows=rows, d2=d2: e.transpose(
                        s["psT"][:, j * 128:j * 128 + rows],
                        s["xg"][0:rows, d2, j * 128:(j + 1) * 128],
                        s["identb"][0:rows, 0:rows]), base=0,
                        deps=[ga, ev["psT"]] if j == 0 else (), inc="pe" if j == 7 else None)
                ex = P.op("vector", lambda e, rows=rows, d2=d2: e.tensor_copy(
                    s["xgfm"][:, d2, :, 0:rows],
                    s["psT"][:].rearrange("p (c t) -> p c t", c=8)[:, :, 0:rows]),
                    deps=[trl], inc="dve")
                ev["psT"] = ex
                prevw[("xg", d2)] = ex
                mm1 = mm2 = None
                for j in range(8):
                    mm1 = P.mm(lambda e, j=j, d=dbuf, rows=rows, d2=d2, bkA=bkA: e.matmul(
                        s[bkA][0:rows, 0:512], s["xgfm"][:, d2, j, 0:rows],
                        s["wexp"][:, d, j, 0:512],
                        start=(j == 0), stop=(j == 7)), base=0,
                        deps=[ex, lw, ev[bkA]] if j == 0 else (), inc="pe" if j == 7 else None)
                for j in range(8):
                    mm2 = P.mm(lambda e, j=j, d=dbuf, rows=rows, d2=d2, bkB=bkB: e.matmul(
                        s[bkB][0:rows, 0:512], s["xgfm"][:, d2, j, 0:rows],
                        s["wexp"][:, d, j, 512:1024],
                        start=(j == 0), stop=(j == 7)), base=0,
                        deps=[ev[bkB]] if j == 0 else (), inc="pe" if j == 7 else None)
                ye1 = P.op("vector", lambda e, rows=rows, d2=d2, bkA=bkA: e.tensor_copy(
                    s["yexp"][0:rows, d2, 0:512], s[bkA][0:rows, 0:512]),
                    deps=[mm1, prevw.get(("yx", d2))], inc="dve")
                ye2 = P.op("vector", lambda e, rows=rows, d2=d2, bkB=bkB: e.tensor_copy(
                    s["yexp"][0:rows, d2, 512:1024], s[bkB][0:rows, 0:512]), deps=[mm2], inc="dve")
                ev[bkA], ev[bkB] = ye1, ye2
                prevw[("scpend",)] = (rows, d2, ye1, ye2, zf)
        prevw[("wexp", dbuf)] = ye2
    if prevw.get(("scpend",)) is not None:
        sc = _moe_scatter(ctx, prevw, *prevw[("scpend",)])
        prevw[("scpend",)] = None
    P.barrier("moe", dma=[sc])
    if ctx["debug"]:
        tk = None
        for b in range(B):
            for tt in range(0, T, 512):
                tk = P.op("sync", lambda e, b=b, tt=tt: e.dma_start(
                    out=dt["dbg_ymoe"][b, tt:tt + 512, :],
                    in_=dt["ymoe_d"][b * T + tt:b * T + tt + 512, :]), inc="dbgs", dma=True)
        P.op("vector", lambda e: e.memset(P.s_barsc[:, 3:4], 0.0), deps=[tk], inc="dve")
        P.barrier("moedbg")


def _moe_scatter(ctx, prevw, rows, d2, ye1, ye2, zf):
    P, dt, s = ctx["P"], ctx["dt"], ctx["s"]
    sc = P.op("gpsimd", lambda e, rows=rows, d2=d2: e.indirect_dma_start(
        dt["ymoe_d"],
        IndirectOffsetOnAxis(s["idx2"][0:rows, d2:d2 + 1], 0),
        s["yexp"][0:rows, d2, :], None,
        bounds_check=ctx["bc_tok"], oob_is_err=False),
        deps=[ye1, ye2, zf], inc="gsy", dma=True)
    prevw[("yx", d2)] = sc
    prevw[("idxcol", d2)] = sc
    return sc


def _ph_state(ctx):
    P, dt, s, ev = ctx["P"], ctx["dt"], ctx["s"], ctx["evac"]
    dbg = ctx["debug"]
    nch_cut = ctx.get("stcut", NCH)
    prev = {}
    banks = ("psA", "psB", "psC", "psD")
    khv = dt["kh_d"].rearrange("r p t -> p r t")
    qhv = dt["qh_d"].rearrange("r p t -> p r t")

    lwo = P.op("gpsimd", lambda e: e.dma_start(
        out=s["wout"][:], in_=dt["w_out"].rearrange("(c p) o -> p c o", p=128)), inc="lwo", dma=True)
    wsc = None
    for cch in range(8):
        wsc = P.op("vector", lambda e, c=cch: e.tensor_scalar(
            s["wout"][:, c, :], s["wout"][:, c, :], s["whn_col"][:, c:c + 1], None, ALU.mult),
            deps=[lwo] if cch == 0 else (), inc="dve" if cch == 7 else None)
    z1 = P.op("vector", lambda e: e.memset(s["Sst"][:], 0.0))
    z2 = P.op("vector", lambda e: e.memset(s["Sbf"][:], 0.0), inc="dve")
    prev[("S",)] = z2

    for c in range(nch_cut):
        rng = c % 4
        tile_i = c // 4
        w = c // 16
        slot = (c % 16) // 4
        tcol = c * 32
        pr = slice(32 * rng, 32 * rng + 32)
        dbuf = tile_i % 2
        if rng == 0:
            vl = None
            for b in range(B):
                vl = P.op("sync", lambda e, b=b, t=tile_i * 128, d=dbuf: e.dma_start(
                    out=s["vtm"][:, d, b * HV:(b + 1) * HV], in_=dt["v_d"][b, t:t + 128, :]),
                    deps=[prev.get(("vread", dbuf))], inc="stv%d" % dbuf, dma=True)
                vl = P.op("sync", lambda e, b=b, t=tile_i * 128, d=dbuf: e.dma_start(
                    out=s["ktls"][:, d, b * HD:(b + 1) * HD], in_=dt["ktld_d"][b, t:t + 128, :]),
                    inc="stv%d" % dbuf, dma=True)
            vl = P.op("sync", lambda e, t=tile_i * 128, d=dbuf: e.dma_start(
                out=s["khs"][:, d, :, :], in_=khv[:, :, t:t + 128]), inc="stv%d" % dbuf, dma=True)
            vl = P.op("sync", lambda e, t=tile_i * 128, d=dbuf: e.dma_start(
                out=s["qhs"][:, d, :, :], in_=qhv[:, :, t:t + 128]), inc="stv%d" % dbuf, dma=True)
            prev[("vload",)] = vl

        lcol = tcol - tile_i * 128
        for bh in range(16):
            b, h = divmod(bh, 8)
            r = b * 4 + h // 2
            pb = 64 * (h % 2)
            bank = banks[bh // 4]
            off = (bh % 4) * 128
            tok = P.mm(lambda e, r=r, pb=pb, bank=bank, off=off, lcol=lcol, pr=pr, rng=rng, d=dbuf: e.matmul(
                s[bank][pr, off:off + 128],
                s["khs"][pb:pb + 64, d, r, lcol:lcol + 32],
                s["Sbf"][pb:pb + 64, r, 0:128],
                start=True, stop=True, tile_position=(pb, 32 * rng)),
                base=pb,
                deps=[ev[bank], prev.get(("S",)), prev.get(("vload",)), wsc] if bh % 4 == 0 else (),
                inc="pe" if bh % 4 == 3 else None)
            if bh % 4 == 3:
                prev[("u0", bh // 4)] = tok
        if ctx.get("stpart", 9) <= 1:
            continue
        xd = []
        for q in range(4):
            bank = banks[q]
            b0, h0 = divmod(4 * q, 8)

            def xfn(e, q=q, bank=bank, pr=pr, d=dbuf, b0=b0, h0=h0):
                return e.tensor_tensor(
                    s["Xst"][pr, q * 512:(q + 1) * 512],
                    s["vtm"][pr, d, b0 * HV + h0 * DV: b0 * HV + h0 * DV + 512],
                    s[bank][pr, 0:512], ALU.subtract)
            t = P.op("vector", xfn, deps=[prev[("u0", q)]], inc="dve", chain=False)
            ev[bank] = t
            xd.append(t)
        if ctx.get("stpart", 9) <= 2:
            continue
        wev = []
        for half in range(2):
            for q in range(8):
                bh = half * 8 + q
                bank = ("psE", "psF")[q // 4]
                off = (q % 4) * 128
                tok = P.mm(lambda e, bh=bh, bank=bank, off=off, w=w, slot=slot, pr=pr, rng=rng: e.matmul(
                    s[bank][pr, off:off + 128],
                    s["Nst"][pr, bh, w * 128 + slot * 32: w * 128 + slot * 32 + 32],
                    s["Xst"][pr, bh * 128:(bh + 1) * 128],
                    start=True, stop=True, tile_position=(32 * rng, 32 * rng)),
                    base=32 * rng,
                    deps=(xd + [ev[bank]]) if q % 4 == 0 else (),
                    inc="pe" if q % 4 == 3 else None)
                if q % 4 == 3:
                    prev[("wmm", q // 4)] = tok
            for q2 in range(2):
                bank = ("psE", "psF")[q2]
                t = P.op("vector", lambda e, q2=q2, half=half, bank=bank, pr=pr: e.tensor_copy(
                    s["Wst"][pr, (half * 8 + q2 * 4) * 128:(half * 8 + q2 * 4 + 4) * 128],
                    s[bank][pr, 0:512]), deps=[prev[("wmm", q2)]], inc="dve")
                ev[bank] = t
                wev.append(t)

        if ctx.get("stpart", 9) <= 3:
            continue
        for bh in range(16):
            b, h = divmod(bh, 8)
            r = b * 4 + h // 2
            pb = 64 * (h % 2)
            bank = banks[bh // 4]
            off = (bh % 4) * 128
            P.mm(lambda e, r=r, pb=pb, bank=bank, off=off, lcol=lcol, pr=pr, rng=rng, d=dbuf: e.matmul(
                s[bank][pr, off:off + 128],
                s["qhs"][pb:pb + 64, d, r, lcol:lcol + 32],
                s["Sbf"][pb:pb + 64, r, 0:128],
                start=True, stop=False, tile_position=(pb, 32 * rng)),
                base=pb,
                deps=(wev + [ev[bank]]) if bh % 4 == 0 else ())
            m2 = P.mm(lambda e, bh=bh, bank=bank, off=off, w=w, slot=slot, pr=pr, rng=rng: e.matmul(
                s[bank][pr, off:off + 128],
                s["Bst"][pr, bh, w * 128 + slot * 32: w * 128 + slot * 32 + 32],
                s["Wst"][pr, bh * 128:(bh + 1) * 128],
                start=False, stop=True, tile_position=(32 * rng, 32 * rng)),
                base=32 * rng, inc="pe" if bh % 4 == 3 else None)
            if bh % 4 == 3:
                prev[("omm", bh // 4)] = m2
        if ctx.get("stpart", 9) <= 4:
            continue

        def yevac_v(e, pr=pr):
            e.tensor_copy(s["yraw"][pr, 0:512], s[banks[0]][pr, 0:512])
            e.tensor_copy(s["yraw"][pr, 512:1024], s[banks[1]][pr, 0:512])
            return None
        yrd = list(prev.get(("yraw_rd", rng), ()))
        yev_v = P.op("vector", yevac_v,
                     deps=[prev[("omm", 0)], prev[("omm", 1)]] + yrd,
                     inc="dve", chain=False)

        def yevac_s(e, pr=pr):
            e.activation(s["yraw"][pr, 1024:1536], s[banks[2]][pr, 0:512], AF.Copy)
            e.activation(s["yraw"][pr, 1536:2048], s[banks[3]][pr, 0:512], AF.Copy)
            return None
        yev_s = P.op("scalar", yevac_s,
                     deps=[prev[("omm", 2)], prev[("omm", 3)]] + yrd,
                     inc="act", chain=False)
        yevq = (yev_v, yev_v, yev_s, yev_s)
        for q in range(4):
            ev[banks[q]] = yevq[q]

        if ctx.get("stpart", 9) <= 5:
            continue
        for half in range(2):
            bank = ("psG", "psE")[half]
            tok = None
            for q in range(8):
                bh = half * 8 + q
                b, h = divmod(bh, 8)
                pb = 64 * (h % 2)
                tok = P.mm(lambda e, b=b, h=h, pb=pb, bank=bank, lcol=lcol, pr=pr, rng=rng, d=dbuf: e.matmul(
                    s[bank][pb:pb + 64, (h // 2) * 128:(h // 2) * 128 + 128],
                    s["ktls"][pr, d, b * HD + h * 64:b * HD + (h + 1) * 64],
                    s["Wst"][pr, (b * 8 + h) * 128:(b * 8 + h + 1) * 128],
                    start=True, stop=True, tile_position=(32 * rng, pb)),
                    base=32 * rng,
                    deps=(wev + [ev[bank]]) if q == 0 else (),
                    inc="pe" if q == 7 else None)
            prev[("smm", half)] = tok

        def sev_w(e, c=c):
            for half in range(2):
                bank = ("psG", "psE")[half]
                e.tensor_tensor(
                    s["scr_b"][:, half * 512:(half + 1) * 512].rearrange(
                        "p (r v) -> p r v", r=4),
                    s["Sst"][:, half * 4:(half + 1) * 4, :],
                    s[bank][0:128, 0:512].rearrange("p (r v) -> p r v", r=4), ALU.add)
            return None
        sew = P.op("vector", sev_w, deps=[prev[("smm", 0)], prev[("smm", 1)]],
                   inc="dve", chain=False)

        def sev_r(e, c=c):
            for half in range(2):
                e.tensor_tensor(
                    s["Sst"][:, half * 4:(half + 1) * 4, :],
                    s["scr_b"][:, half * 512:(half + 1) * 512].rearrange(
                        "p (r v) -> p r v", r=4),
                    s["aCt"][:, half * 4:(half + 1) * 4, c:c + 1].broadcast_to(
                        [128, 4, DV]), ALU.mult)
            return None
        se = P.op("vector", sev_r, deps=[sew], inc="dve", chain=False)
        scp = P.op("vector", lambda e: e.tensor_copy(s["Sbf"][:], s["Sst"][:]),
                   deps=[se], inc="dve")
        ev["psG"], ev["psE"] = sew, sew
        prev[("S",)] = scp

        sqt = P.op("scalar", lambda e, pr=pr: e.activation(
            s["scr_e"][pr, 0:2048], s["yraw"][pr, 0:2048],
            AF.Square), deps=[yev_v, yev_s], inc="act")
        rdt = P.op("vector", lambda e, pr=pr: e.tensor_reduce(
            s["colA"][pr, 0:16],
            s["scr_e"][pr, 0:2048].rearrange("p (x v) -> p x v", x=16),
            AX.X, ALU.add), deps=[sqt], inc="dve")
        rst = P.op("vector", lambda e, pr=pr: e.tensor_scalar(
            s["scr_y"][pr, 0:16], s["colA"][pr, 0:16],
            1.0 / DV, EPS, ALU.mult, ALU.add), deps=[rdt], inc="dve")
        sq2 = P.op("scalar", lambda e, pr=pr: e.activation(
            s["scr_y"][pr, 16:32], s["scr_y"][pr, 0:16], AF.Sqrt),
            deps=[rst], inc="act")
        rc = P.op("vector", lambda e, pr=pr: e.reciprocal(
            s["colA"][pr, 0:16], s["scr_y"][pr, 16:32]),
            deps=[sq2], inc="dve")
        yw = P.op("vector", lambda e, pr=pr, d=dbuf: e.tensor_tensor(
            s["ytm"][:, d, 0:2048].rearrange("p (x v) -> p x v", x=16)[pr],
            s["yraw"][pr, 0:2048].rearrange("p (x v) -> p x v", x=16),
            s["colA"][pr, 0:16].rearrange("p x -> p x ()").broadcast_to(
                [32, 16, DV]), ALU.mult), deps=[rc, prev.get(("ytmrd", dbuf))], inc="dve")
        ysl = [yw]
        prev[("ytile", rng)] = ysl
        prev[("yraw_rd", rng)] = ysl

        if rng == 3:
            ytoks = prev[("ytile", 0)] + prev[("ytile", 1)] + prev[("ytile", 2)] + prev[("ytile", 3)]
            pend = prev.get(("ypend",))
            prev[("ypend",)] = (tile_i, dbuf, ytoks)
            if pend is not None:
                _emit_y1(ctx, prev, *pend)
            prev[("vread", dbuf)] = scp
    pend = prev.get(("ypend",))
    if pend is not None:
        _emit_y1(ctx, prev, *pend)
    P.barrier("state", dma=[prev.get(("y1t", 0)), prev.get(("y1t", 1))])


def _emit_y1(ctx, prev, tile_i, dbuf, ytoks):
    P, dt, s, ev = ctx["P"], ctx["dt"], ctx["s"], ctx["evac"]
    tt = tile_i * 128
    trl = None
    for b in range(B):
        for j in range(8):
            trl = P.mm(lambda e, j=j, b=b, d=dbuf: e.transpose(
                s["psT"][:, j * 128:(j + 1) * 128],
                s["ytm"][:, d, b * HV + j * 128:b * HV + (j + 1) * 128],
                s["identb"][:]), base=0,
                deps=(ytoks + [ev["psT"]]) if j == 0 else (), inc="pe" if j == 7 else None)
        ey = P.op("vector", lambda e: e.tensor_copy(
            s["yfm"][:], s["psT"][:].rearrange("p (c t) -> p c t", c=8)),
            deps=[trl], inc="dve")
        ev["psT"] = ey
        mo = None
        for half in range(2):
            bank = ("psF", "psG")[half]
            for j in range(8):
                mo = P.mm(lambda e, j=j, half=half, bank=bank: e.matmul(
                    s[bank][0:128, 0:512], s["yfm"][:, j, :],
                    s["wout"][:, j, half * 512:(half + 1) * 512],
                    start=(j == 0), stop=(j == 7)), base=0,
                    deps=[ey, ev[bank]] if j == 0 else (), inc="pe" if j == 7 else None)
            prev[("wo", half)] = mo
        for half in range(2):
            bank = ("psA", "psB")[half]
            mg = P.mm(lambda e, b=b, tt=tt, half=half, bank=bank: e.matmul(
                s[bank][0:128, 0:512], s["u1"][:, b * T + tt:b * T + tt + 128],
                s["wu2"][:, half * 512:(half + 1) * 512], start=True, stop=True),
                base=0, deps=[ev[bank]], inc="pe")
            prev[("gt", half)] = mg
        lxn = P.op("sync", lambda e, b=b, tt=tt: e.dma_start(
            out=s["scr_bf"][:, 0, :], in_=dt["xn_d"][b * T + tt:b * T + tt + 128, :]),
            deps=[prev.get(("xnld",))], inc="xnl", dma=True)

        def y1_fn(e):
            for half in range(2):
                gb_ = ("psA", "psB")[half]
                wb_ = ("psF", "psG")[half]
                sl_ = slice(half * 512, (half + 1) * 512)
                e.tensor_tensor(s["scr_a"][:, sl_], s[gb_][0:128, 0:512],
                                s["b_u2B"][:, sl_], ALU.add)
                e.tensor_copy(s["scr_c"][:, sl_], s[wb_][0:128, 0:512])
            return None
        y1a = P.op("vector", y1_fn,
                   deps=[prev[("wo", 0)], prev[("wo", 1)], prev[("gt", 0)],
                         prev[("gt", 1)], lxn, prev.get(("r2a",))], inc="dve")
        for bank in ("psA", "psB", "psF", "psG"):
            ev[bank] = y1a
        y1b = P.op("scalar", lambda e: e.activation(s["scr_a"][:, 0:1024],
                                                    s["scr_a"][:, 0:1024], AF.Sigmoid),
                   deps=[y1a], inc="act")

        def y1c_fn(e, d=dbuf):
            e.tensor_tensor(s["scr_c"][:, 0:1024], s["scr_c"][:, 0:1024],
                            s["scr_a"][:, 0:1024], ALU.mult)
            return e.tensor_tensor(s["y1t"][:, d, :], s["scr_bf"][:, 0, :],
                                   s["scr_c"][:, 0:1024], ALU.add)
        y1c = P.op("vector", y1c_fn, deps=[y1b, prev.get(("y1t", dbuf))], inc="dve")
        prev[("xnld",)] = y1c
        sy = P.op("sync", lambda e, b=b, tt=tt, d=dbuf: e.dma_start(
            out=dt["y1_d"][b, tt:tt + 128, :], in_=s["y1t"][:, d, :]),
            deps=[y1c], inc="sy1%d" % dbuf, dma=True)
        r2a = P.op("scalar", lambda e, d=dbuf: e.activation(
            s["scr_a"][:, 0:1024], s["y1t"][:, d, :], AF.Square,
            accum_out=s["stats"][:, 0:1]), deps=[y1c, prev.get(("r2b",))], inc="act")
        prev[("r2a",)] = r2a
        r2b = P.op("vector", lambda e: e.tensor_scalar(
            s["stats"][:, 1:2], s["stats"][:, 0:1], 1.0 / D, EPS, ALU.mult, ALU.add),
            deps=[r2a], inc="dve")
        prev[("r2b",)] = r2b
        r2c = P.op("scalar", lambda e: e.activation(s["stats"][:, 2:3], s["stats"][:, 1:2],
                                                    AF.Sqrt), deps=[r2b], inc="act")
        r2d = P.op("vector", lambda e, b=b, tile_i=tile_i: e.reciprocal(
            s["rstd2"][:, b * NTILE + tile_i:b * NTILE + tile_i + 1], s["stats"][:, 2:3]),
            deps=[r2c], inc="dve")
        prev[("y1t", dbuf)] = sy
    prev[("ytmrd", dbuf)] = trl


def _ph_ffn(ctx):
    P, dt, s, ev = ctx["P"], ctx["dt"], ctx["s"], ctx["evac"]
    prev = {}
    mo = None
    for half in range(2):
        lf1 = P.op("gpsimd", lambda e, half=half: e.dma_start(
            out=s["ff1h"][:], in_=dt["ff1"][:, half * 2048:(half + 1) * 2048]
            .rearrange("(c p) o -> p c o", p=128)), deps=[prev.get(("ffw",))], inc="ff1w", dma=True)
        lf3 = P.op("gpsimd", lambda e, half=half: e.dma_start(
            out=s["ff3h"][:], in_=dt["ff3"][:, half * 2048:(half + 1) * 2048]
            .rearrange("(c p) o -> p c o", p=128)), inc="ff3w", dma=True)
        lf2 = P.op("gpsimd", lambda e, half=half: e.dma_start(
            out=s["ff2h"][:], in_=dt["ff2"][half * 2048:(half + 1) * 2048, :]
            .rearrange("(c p) o -> p c o", p=128)), inc="ff2w", dma=True)
        for b in range(B):
            for gb in range(4):
                t0 = gb * 512
                gp = 0
                for it in range(4):
                    tt = t0 + it * 128
                    tile = b * NTILE + gb * 4 + it
                    db = it % 2
                    ly = P.op("sync", lambda e, b=b, tt=tt, d=db: e.dma_start(
                        out=s["y1t"][:, d, :], in_=dt["y1_d"][b, tt:tt + 128, :]),
                        deps=[prev.get(("y1ld", db))], inc="y1l%d" % db, dma=True)

                    def h2fn(e, d=db, tile=tile):
                        return e.scalar_tensor_tensor(
                            s["h2st"][:, d, :], s["y1t"][:, d, :],
                            s["rstd2"][:, tile:tile + 1], s["wffnB"][:],
                            ALU.mult, ALU.mult)
                    h2 = P.op("vector", h2fn, deps=[ly, prev.get(("h2rd", db)), lf1],
                              inc="dve")
                    trl = None
                    for j in range(8):
                        trl = P.mm(lambda e, j=j, d=db: e.transpose(
                            s["psT"][:, j * 128:(j + 1) * 128],
                            s["h2st"][:, d, j * 128:(j + 1) * 128],
                            s["identb"][:]), base=0,
                            deps=[h2, ev["psT"]] if j == 0 else (), inc="pe" if j == 7 else None)
                    prev[("h2rd", db)] = trl
                    eh = P.op("vector", lambda e, it=it, gp=gp: e.tensor_copy(
                        s["h2fm"][:, gp, :, it * 128:(it + 1) * 128],
                        s["psT"][:].rearrange("p (c t) -> p c t", c=8)),
                        deps=[trl, prev.get(("h2w", gp))], inc="dve")
                    ev["psT"] = eh
                    prev[("y1ld", db)] = eh
                    prev[("h2", it)] = eh
                h2done = [prev[("h2", i)] for i in range(4)]
                for oc in range(16):
                    bankA = ("psA", "psE")[oc % 2]
                    bankB = ("psB", "psF")[oc % 2]
                    sca = ("scr_a", "scr_e")[oc % 2]
                    mm1 = mm3 = None
                    for j in range(8):
                        mm1 = P.mm(lambda e, j=j, oc=oc, bankA=bankA, gp=gp: e.matmul(
                            s[bankA][0:128, 0:512], s["ff1h"][:, j, oc * 128:(oc + 1) * 128],
                            s["h2fm"][:, gp, j, :], start=(j == 0), stop=(j == 7)), base=0,
                            deps=(h2done + [lf1, ev[bankA]]) if j == 0 else (),
                            inc="pe" if j == 7 else None)
                    for j in range(8):
                        mm3 = P.mm(lambda e, j=j, oc=oc, bankB=bankB, gp=gp: e.matmul(
                            s[bankB][0:128, 0:512], s["ff3h"][:, j, oc * 128:(oc + 1) * 128],
                            s["h2fm"][:, gp, j, :], start=(j == 0), stop=(j == 7)), base=0,
                            deps=[lf3, ev[bankB]] if j == 0 else (), inc="pe" if j == 7 else None)
                    s1 = P.op("scalar", lambda e, bankA=bankA, sca=sca: e.activation(
                        s[sca][:, 0:512], s[bankA][0:128, 0:512], AF.Silu),
                        deps=[mm1, prev.get(("hhp", oc % 2))], inc="act")
                    ev[bankA] = s1
                    hh = P.op("vector", lambda e, oc=oc, bankB=bankB, sca=sca: e.tensor_tensor(
                        s["hhfm"][:, oc, :], s[sca][:, 0:512], s[bankB][0:128, 0:512], ALU.mult),
                        deps=[s1, mm3], inc="dve")
                    ev[bankB] = hh
                    prev[("hh", oc)] = hh
                    prev[("hhp", oc % 2)] = hh
                    prev[("hhprev",)] = hh
                prev[("h2w", gp)] = mm3
                hhdone = [prev[("hh", i)] for i in range(16)]
                for it in range(4):
                    tt = t0 + it * 128
                    db = it % 2
                    for hf in range(2):
                        bank = (("psC", "psD"), ("psG", "psA"))[it % 2][hf]
                        mo = None
                        for j in range(16):
                            mo = P.mm(lambda e, j=j, hf=hf, it=it, bank=bank: e.matmul(
                                s[bank][0:128, 0:512], s["hhfm"][:, j, it * 128:(it + 1) * 128],
                                s["ff2h"][:, j, hf * 512:(hf + 1) * 512],
                                start=(j == 0), stop=(j == 15)), base=0,
                                deps=(hhdone + [lf2, ev[bank]]) if j == 0 else (),
                                inc="pe" if j == 15 else None)
                        prev[("f2", hf)] = mo
                    ly = P.op("sync", lambda e, b=b, tt=tt, d=db, half=half: e.dma_start(
                        out=s["y1t"][:, d, :],
                        in_=(dt["y1_d"] if half == 0 else dt["y2p_d"])[b, tt:tt + 128, :]),
                        deps=[prev.get(("y2ld", db)), prev.get(("y1ld", db))],
                        inc="y1l%d" % db, dma=True)
                    prev[("y1ld", db)] = None
                    bk0, bk1 = (("psC", "psD"), ("psG", "psA"))[it % 2]
                    if half == 0:
                        def fin0(e, d=db, bk0=bk0, bk1=bk1):
                            e.tensor_tensor(s["scr_a"][:, 0:512], s["y1t"][:, d, 0:512],
                                            s[bk0][0:128, 0:512], ALU.add)
                            e.tensor_tensor(s["scr_a"][:, 512:1024], s["y1t"][:, d, 512:1024],
                                            s[bk1][0:128, 0:512], ALU.add)
                            return e.tensor_copy(s["scr_bf"][:, 1, :], s["scr_a"][:, 0:1024])
                        f0 = P.op("vector", fin0, deps=[prev[("f2", 0)], prev[("f2", 1)], ly,
                                                        prev.get(("h2bf",)),
                                                        prev.get(("hhprev",))], inc="dve")
                        ev[bk0] = ev[bk1] = f0
                        sy = P.op("sync", lambda e, b=b, tt=tt: e.dma_start(
                            out=dt["y2p_d"][b, tt:tt + 128, :], in_=s["scr_bf"][:, 1, :]),
                            deps=[f0], inc="sy2p", dma=True)
                        prev[("h2bf",)] = sy
                        prev[("y2ld", db)] = sy
                    else:
                        lym = P.op("sync", lambda e, b=b, tt=tt: e.dma_start(
                            out=s["ymt"][:], in_=dt["ymoe_d"][b * T + tt:b * T + tt + 128, :]),
                            deps=[prev.get(("ymld",))], inc="yml", dma=True)
                        lxx = P.op("sync", lambda e, b=b, tt=tt: e.dma_start(
                            out=s["xtf"][:, 0, :], in_=dt["x_tm"][b, tt:tt + 128, :]),
                            deps=[prev.get(("outst",))], inc="xtl", dma=True)
                        gsl = P.op("scalar", lambda e: e.activation(
                            s["scr_b"][:, 0:1024], s["ymt"][:], AF.Silu), deps=[lym], inc="act")

                        def fin1(e, d=db, bk0=bk0, bk1=bk1):
                            e.tensor_tensor(s["scr_a"][:, 0:512], s["y1t"][:, d, 0:512],
                                            s[bk0][0:128, 0:512], ALU.add)
                            e.tensor_tensor(s["scr_a"][:, 512:1024], s["y1t"][:, d, 512:1024],
                                            s[bk1][0:128, 0:512], ALU.add)
                            e.tensor_tensor(s["scr_a"][:, 0:1024], s["scr_a"][:, 0:1024],
                                            s["scr_b"][:, 0:1024], ALU.mult)
                            return e.tensor_tensor(s["scr_c"][:, 0:1024], s["xtf"][:, 0, :],
                                                   s["scr_a"][:, 0:1024], ALU.add)
                        f1 = P.op("vector", fin1, deps=[prev[("f2", 0)], prev[("f2", 1)], ly,
                                                        gsl, lxx, prev.get(("outst",)),
                                                        prev.get(("hhprev",))], inc="dve")
                        ev[bk0] = ev[bk1] = f1
                        prev[("ymld",)] = f1
                        so = P.op("sync", lambda e, b=b, tt=tt: e.dma_start(
                            out=dt["out"][b, tt:tt + 128, :], in_=s["scr_c"][:, 0:1024]),
                            deps=[f1], inc="sout", dma=True)
                        prev[("y2ld", db)] = so
                        prev[("outst",)] = so
        prev[("ffw",)] = mo
    fin = ("sout", P.sem_counts["sout"])
    P.op("sync", lambda e: None, deps=[fin])
    P.barrier("fin")


_CONSTS = _host_consts()

_WMAP = [("wq", "L0_attn_wq"), ("wk", "L0_attn_wk"), ("wv", "L0_attn_wv"),
         ("w_out", "L0_attn_w_out"), ("w_spike", "L0_attn_w_spike"), ("w_au", "L0_attn_w_au"),
         ("b_au", "L0_attn_b_au"), ("w_ad", "L0_attn_w_ad"), ("b_ad", "L0_attn_b_ad"),
         ("w_beta", "L0_attn_w_beta"), ("b_beta", "L0_attn_b_beta"), ("w_hn", "L0_attn_w_hn"),
         ("w_u1", "L0_attn_w_u1"), ("b_u1", "L0_attn_b_u1"), ("w_u2", "L0_attn_w_u2"),
         ("b_u2", "L0_attn_b_u2"), ("w_norm_in", "L0_attn_w_norm_in"),
         ("conv_w", "L0_attn_conv_w"), ("conv_b", "L0_attn_conv_b"), ("w_ffn", "L0_attn_w_ffn"),
         ("ff1", "L0_attn_w_ff1"), ("ff3", "L0_attn_w_ff3"), ("ff2", "L0_attn_w_ff2"),
         ("experts", "L0_ffn_experts")]


def make_in_maps(inputs, n_cores=8):
    x = np.ascontiguousarray(np.asarray(inputs["L0_attn_x"], dtype=np.float32))
    shared = {}
    for kn, inn in _WMAP:
        shared[kn] = np.ascontiguousarray(np.asarray(inputs[inn], dtype=np.float32))
    shared["gate_w"] = np.ascontiguousarray(np.asarray(inputs["L0_ffn_gate_w"],
                                                       dtype=np.float32)[0])
    shared["alpha_spike"] = np.ascontiguousarray(
        np.asarray(inputs["L0_attn_alpha_spike"], dtype=np.float32).reshape(HD))
    shared["beta_spike"] = np.ascontiguousarray(
        np.asarray(inputs["L0_attn_beta_spike"], dtype=np.float32).reshape(HD))
    shared.update(_CONSTS)
    maps = []
    for core in range(n_cores):
        m = dict(shared)
        m["x_tm"] = np.ascontiguousarray(x[B * core:B * core + B])
        maps.append(m)
    return maps


def kernel(**inputs):
    nc = build_nc(debug=False)
    in_maps = make_in_maps(inputs)
    res = run_bass_kernel_spmd(nc, in_maps, core_ids=list(range(8)))
    return np.concatenate([r["out"] for r in res.results], axis=0)
